# version 14
# baseline (speedup 1.0000x reference)
import numpy as np
import concourse.bass as bass
import concourse.mybir as mybir
from concourse.bass_utils import run_bass_kernel_spmd

F32 = mybir.dt.float32
BF16 = mybir.dt.bfloat16
I32 = mybir.dt.int32
AF = mybir.ActivationFunctionType
OP = mybir.AluOpType

N_CORES = 8


class CFG:
    def __init__(self, D=2048, FF=8192, TOK=2048, TILE=512, W=128, SLAB=256):
        self.D, self.FF, self.TOK, self.TILE, self.W = D, FF, TOK, TILE, W
        self.SLAB = SLAB
        self.DC = D // 128
        self.FC = FF // 128
        self.FCH = self.FC // 2
        self.FF2 = FF // 2
        self.FF4 = FF // 4
        self.NT = TOK // TILE
        self.NB = TILE // W
        self.NSUB = TILE // 128
        self.TOKH = TOK + W
        self.NSLAB_A = 3 * D // SLAB
        self.NSLAB_O = D // SLAB
        self.SJ = SLAB // 128
        self.NQ = D // SLAB


def TS(i, n):
    return slice(i * n, (i + 1) * n)


def build(cfg: CFG):
    c = cfg
    nc = bass.Bass(target_bir_lowering=False, trn_type="TRN2")

    x_fm = nc.dram_tensor("x_fm", [c.D, c.TOKH], F32, kind="ExternalInput").ap()
    x_tm = nc.dram_tensor("x_tm", [c.TOKH, c.D], F32, kind="ExternalInput").ap()
    w_attn_t = nc.dram_tensor("w_attn_t", [128, c.NSLAB_A * c.DC * c.SLAB], F32,
                              kind="ExternalInput").ap()
    w_o_t = nc.dram_tensor("w_o_t", [128, c.NSLAB_O * c.DC * c.SLAB], F32,
                           kind="ExternalInput").ap()
    gate_w_t = nc.dram_tensor("gate_w_t", [c.D, c.FF], I32, kind="ExternalInput").ap()
    up_w_t = nc.dram_tensor("up_w_t", [c.D, c.FF], I32, kind="ExternalInput").ap()
    down_w_t = nc.dram_tensor("down_w_t", [c.FF, c.D], I32, kind="ExternalInput").ap()
    gate_s_in = nc.dram_tensor("gate_s", [128, c.FC], F32, kind="ExternalInput").ap()
    up_s_in = nc.dram_tensor("up_s", [128, c.FC], F32, kind="ExternalInput").ap()
    down_s_in = nc.dram_tensor("down_s", [128, c.DC], F32, kind="ExternalInput").ap()
    masks_in = nc.dram_tensor("masks", [128, 3 * c.W], BF16, kind="ExternalInput").ap()
    ident_in = nc.dram_tensor("ident", [128, 128], F32, kind="ExternalInput").ap()
    out_fm = nc.dram_tensor("out_fm", [c.D, c.TOK], F32, kind="ExternalOutput").ap()

    s_wah = nc.dram_tensor("s_wah", [c.NSLAB_A, 128, c.DC * c.SLAB], BF16, kind="Internal").ap()
    s_wal = nc.dram_tensor("s_wal", [c.NSLAB_A, 128, c.DC * c.SLAB], BF16, kind="Internal").ap()
    s_woh = nc.dram_tensor("s_woh", [c.NSLAB_O, 128, c.DC * c.SLAB], BF16, kind="Internal").ap()
    s_wol = nc.dram_tensor("s_wol", [c.NSLAB_O, 128, c.DC * c.SLAB], BF16, kind="Internal").ap()
    s_res = nc.dram_tensor("s_res", [c.DC, 128, c.TILE], F32, kind="Internal").ap()
    s_rows = nc.dram_tensor("s_rows", [2 * 4, 128], F32, kind="Internal").ap()
    s_khalo_h = nc.dram_tensor("s_khalo_h", [128, c.DC * c.W], BF16, kind="Internal").ap()
    s_khalo_l = nc.dram_tensor("s_khalo_l", [128, c.DC * c.W], BF16, kind="Internal").ap()
    s_vhalo_h = nc.dram_tensor("s_vhalo_h", [128, c.D], BF16, kind="Internal").ap()
    s_vhalo_l = nc.dram_tensor("s_vhalo_l", [128, c.D], BF16, kind="Internal").ap()

    cur = [nc.sbuf_base + 64]
    ded = {}

    def salloc(name, shape, dtype, offset=None):
        if offset is None:
            off = (cur[0] + 63) // 64 * 64
            cur[0] = off + shape[-1] * mybir.dt.size(dtype)
        else:
            off = offset
        t = nc.alloc_sbuf_tensor_at(name, shape, dtype, offset=off)
        ded[name] = off
        return t.ap()

    sb_masks = salloc("sb_masks", [128, 3 * c.W], BF16)
    sb_gs = salloc("sb_gs", [128, c.FC], F32)
    sb_us = salloc("sb_us", [128, c.FC], F32)
    sb_ds = salloc("sb_ds", [128, c.DC], F32)
    sb_ident = salloc("sb_ident", [128, 128], F32)
    sb_ones = salloc("sb_ones", [2, 128], BF16)
    sb_ones_col = salloc("sb_ones_col", [128, 1], BF16)
    sb_neg128 = salloc("sb_neg128", [1, 128], BF16)
    sb_stat = salloc("sb_stat", [128, 8 * c.NSUB], F32)
    sb_pk = salloc("sb_pk", [128, 2 * c.NSUB], F32)
    sb_t8 = salloc("sb_t8", [2 * c.NSUB, 128], F32)
    sb_mu_bh = salloc("sb_mu_bh", [1, c.TILE], BF16)
    sb_mu_bl = salloc("sb_mu_bl", [1, c.TILE], BF16)
    sb_r_bh = salloc("sb_r_bh", [1, c.TILE], BF16)
    sb_r_bl = salloc("sb_r_bl", [1, c.TILE], BF16)
    sb_ah = salloc("sb_ah", [128, 2 * c.W], BF16)
    sb_al = salloc("sb_al", [128, 2 * c.W], BF16)
    sb_ah2 = salloc("sb_ah2", [128, 2 * c.W], BF16)
    sb_al2 = salloc("sb_al2", [128, 2 * c.W], BF16)
    sb_rbuf = salloc("sb_rbuf", [128, 2 * c.W], F32)
    sb_ambuf = salloc("sb_ambuf", [128, 2 * c.W], F32)
    sumh_h = salloc("sumh_h", [1, c.TILE], BF16, offset=ded["sb_rbuf"])
    sumh_l = salloc("sumh_l", [1, c.TILE], BF16, offset=ded["sb_ambuf"])
    reschunk = [salloc(f"reschunk{i}", [128, c.TILE], F32) for i in range(2)]

    small_end = (cur[0] + 63) // 64 * 64

    xn_half = c.DC * c.TILE * 2
    h_half = c.FCH * c.TILE * 2
    XN_B = small_end
    A_B = XN_B + 2 * xn_half
    prep_end = 2 * xn_half + 2 * (2 * c.DC * c.SLAB * 2)
    halo_base = max(4 * xn_half, prep_end)
    A_SZ = max(6 * xn_half,
               2 * h_half + 2 * (2 * c.DC * 256) + 2 * c.FCH * 256,
               halo_base + 2 * c.DC * c.W * 2 + 2 * c.D * 2,
               2 * (2 * c.DC * c.SLAB * 2) + 2 * c.FCH * 256)
    B_B = A_B + A_SZ
    B_SZ = max(c.DC * c.TILE * 4, 10 * c.TILE * 4 + c.D * 4 + 2 * c.NSUB * 512,
               2 * 4 * max(c.DC * c.SLAB, c.FCH * 128, c.DC * 128))
    C_B = B_B + B_SZ
    C_SZ = max(c.DC * c.TILE * 4,
               2 * (2 * c.DC * c.SLAB * 2),
               2 * c.DC * c.W * 2 + 2 * c.D * 2)
    TOP = C_B + C_SZ
    assert TOP <= nc.sbuf_top - 64, (TOP, nc.sbuf_top)

    xnh = salloc("xnh", [128, c.DC * c.TILE], BF16, offset=XN_B)
    xnl = salloc("xnl", [128, c.DC * c.TILE], BF16, offset=XN_B + xn_half)

    qh = salloc("qh", [128, c.DC * c.TILE], BF16, offset=A_B + 0 * xn_half)
    ql = salloc("ql", [128, c.DC * c.TILE], BF16, offset=A_B + 1 * xn_half)
    kh = salloc("kh", [128, c.DC * c.TILE], BF16, offset=A_B + 2 * xn_half)
    kl = salloc("kl", [128, c.DC * c.TILE], BF16, offset=A_B + 3 * xn_half)
    vh = salloc("vh", [128, c.NSUB * c.D], BF16, offset=A_B + 4 * xn_half)
    vl = salloc("vl", [128, c.NSUB * c.D], BF16, offset=A_B + 5 * xn_half)
    hh = salloc("hh", [128, c.FCH * c.TILE], BF16, offset=A_B)
    hl = salloc("hl", [128, c.FCH * c.TILE], BF16, offset=A_B + h_half)
    off = A_B + 2 * h_half
    gu_ring = []
    for i in range(2):
        g = salloc(f"gbuf{i}", [128, c.DC * 128], BF16, offset=off); off += c.DC * 256
        u = salloc(f"ubuf{i}", [128, c.DC * 128], BF16, offset=off); off += c.DC * 256
        gu_ring.append((g, u))
    dn_ring = []
    for i in range(2):
        d = salloc(f"dbuf{i}", [128, c.FCH * 128], BF16, offset=off); off += c.FCH * 256
        dn_ring.append(d)
    assert off <= A_B + A_SZ
    off = A_B + 2 * xn_half
    prep_pair = []
    for i in range(2):
        ph = salloc(f"preph{i}", [128, c.DC * c.SLAB], BF16, offset=off); off += c.DC * c.SLAB * 2
        pl_ = salloc(f"prepl{i}", [128, c.DC * c.SLAB], BF16, offset=off); off += c.DC * c.SLAB * 2
        prep_pair.append((ph, pl_))
    assert off <= A_B + A_SZ
    hoff = A_B + halo_base
    khalo_pr_h = salloc("khalo_pr_h", [128, c.DC * c.W], BF16, offset=hoff); hoff += c.DC * c.W * 2
    khalo_pr_l = salloc("khalo_pr_l", [128, c.DC * c.W], BF16, offset=hoff); hoff += c.DC * c.W * 2
    vhalo_pr_h = salloc("vhalo_pr_h", [128, c.D], BF16, offset=hoff); hoff += c.D * 2
    vhalo_pr_l = salloc("vhalo_pr_l", [128, c.D], BF16, offset=hoff)

    oh = salloc("oh", [128, c.DC * c.TILE], BF16, offset=B_B)
    ol = salloc("ol", [128, c.DC * c.TILE], BF16, offset=B_B + xn_half)
    off = B_B
    prep_in = []
    PREP_N = c.DC * c.SLAB
    for i in range(2):
        t_ = salloc(f"prepin{i}", [128, PREP_N], F32, offset=off)
        off += PREP_N * 4
        prep_in.append(t_)
    assert off <= B_B + B_SZ
    off = B_B
    sgbuf, hf32, t0buf, resbuf, xbuf = [], [], [], [], []
    for i in range(2):
        sgbuf.append(salloc(f"sgbuf{i}", [128, c.TILE], F32, offset=off)); off += c.TILE * 4
        hf32.append(salloc(f"hf32_{i}", [128, c.TILE], F32, offset=off)); off += c.TILE * 4
        t0buf.append(salloc(f"t0buf{i}", [128, c.TILE], F32, offset=off)); off += c.TILE * 4
        resbuf.append(salloc(f"resbuf{i}", [128, c.TILE], F32, offset=off)); off += c.TILE * 4
        xbuf.append(salloc(f"xbuf{i}", [128, c.TILE], F32, offset=off)); off += c.TILE * 4
    statin_b = salloc("statin_b", [128, c.D], F32, offset=off); off += c.D * 4
    xring = [statin_b[:, TS(i, c.TILE)] for i in range(2)]
    sb_rowsg = salloc("sb_rowsg", [1, 2 * c.NSUB * 128], F32, offset=off)
    off += 2 * c.NSUB * 512
    assert off <= B_B + B_SZ, (off, B_B + B_SZ)

    accum = salloc("accum", [128, c.DC * c.TILE], F32, offset=C_B)
    off = C_B
    ws_ring = []
    for i in range(2):
        wh_ = salloc(f"wsh{i}", [128, c.DC * c.SLAB], BF16, offset=off); off += c.DC * c.SLAB * 2
        wl_ = salloc(f"wsl{i}", [128, c.DC * c.SLAB], BF16, offset=off); off += c.DC * c.SLAB * 2
        ws_ring.append((wh_, wl_))
    assert off <= C_B + C_SZ
    off = C_B
    khalo_sb_h = salloc("khalo_sb_h", [128, c.DC * c.W], BF16, offset=off); off += c.DC * c.W * 2
    khalo_sb_l = salloc("khalo_sb_l", [128, c.DC * c.W], BF16, offset=off); off += c.DC * c.W * 2
    vhalo_sb_h = salloc("vhalo_sb_h", [128, c.D], BF16, offset=off); off += c.D * 2
    vhalo_sb_l = salloc("vhalo_sb_l", [128, c.D], BF16, offset=off); off += c.D * 2
    assert off <= C_B + C_SZ

    P = [nc.alloc_psum_tensor(f"pb{i}", [128, 512], F32).ap() for i in range(8)]
    p_mm = [P[0][:, 0:c.TILE], P[1][:, 0:c.TILE], P[2][:, 0:c.TILE], P[3][:, 0:c.TILE]]
    p_at2 = [P[4][:, 0:256], P[2][:, 0:256]]
    p_o = [P[5][:, 0:128], P[6][:, 0:128]]
    p_bc_r = P[5]
    p_bc_mu = P[6]
    p_tr = P[7]
    p_sumh = P[4][0:1, 0:c.TILE]

    from contextlib import ExitStack
    stack = ExitStack()
    nc._kernel_stack = stack
    S, CNT = {}, {}
    for n in (["ld_const", "prep_done", "shmm", "shev",
               "xfm_ld", "stat", "vpe", "sqrt", "pack", "tmm", "rows",
               "xnf", "xnh_c",
               "qkmm", "qkhi", "qkev", "vmm", "vhi", "vev",
               "halo_ld", "halo_st", "atmm", "at_relu", "at_ev", "omm", "ohi", "oev",
               "womm", "wocp",
               "gmm", "umm", "sg", "hf32s", "hhi", "hev",
               "dnmm", "acc", "fin", "t0s", "obuf", "dvch", "scch", "t8c", "rowsd"]
              + ["xnl_c", "pxtm0", "prowsd"]
              + [f"{p}{i}" for p in ("wld", "wst", "xtm", "gld", "uld", "lnx", "plnx",
                                     "dld", "dld2", "resld", "xld", "resst", "outst")
                 for i in range(2)]):
        S[n] = stack.enter_context(nc.semaphore(n))
        CNT[n] = 0

    def inc(ins, name, n=1):
        ins.then_inc(S[name], n)
        CNT[name] += n

    def w(eng, name, val):
        eng.wait_ge(S[name], val)

    def incp(ins, prefix, idx, n=16):
        inc(ins, f"{prefix}{idx % 2}", n)

    def wp(eng, B0, prefix, idx, per):
        nm = f"{prefix}{idx % 2}"
        eng.wait_ge(S[nm], B0[nm] + per * (idx // 2 + 1))

    def wp_all(eng, B0, prefix):
        for i in range(2):
            if CNT[f"{prefix}{i}"]:
                eng.wait_ge(S[f"{prefix}{i}"], CNT[f"{prefix}{i}"])

    def chain(eng, ins, name="dvch"):
        inc(ins, name, 1)
        eng.wait_ge(S[name], CNT[name])

    def hbm_cols(t, c0, c1):
        return t[:, c0:c1].rearrange("(c p) n -> p c n", p=128)

    def sb3(t, width):
        return t.rearrange("p (c n) -> p c n", n=width)

    D_inv = 1.0 / c.D
    EPS = 1e-5

    def ln_vec_stats(dve, s):
        ns = c.NSUB
        s1 = sb_stat[:, s:s + 1]
        s2 = sb_stat[:, ns + s:ns + s + 1]
        mu = sb_stat[:, 2 * ns + s:2 * ns + s + 1]
        e2 = sb_stat[:, 3 * ns + s:3 * ns + s + 1]
        msq = sb_stat[:, 4 * ns + s:4 * ns + s + 1]
        vpe = sb_stat[:, 5 * ns + s:5 * ns + s + 1]
        dve.tensor_scalar_mul(mu, s1, D_inv)
        chain(dve, dve.tensor_scalar_mul(e2, s2, D_inv))
        chain(dve, dve.scalar_tensor_tensor(msq, mu, -1.0, mu, OP.mult, OP.mult))
        ins = dve.scalar_tensor_tensor(vpe, e2, EPS, msq, OP.add, OP.add)
        inc(ins, "vpe", 1)

    def ln_vec_newton(dve, s):
        ns = c.NSUB
        mu = sb_stat[:, 2 * ns + s:2 * ns + s + 1]
        vpe = sb_stat[:, 5 * ns + s:5 * ns + s + 1]
        r0 = sb_stat[:, 6 * ns + s:6 * ns + s + 1]
        tn = sb_stat[:, 7 * ns + s:7 * ns + s + 1]
        chain(dve, dve.reciprocal(r0, tn))
        for _ in range(2):
            chain(dve, dve.tensor_mul(tn, r0, r0))
            chain(dve, dve.tensor_mul(tn, tn, vpe))
            chain(dve, dve.tensor_scalar(tn, tn, -0.5, 1.5, OP.mult, OP.add))
            chain(dve, dve.tensor_mul(r0, r0, tn))
        chain(dve, dve.tensor_mul(sb_pk[:, s:s + 1], mu, r0))
        ins = dve.tensor_copy(sb_pk[:, ns + s:ns + s + 1], r0)
        inc(ins, "pack", 1)

    def ln_ops(B0, n_sub, wid, xrow0, xcol0, xd_h, xd_l, overwrite_gate=None,
               defer_xn=False, dsem=("xtm0", "lnx", "rowsd")):
        L = {"dma": [], "act": [], "dve": [], "pe": [], "pl": []}

        def sp_xtm(s):
            def f(sp):
                if s >= 1:
                    w(sp, "stat", B0["stat"] + 2 * s)
                ins = sp.dma_start(statin_b, x_tm[xrow0 + s * 128: xrow0 + (s + 1) * 128, :])
                inc(ins, dsem[0], 16)
            return f

        def sp_x(dc):
            def f(sp):
                if dc < 2:
                    w(sp, "stat", B0["stat"] + 2 * n_sub)
                else:
                    w(sp, "xnl_c", B0["xnl_c"] + dc - 1)
                ins = sp.dma_start(xring[dc % 2][:, 0:wid],
                                   x_fm[TS(dc, 128), xcol0: xcol0 + wid])
                incp(ins, dsem[1], dc)
            return f

        def sp_rowsd(sp):
            w(sp, "t8c", B0["t8c"] + 1)
            ins = sp.dma_start(s_rows[0:2 * c.NSUB, :], sb_t8)
            inc(ins, dsem[2], 16)
            w(sp, dsem[2], B0[dsem[2]] + 16)
            dst_ = sb_rowsg.rearrange("p (a n) -> p a n", n=128)
            src_ = s_rows[0:2 * c.NSUB, :].rearrange("a n -> a () n") \
                if False else s_rows[0:2 * c.NSUB, :]
            ins = sp.dma_start(dst_, src_)
            inc(ins, dsem[2], 16)

        L["dma"] = ([sp_xtm(s) for s in range(n_sub)]
                    + [sp_x(0), sp_x(1), sp_rowsd]
                    + [sp_x(dc) for dc in range(2, c.DC)])

        def act_stats(s):
            def f(act):
                w(act, dsem[0], B0[dsem[0]] + 16 * (s + 1))
                if s >= 1:
                    w(act, "stat", B0["stat"] + 2 * s)
                ins = act.activation(statin_b, statin_b, AF.Copy,
                                     accum_out=sb_stat[:, s:s + 1])
                inc(ins, "stat", 1)
                w(act, "stat", B0["stat"] + 2 * s + 1)
                ins = act.activation(statin_b, statin_b, AF.Square,
                                     accum_out=sb_stat[:, c.NSUB + s:c.NSUB + s + 1])
                inc(ins, "stat", 1)
            return f

        def act_sqrt(s):
            def f(act):
                w(act, "vpe", B0["vpe"] + s + 1)
                ins = act.activation(sb_stat[:, 7 * c.NSUB + s:7 * c.NSUB + s + 1],
                                     sb_stat[:, 5 * c.NSUB + s:5 * c.NSUB + s + 1], AF.Sqrt)
                inc(ins, "sqrt", 1)
            return f

        L["act"] = ([None] + [act_stats(s) for s in range(n_sub)]
                    + [act_sqrt(s) for s in range(n_sub)])
        L["act2"] = []

        def dve_vstats(s):
            def f(dve):
                w(dve, "stat", B0["stat"] + 2 * (s + 1))
                ln_vec_stats(dve, s)
            return f

        def dve_newton(s):
            def f(dve):
                w(dve, "sqrt", B0["sqrt"] + s + 1)
                ln_vec_newton(dve, s)
            return f

        def dve_t8c(dve):
            w(dve, "tmm", B0["tmm"] + 1)
            ins = dve.tensor_copy(sb_t8, p_tr[0:2 * c.NSUB, 0:128])
            inc(ins, "t8c", 1)

        def dve_rowpair(dve):
            w(dve, dsem[2], B0[dsem[2]] + 32)
            mur = sb_rowsg[:, 0:wid]
            rr = sb_rowsg[:, c.NSUB * 128: c.NSUB * 128 + wid]
            dve.tensor_copy(sb_mu_bh[:, 0:wid], mur)
            chain(dve, dve.tensor_copy(sb_r_bh[:, 0:wid], rr))
            dve.tensor_sub(sb_mu_bl[:, 0:wid], mur, sb_mu_bh[:, 0:wid])
            ins = dve.tensor_sub(sb_r_bl[:, 0:wid], rr, sb_r_bh[:, 0:wid])
            inc(ins, "rows", 1)

        def dve_xn(dc):
            def f(dve):
                if dc == 0:
                    w(dve, "tmm", B0["tmm"] + 3)
                    if overwrite_gate is not None:
                        w(dve, overwrite_gate[0], overwrite_gate[1])
                wp(dve, B0, dsem[1], dc, 16)
                xc = xring[dc % 2][:, 0:wid]
                chain(dve, dve.tensor_mul(xc, xc, p_bc_r[:, 0:wid]))
                chain(dve, dve.tensor_sub(xc, xc, p_bc_mu[:, 0:wid]))
                chain(dve, dve.tensor_copy(xd_h[:, dc * c.TILE: dc * c.TILE + wid], xc))
                ins = dve.tensor_sub(xd_l[:, dc * c.TILE: dc * c.TILE + wid], xc,
                                     xd_h[:, dc * c.TILE: dc * c.TILE + wid])
                inc(ins, "xnl_c", 1)
            return f

        tail = [dve_xn(dc) for dc in range(c.DC)]
        head = [None] * 2 + [dve_vstats(s) for s in range(n_sub)] \
            + [dve_newton(s) for s in range(n_sub)]
        head += [None] * (12 - len(head)) + [dve_t8c, None, dve_rowpair]
        if defer_xn:
            L["dve"] = head
            L["dve2"] = tail
        else:
            L["dve"] = head + tail
            L["dve2"] = []

        def pe_tr(pe):
            w(pe, "pack", B0["pack"] + n_sub)
            ins = pe.transpose(p_tr[0:2 * c.NSUB, 0:128], sb_pk, sb_ident)
            inc(ins, "tmm", 1)

        def pe_bc(pe):
            w(pe, "rows", B0["rows"] + 1)
            pe.matmul(p_bc_mu[:, 0:wid], sb_ones[0:1, :], sb_mu_bh[:, 0:wid],
                      start=True, stop=False)
            ins = pe.matmul(p_bc_mu[:, 0:wid], sb_ones[0:1, :], sb_mu_bl[:, 0:wid],
                            start=False, stop=True)
            inc(ins, "tmm", 1)
            pe.matmul(p_bc_r[:, 0:wid], sb_ones[0:1, :], sb_r_bh[:, 0:wid],
                      start=True, stop=False)
            ins = pe.matmul(p_bc_r[:, 0:wid], sb_ones[0:1, :], sb_r_bl[:, 0:wid],
                            start=False, stop=True)
            inc(ins, "tmm", 1)

        L["pe"] = [pe_tr, pe_bc]
        L["pl"] = []
        return L

    def ln_pop(L, eng_name, eng):
        if L is not None and L[eng_name]:
            f = L[eng_name].pop(0)
            if f is not None:
                f(eng)

    def ln_drain(L, eng_name, eng):
        if L is None:
            return
        while L[eng_name]:
            f = L[eng_name].pop(0)
            if f is not None:
                f(eng)

    B0 = dict(CNT)
    with nc.Block() as blk:
        @blk.sync
        def _(sp):
            for dst, src in ((sb_masks, masks_in), (sb_gs, gate_s_in), (sb_us, up_s_in),
                             (sb_ds, down_s_in), (sb_ident, ident_in)):
                inc(sp.dma_start(dst, src), "ld_const", 16)
            sp.wait_ge(S["ld_const"], CNT["ld_const"])

        @blk.vector
        def _(dve):
            dve.memset(sb_ones, 1.0)
            dve.memset(sb_ones_col, 1.0)
            dve.memset(sb_neg128, -128.0)
            dve.memset(sb_pk, 0.0)
            dve.memset(sb_stat, 0.0)

    NSL = c.NSLAB_A + c.NSLAB_O
    B0 = dict(CNT)
    with nc.Block() as blk:
        HSL = c.DC // 2 * c.SLAB

        def prep_src(s, h):
            wt = w_attn_t if s < c.NSLAB_A else w_o_t
            s0 = s if s < c.NSLAB_A else s - c.NSLAB_A
            base = s0 * c.DC * c.SLAB + h * HSL
            return wt[:, base: base + HSL]

        @blk.sync
        def _(sp):
            for s in range(NSL):
                if s >= 2:
                    w(sp, "prep_done", B0["prep_done"] + s - 1)
                dst = prep_in[s % 2][:, 0:HSL]
                incp(sp.dma_start(dst, prep_src(s, 0)), "wld", s)

        @blk.scalar
        def _(act):
            for s in range(NSL):
                if s >= 2:
                    w(act, "prep_done", B0["prep_done"] + s - 1)
                dst = prep_in[s % 2][:, HSL:2 * HSL]
                incp(act.dma_start(dst, prep_src(s, 1)), "wld", s)

        @blk.vector
        def _(dve):
            for s in range(NSL):
                wp(dve, B0, "wld", s, 32)
                if s >= 2:
                    wp(dve, B0, "wst", s - 2, 32)
                pin = prep_in[s % 2][:, 0:c.DC * c.SLAB]
                phi, plo = prep_pair[s % 2]
                if s < c.NQ:
                    chain(dve, dve.tensor_scalar_mul(phi, pin, 1.0 / c.W))
                    ins = dve.scalar_tensor_tensor(plo, pin, 1.0 / c.W, phi,
                                                   OP.mult, OP.subtract)
                else:
                    chain(dve, dve.tensor_copy(phi, pin))
                    ins = dve.tensor_sub(plo, pin, phi)
                inc(ins, "prep_done", 1)

        @blk.gpsimd
        def _(pl):
            for s in range(NSL):
                w(pl, "prep_done", B0["prep_done"] + s + 1)
                phi, plo = prep_pair[s % 2]
                dh = s_wah[s] if s < c.NSLAB_A else s_woh[s - c.NSLAB_A]
                dl = s_wal[s] if s < c.NSLAB_A else s_wol[s - c.NSLAB_A]
                incp(pl.dma_start(dh, phi), "wst", s)
                incp(pl.dma_start(dl, plo), "wst", s)
            for i in range(2):
                n_attn = (c.NSLAB_A + 1 - i) // 2
                w(pl, f"wst{i}", B0[f"wst{i}"] + 32 * n_attn)

    WST_FULL = {n: CNT[n] for n in ("wst0", "wst1")}

    B0 = dict(CNT)
    with nc.Block() as blk:
        LH = ln_ops(B0, 1, c.W, 0, 0, qh, ql)

        @blk.sync
        def _(sp):
            ln_drain(LH, "dma", sp)

        @blk.scalar
        def _(act):
            ln_drain(LH, "act", act)

        @blk.vector
        def _(dve):
            ln_drain(LH, "dve", dve)

        @blk.tensor
        def _(pe):
            ln_drain(LH, "pe", pe)


    B0 = dict(CNT)
    with nc.Block() as blk:
        KV = list(range(c.NQ, c.NSLAB_A))
        NK = c.NQ
        L0 = ln_ops(B0, c.NSUB, c.TILE, c.W, c.W, xnh, xnl,
                    dsem=("pxtm0", "plnx", "prowsd"))

        @blk.sync
        def _(sp):
            for i, s in enumerate(KV):
                if i >= 2:
                    pi = i - 2
                    if pi < NK:
                        w(sp, "qkmm", B0["qkmm"] + (pi + 1) * c.SJ)
                    else:
                        w(sp, "vmm", B0["vmm"] + (pi - NK + 1))
                wh_, wl_ = ws_ring[i % 2]
                incp(sp.dma_start(wh_, s_wah[s]), "wld", i)
                incp(sp.dma_start(wl_, s_wal[s]), "wld", i)

        @blk.tensor
        def _(pe):
            for i, s in enumerate(KV):
                wp(pe, B0, "wld", i, 32)
                wh_, wl_ = ws_ring[i % 2]
                if i < NK:
                    for j in range(c.SJ):
                        oc = i * c.SJ + j
                        if oc >= 2:
                            w(pe, "qkev", B0["qkev"] + oc - 1)
                        pb = p_mm[oc % 2]
                        nmm = 0
                        for dc in range(c.DC):
                            lt_h = wh_[:, dc * c.SLAB + j * 128: dc * c.SLAB + (j + 1) * 128]
                            lt_l = wl_[:, dc * c.SLAB + j * 128: dc * c.SLAB + (j + 1) * 128]
                            for lt, rh in ((lt_h, qh), (lt_h, ql), (lt_l, qh)):
                                ins = pe.matmul(pb[:, 0:c.W], lt,
                                                rh[:, dc * c.TILE: dc * c.TILE + c.W],
                                                start=(nmm == 0), stop=(nmm == 3 * c.DC - 1))
                                nmm += 1
                        inc(ins, "qkmm", 1)
                else:
                    g = i - NK
                    if g >= 2:
                        w(pe, "vev", B0["vev"] + g - 1)
                    pb = p_mm[2 + g % 2][:, 0:c.SLAB]
                    nmm = 0
                    for dc in range(c.DC):
                        xh_ = qh[:, dc * c.TILE: dc * c.TILE + c.W]
                        xl_ = ql[:, dc * c.TILE: dc * c.TILE + c.W]
                        wh_c = wh_[:, TS(dc, c.SLAB)]
                        wl_c = wl_[:, TS(dc, c.SLAB)]
                        for lt, rh in ((xh_, wh_c), (xh_, wl_c), (xl_, wh_c)):
                            ins = pe.matmul(pb, lt, rh, start=(nmm == 0),
                                            stop=(nmm == 3 * c.DC - 1))
                            nmm += 1
                    inc(ins, "vmm", 1)
                if i in (5, 8):
                    ln_pop(L0, "pe", pe)
            ln_drain(L0, "pe", pe)

        @blk.scalar
        def _(act):
            for oc in range(NK * c.SJ):
                w(act, "qkmm", B0["qkmm"] + oc + 1)
                ins = act.copy(khalo_pr_h[:, TS(oc, c.W)], p_mm[oc % 2][:, 0:c.W])
                inc(ins, "qkhi", 1)
                ln_pop(L0, "act", act)
            for g in range(c.NSLAB_A - 2 * c.NQ):
                w(act, "vmm", B0["vmm"] + g + 1)
                ins = act.copy(vhalo_pr_h[:, TS(g, c.SLAB)], p_mm[2 + g % 2][:, 0:c.SLAB])
                inc(ins, "vhi", 1)
                ln_pop(L0, "act", act)
            ln_drain(L0, "act", act)

        @blk.vector
        def _(dve):
            for oc in range(NK * c.SJ):
                w(dve, "qkhi", B0["qkhi"] + oc + 1)
                ins = dve.tensor_sub(khalo_pr_l[:, TS(oc, c.W)], p_mm[oc % 2][:, 0:c.W],
                                     khalo_pr_h[:, TS(oc, c.W)])
                inc(ins, "qkev", 1)
                ln_pop(L0, "dve", dve)
            for g in range(c.NSLAB_A - 2 * c.NQ):
                w(dve, "vhi", B0["vhi"] + g + 1)
                ins = dve.tensor_sub(vhalo_pr_l[:, TS(g, c.SLAB)],
                                     p_mm[2 + g % 2][:, 0:c.SLAB],
                                     vhalo_pr_h[:, TS(g, c.SLAB)])
                inc(ins, "vev", 1)
                ln_pop(L0, "dve", dve)
            ln_drain(L0, "dve", dve)

        @blk.gpsimd
        def _(pl):
            ln_drain(L0, "dma", pl)
            w(pl, "qkev", B0["qkev"] + NK * c.SJ)
            w(pl, "vev", B0["vev"] + (c.NSLAB_A - 2 * c.NQ))
            inc(pl.dma_start(s_khalo_h, khalo_pr_h), "halo_st", 16)
            inc(pl.dma_start(s_khalo_l, khalo_pr_l), "halo_st", 16)
            inc(pl.dma_start(s_vhalo_h, vhalo_pr_h), "halo_st", 16)
            inc(pl.dma_start(s_vhalo_l, vhalo_pr_l), "halo_st", 16)

    for t in range(c.NT):
        tok0 = c.W + t * c.TILE
        xnt_h, xnt_l = xnh, xnl
        qt_h, qt_l = qh, ql

        B0 = dict(CNT)
        with nc.Block() as blk:
            @blk.sync
            def _(sp):
                for s in range(c.NSLAB_A):
                    if s >= 2:
                        sc = s - 2
                        if sc < 2 * c.NQ:
                            w(sp, "qkmm", B0["qkmm"] + (sc + 1) * c.SJ)
                        else:
                            w(sp, "vmm", B0["vmm"] + (sc - 2 * c.NQ + 1) * c.NSUB)
                    wh_, wl_ = ws_ring[s % 2]
                    incp(sp.dma_start(wh_, s_wah[s]), "wld", s)
                    incp(sp.dma_start(wl_, s_wal[s]), "wld", s)
                s_le = c.NSLAB_A - 2
                w(sp, "vmm", B0["vmm"] + (s_le - 2 * c.NQ + 1) * c.NSUB)
                w(sp, "halo_st", B0["halo_st"])
                inc(sp.dma_start(khalo_sb_h, s_khalo_h), "halo_ld", 16)
                inc(sp.dma_start(khalo_sb_l, s_khalo_l), "halo_ld", 16)
                inc(sp.dma_start(vhalo_sb_h, s_vhalo_h), "halo_ld", 16)
                inc(sp.dma_start(vhalo_sb_l, s_vhalo_l), "halo_ld", 16)

            @blk.tensor
            def _(pe):
                for s in range(c.NSLAB_A):
                    wp(pe, B0, "wld", s, 32)
                    wh_, wl_ = ws_ring[s % 2]
                    if s < 2 * c.NQ:
                        for j in range(c.SJ):
                            oc = s * c.SJ + j
                            if oc >= 2:
                                w(pe, "qkev", B0["qkev"] + oc - 1)
                            pb = p_mm[oc % 2]
                            nmm = 0
                            for dc in range(c.DC):
                                lt_h = wh_[:, dc * c.SLAB + j * 128: dc * c.SLAB + (j + 1) * 128]
                                lt_l = wl_[:, dc * c.SLAB + j * 128: dc * c.SLAB + (j + 1) * 128]
                                for lt, rh in ((lt_h, xnt_h), (lt_h, xnt_l), (lt_l, xnt_h)):
                                    ins = pe.matmul(pb, lt, rh[:, TS(dc, c.TILE)],
                                                    start=(nmm == 0), stop=(nmm == 3 * c.DC - 1))
                                    nmm += 1
                            inc(ins, "qkmm", 1)
                    else:
                        sv = s - 2 * c.NQ
                        for ts_ in range(c.NSUB):
                            g = sv * c.NSUB + ts_
                            if g >= 2:
                                w(pe, "vev", B0["vev"] + g - 1)
                            pb = p_mm[2 + g % 2][:, 0:c.SLAB]
                            nmm = 0
                            for dc in range(c.DC):
                                xh_ = xnt_h[:, dc * c.TILE + ts_ * 128: dc * c.TILE + (ts_ + 1) * 128]
                                xl_ = xnt_l[:, dc * c.TILE + ts_ * 128: dc * c.TILE + (ts_ + 1) * 128]
                                wh_c = wh_[:, TS(dc, c.SLAB)]
                                wl_c = wl_[:, TS(dc, c.SLAB)]
                                for lt, rh in ((xh_, wh_c), (xh_, wl_c), (xl_, wh_c)):
                                    ins = pe.matmul(pb, lt, rh, start=(nmm == 0),
                                                    stop=(nmm == 3 * c.DC - 1))
                                    nmm += 1
                            inc(ins, "vmm", 1)

            @blk.scalar
            def _(act):
                w(act, "halo_st", B0["halo_st"])
                if t == 0:
                    for n in ("wst0", "wst1"):
                        w(act, n, WST_FULL[n])
                for s in range(c.NSLAB_A):
                    if s < 2 * c.NQ:
                        for j in range(c.SJ):
                            oc = s * c.SJ + j
                            w(act, "qkmm", B0["qkmm"] + oc + 1)
                            dst = qt_h if oc < c.DC else kh
                            occ = oc if oc < c.DC else oc - c.DC
                            ins = act.copy(dst[:, TS(occ, c.TILE)], p_mm[oc % 2])
                            inc(ins, "qkhi", 1)
                    else:
                        sv = s - 2 * c.NQ
                        for ts_ in range(c.NSUB):
                            g = sv * c.NSUB + ts_
                            w(act, "vmm", B0["vmm"] + g + 1)
                            sl = slice(ts_ * c.D + sv * c.SLAB, ts_ * c.D + (sv + 1) * c.SLAB)
                            ins = act.copy(vh[:, sl], p_mm[2 + g % 2][:, 0:c.SLAB])
                            inc(ins, "vhi", 1)

            @blk.vector
            def _(dve):
                w(dve, "halo_st", B0["halo_st"])
                if t == 0:
                    for n in ("wst0", "wst1"):
                        w(dve, n, WST_FULL[n])
                for s in range(c.NSLAB_A):
                    if s < 2 * c.NQ:
                        for j in range(c.SJ):
                            oc = s * c.SJ + j
                            w(dve, "qkhi", B0["qkhi"] + oc + 1)
                            dsth = qt_h if oc < c.DC else kh
                            dstl = qt_l if oc < c.DC else kl
                            occ = oc if oc < c.DC else oc - c.DC
                            ins = dve.tensor_sub(dstl[:, TS(occ, c.TILE)], p_mm[oc % 2],
                                                 dsth[:, TS(occ, c.TILE)])
                            inc(ins, "qkev", 1)
                    else:
                        sv = s - 2 * c.NQ
                        for ts_ in range(c.NSUB):
                            g = sv * c.NSUB + ts_
                            w(dve, "vhi", B0["vhi"] + g + 1)
                            sl = slice(ts_ * c.D + sv * c.SLAB, ts_ * c.D + (sv + 1) * c.SLAB)
                            ins = dve.tensor_sub(vl[:, sl], p_mm[2 + g % 2][:, 0:c.SLAB], vh[:, sl])
                            inc(ins, "vev", 1)

        B0 = dict(CNT)
        with nc.Block() as blk:
            @blk.tensor
            def _(pe):
                w(pe, "halo_ld", B0["halo_ld"])

                def at_group(b):
                    if b >= 2:
                        w(pe, "at_ev", B0["at_ev"] + 2 * (b - 1))
                    nmm = 0
                    for half in range(2):
                        pb = p_at2[b % 2][:, half * 128:(half + 1) * 128]
                        for dc in range(c.DC):
                            if half == 0:
                                if b == 0:
                                    kh_ = khalo_sb_h[:, TS(dc, c.W)]
                                    kl_ = khalo_sb_l[:, TS(dc, c.W)]
                                else:
                                    kh_ = kh[:, dc * c.TILE + (b - 1) * c.W: dc * c.TILE + b * c.W]
                                    kl_ = kl[:, dc * c.TILE + (b - 1) * c.W: dc * c.TILE + b * c.W]
                            else:
                                kh_ = kh[:, dc * c.TILE + b * c.W: dc * c.TILE + (b + 1) * c.W]
                                kl_ = kl[:, dc * c.TILE + b * c.W: dc * c.TILE + (b + 1) * c.W]
                            qh_ = qt_h[:, dc * c.TILE + b * c.W: dc * c.TILE + (b + 1) * c.W]
                            ql_ = qt_l[:, dc * c.TILE + b * c.W: dc * c.TILE + (b + 1) * c.W]
                            for lt, rh in ((kh_, qh_), (kh_, ql_), (kl_, qh_)):
                                ins = pe.matmul(pb, lt, rh, start=(nmm == 0),
                                                stop=(nmm == 6 * c.DC - 1))
                                nmm += 1
                    inc(ins, "atmm", 1)

                def o_groups(b):
                    w(pe, "at_ev", B0["at_ev"] + 2 * b + 2)
                    for dc in range(c.DC):
                        g = b * c.DC + dc
                        if g >= 2:
                            w(pe, "oev", B0["oev"] + g - 1)
                        pb = p_o[g % 2]
                        nmm = 0
                        for half in range(2):
                            if half == 0:
                                if b == 0:
                                    vh_ = vhalo_sb_h[:, TS(dc, 128)]
                                    vl_ = vhalo_sb_l[:, TS(dc, 128)]
                                else:
                                    vh_ = vh[:, (b - 1) * c.D + dc * 128:(b - 1) * c.D + (dc + 1) * 128]
                                    vl_ = vl[:, (b - 1) * c.D + dc * 128:(b - 1) * c.D + (dc + 1) * 128]
                            else:
                                vh_ = vh[:, b * c.D + dc * 128: b * c.D + (dc + 1) * 128]
                                vl_ = vl[:, b * c.D + dc * 128: b * c.D + (dc + 1) * 128]
                            ah_ = (sb_ah if b % 2 == 0 else sb_ah2)[:, TS(half, c.W)]
                            al_ = (sb_al if b % 2 == 0 else sb_al2)[:, TS(half, c.W)]
                            for lt, rh in ((vh_, ah_), (vh_, al_), (vl_, ah_)):
                                ins = pe.matmul(pb, lt, rh, start=(nmm == 0), stop=(nmm == 5))
                                nmm += 1
                        inc(ins, "omm", 1)

                for b in range(c.NB):
                    at_group(b)
                    if b >= 1:
                        o_groups(b - 1)
                o_groups(c.NB - 1)

            @blk.scalar
            def _(act):
                wp_all(act, B0, "outst")
                for b in range(c.NB):
                    w(act, "atmm", B0["atmm"] + b + 1)
                    if b >= 1:
                        w(act, "at_ev", B0["at_ev"] + 2 * b)
                    for half in range(2):
                        ins = act.activation(sb_rbuf[:, TS(half, c.W)],
                                             p_at2[b % 2][:, half * 128:(half + 1) * 128],
                                             AF.Relu)
                        inc(ins, "at_relu", 1)
                    for dc in range(c.DC):
                        g = b * c.DC + dc
                        w(act, "omm", B0["omm"] + g + 1)
                        ins = act.copy(oh[:, dc * c.TILE + b * c.W: dc * c.TILE + (b + 1) * c.W],
                                       p_o[g % 2])
                        inc(ins, "ohi", 1)

            @blk.vector
            def _(dve):
                wp_all(dve, B0, "outst")
                for b in range(c.NB):
                    w(dve, "at_relu", B0["at_relu"] + 2 * b + 2)
                    if b >= 2:
                        w(dve, "omm", B0["omm"] + (b - 1) * c.DC)
                    for half in range(2):
                        pb = p_at2[b % 2][:, half * 128:(half + 1) * 128]
                        ins = dve.scalar_tensor_tensor(sb_ambuf[:, TS(half, c.W)], pb, 1.0,
                                                       sb_rbuf[:, TS(half, c.W)],
                                                       OP.mult, OP.mult)
                    chain(dve, ins)
                    for half in range(2):
                        if half == 0:
                            msk = sb_masks[:, 2 * c.W:3 * c.W] if (t == 0 and b == 0) \
                                else sb_masks[:, 0:c.W]
                        else:
                            msk = sb_masks[:, c.W:2 * c.W]
                        ins = dve.tensor_mul(sb_ambuf[:, TS(half, c.W)],
                                             sb_ambuf[:, TS(half, c.W)], msk)
                    chain(dve, ins)
                    ah_b = sb_ah if b % 2 == 0 else sb_ah2
                    al_b = sb_al if b % 2 == 0 else sb_al2
                    for half in range(2):
                        ins = dve.tensor_copy(ah_b[:, TS(half, c.W)],
                                              sb_ambuf[:, TS(half, c.W)])
                    chain(dve, ins)
                    for half in range(2):
                        ins = dve.tensor_sub(al_b[:, TS(half, c.W)],
                                             sb_ambuf[:, TS(half, c.W)],
                                             ah_b[:, TS(half, c.W)])
                        inc(ins, "at_ev", 1)
                    for dc in range(c.DC):
                        g = b * c.DC + dc
                        w(dve, "ohi", B0["ohi"] + g + 1)
                        ins = dve.tensor_sub(ol[:, dc * c.TILE + b * c.W: dc * c.TILE + (b + 1) * c.W],
                                             p_o[g % 2],
                                             oh[:, dc * c.TILE + b * c.W: dc * c.TILE + (b + 1) * c.W])
                        inc(ins, "oev", 1)

            @blk.gpsimd
            def _(pl):
                w(pl, "halo_ld", B0["halo_ld"])
                src = sb3(kh, c.TILE)[:, :, c.TILE - c.W:c.TILE]
                inc(pl.dma_start(sb3(s_khalo_h, c.W), src), "halo_st", 16)
                src = sb3(kl, c.TILE)[:, :, c.TILE - c.W:c.TILE]
                inc(pl.dma_start(sb3(s_khalo_l, c.W), src), "halo_st", 16)
                inc(pl.dma_start(s_vhalo_h, vh[:, (c.NSUB - 1) * c.D: c.NSUB * c.D]), "halo_st", 16)
                inc(pl.dma_start(s_vhalo_l, vl[:, (c.NSUB - 1) * c.D: c.NSUB * c.D]), "halo_st", 16)

        B0 = dict(CNT)
        with nc.Block() as blk:
            @blk.sync
            def _(sp):
                if t == 0:
                    for n in ("wst0", "wst1"):
                        w(sp, n, WST_FULL[n])
                for s in range(c.NSLAB_O):
                    if s >= 2:
                        w(sp, "womm", B0["womm"] + (s - 1) * c.SJ)
                    wh_, wl_ = ws_ring[s % 2]
                    incp(sp.dma_start(wh_, s_woh[s]), "wld", s)
                    incp(sp.dma_start(wl_, s_wol[s]), "wld", s)

            @blk.tensor
            def _(pe):
                for s in range(c.NSLAB_O):
                    wp(pe, B0, "wld", s, 32)
                    wh_, wl_ = ws_ring[s % 2]
                    for j in range(c.SJ):
                        oc = s * c.SJ + j
                        if oc >= 2:
                            w(pe, "wocp", B0["wocp"] + oc - 1)
                        pb = p_mm[oc % 2]
                        nmm = 0
                        for dc in range(c.DC):
                            lt_h = wh_[:, dc * c.SLAB + j * 128: dc * c.SLAB + (j + 1) * 128]
                            lt_l = wl_[:, dc * c.SLAB + j * 128: dc * c.SLAB + (j + 1) * 128]
                            for lt, rh in ((lt_h, oh), (lt_h, ol), (lt_l, oh)):
                                ins = pe.matmul(pb, lt, rh[:, TS(dc, c.TILE)],
                                                start=(nmm == 0), stop=(nmm == 3 * c.DC - 1))
                                nmm += 1
                        inc(ins, "womm", 1)

            @blk.scalar
            def _(act):
                for oc in range(c.DC):
                    w(act, "womm", B0["womm"] + oc + 1)
                    if oc >= 2:
                        wp(act, B0, "resst", oc - 2, 16)
                    else:
                        nm = f"resst{oc % 2}"
                        if B0[nm]:
                            w(act, nm, B0[nm])
                    ins = act.copy(reschunk[oc % 2], p_mm[oc % 2])
                    inc(ins, "wocp", 1)

            @blk.gpsimd
            def _(pl):
                wp_all(pl, B0, "resld")
                for oc in range(c.DC):
                    w(pl, "wocp", B0["wocp"] + oc + 1)
                    incp(pl.dma_start(s_res[oc], reschunk[oc % 2]), "resst", oc)

        B0 = dict(CNT)
        with nc.Block() as blk:
            if t + 1 < c.NT:
                nxt = c.W + (t + 1) * c.TILE
                LF = ln_ops(B0, c.NSUB, c.TILE, nxt, nxt, xnh, xnl,
                            overwrite_gate=("umm", B0["umm"] + c.FC), defer_xn=True)
            else:
                LF = None

            @blk.sync
            def _(sp):
                ln_drain(LF, "dma", sp)

            @blk.tensor
            def _(pe):
                for hf in range(2):
                    for icl in range(c.FCH):
                        j = hf * c.FCH + icl
                        wp(pe, B0, "gld", j, 16)
                        if hf == 1 and icl == 0:
                            w(pe, "acc", B0["acc"] + c.DC)
                        if j >= 2:
                            w(pe, "hev", B0["hev"] + j - 1)
                        g_, u_ = gu_ring[j % 2]
                        pg = p_mm[(j % 2) * 2]
                        pu = p_mm[(j % 2) * 2 + 1]
                        nmm = 0
                        for dc in range(c.DC):
                            lt = g_[:, TS(dc, 128)]
                            for rh in (xnt_h, xnt_l):
                                ins = pe.matmul(pg, lt, rh[:, TS(dc, c.TILE)],
                                                start=(nmm == 0), stop=(nmm == 2 * c.DC - 1))
                                nmm += 1
                        inc(ins, "gmm", 1)
                        wp(pe, B0, "uld", j, 16)
                        nmm = 0
                        for dc in range(c.DC):
                            lt = u_[:, TS(dc, 128)]
                            for rh in (xnt_h, xnt_l):
                                ins = pe.matmul(pu, lt, rh[:, TS(dc, c.TILE)],
                                                start=(nmm == 0), stop=(nmm == 2 * c.DC - 1))
                                nmm += 1
                        inc(ins, "umm", 1)
                        if icl >= 2:
                            k = icl - 2
                            if k == 0 and hf == 1:
                                w(pe, "shev", B0["shev"] + 1)
                            pe.matmul(p_sumh, sb_ones_col, hh[:, TS(k, c.TILE)],
                                      start=(k == 0), stop=False)
                            pe.matmul(p_sumh, sb_ones_col, hl[:, TS(k, c.TILE)],
                                      start=False, stop=False)
                        if j in (11, 15):
                            ln_pop(LF, "pe", pe)
                    if hf == 1:
                        ln_drain(LF, "pe", pe)
                    w(pe, "hev", B0["hev"] + (hf + 1) * c.FCH)
                    for k in (c.FCH - 2, c.FCH - 1):
                        pe.matmul(p_sumh, sb_ones_col, hh[:, TS(k, c.TILE)],
                                  start=False, stop=False)
                        ins = pe.matmul(p_sumh, sb_ones_col, hl[:, TS(k, c.TILE)],
                                        start=False, stop=(k == c.FCH - 1))
                    inc(ins, "shmm", 1)
                    w(pe, "shev", B0["shev"] + hf + 1)
                    for ch in range(c.DC):
                        j = hf * c.DC + ch
                        wp(pe, B0, "dld", j, 16)
                        if hf == 0:
                            if j >= 2:
                                w(pe, "acc", B0["acc"] + j - 1)
                        else:
                            if ch >= 2:
                                w(pe, "fin", B0["fin"] + ch - 1)
                            else:
                                w(pe, "acc", B0["acc"] + c.DC + ch - 1)
                        pb = p_mm[j % 2]
                        nmm = 0
                        for icl in range(c.FCH):
                            if icl == c.FCH // 2:
                                wp(pe, B0, "dld2", j, 16)
                            lt = dn_ring[j % 2][:, TS(icl, 128)]
                            for rh in (hh, hl):
                                ins = pe.matmul(pb, lt, rh[:, TS(icl, c.TILE)],
                                                start=(nmm == 0), stop=False)
                                nmm += 1
                        pe.matmul(pb, sb_neg128, sumh_h[:, 0:c.TILE], start=False, stop=False)
                        ins = pe.matmul(pb, sb_neg128, sumh_l[:, 0:c.TILE], start=False, stop=True)
                        inc(ins, "dnmm", 1)

            @blk.scalar
            def _(act):
                w(act, "halo_st", B0["halo_st"])
                for hf in range(2):
                    for icl in range(c.FCH):
                        j = hf * c.FCH + icl
                        w(act, "gmm", B0["gmm"] + j + 1)
                        if j >= 2:
                            w(act, "hf32s", B0["hf32s"] + j - 1)
                        if hf == 1 and icl == 0:
                            w(act, "dnmm", B0["dnmm"] + c.DC)
                        pg = p_mm[(j % 2) * 2]
                        i2 = hf * c.FCH + icl
                        ins = act.activation(sgbuf[j % 2], pg, AF.Silu,
                                             scale=sb_gs[:, i2:i2 + 1])
                        inc(ins, "sg", 1)
                        w(act, "hf32s", B0["hf32s"] + j + 1)
                        ins = act.copy(hh[:, TS(icl, c.TILE)], hf32[j % 2])
                        inc(ins, "hhi", 1)
                        ln_pop(LF, "act", act)
                ln_drain(LF, "act", act)
                for ch in range(c.DC):
                    w(act, "acc", B0["acc"] + ch + 1)
                    if ch >= 2:
                        w(act, "obuf", B0["obuf"] + ch - 1)
                    ins = act.activation(t0buf[ch % 2], accum[:, TS(ch, c.TILE)], AF.Copy,
                                         scale=sb_ds[:, ch:ch + 1])
                    inc(ins, "t0s", 1)

            @blk.vector
            def _(dve):
                w(dve, "halo_st", B0["halo_st"])
                for hf in range(2):
                    for icl in range(c.FCH):
                        j = hf * c.FCH + icl
                        w(dve, "sg", B0["sg"] + j + 1)
                        w(dve, "umm", B0["umm"] + j + 1)
                        if j >= 2:
                            w(dve, "hhi", B0["hhi"] + j - 1)
                        if hf == 1 and icl == 0:
                            w(dve, "dnmm", B0["dnmm"] + c.DC)
                        pu = p_mm[(j % 2) * 2 + 1]
                        i2 = hf * c.FCH + icl
                        ins = dve.scalar_tensor_tensor(hf32[j % 2], pu, sb_us[:, i2:i2 + 1],
                                                       sgbuf[j % 2], OP.mult, OP.mult)
                        inc(ins, "hf32s", 1)
                        w(dve, "hhi", B0["hhi"] + j + 1)
                        ins = dve.tensor_sub(hl[:, TS(icl, c.TILE)], hf32[j % 2],
                                             hh[:, TS(icl, c.TILE)])
                        inc(ins, "hev", 1)
                        ln_pop(LF, "dve", dve)
                    if hf == 1:
                        ln_drain(LF, "dve", dve)
                    w(dve, "shmm", B0["shmm"] + hf + 1)
                    chain(dve, dve.tensor_copy(sumh_h[:, 0:c.TILE], p_sumh))
                    ins = dve.tensor_sub(sumh_l[:, 0:c.TILE], p_sumh, sumh_h[:, 0:c.TILE])
                    inc(ins, "shev", 1)
                    for ch in range(c.DC):
                        j = hf * c.DC + ch
                        w(dve, "dnmm", B0["dnmm"] + j + 1)
                        pb = p_mm[j % 2]
                        if hf == 0:
                            ins = dve.tensor_copy(accum[:, TS(ch, c.TILE)], pb)
                            inc(ins, "acc", 1)
                        else:
                            w(dve, "t0s", B0["t0s"] + ch + 1)
                            ins = dve.scalar_tensor_tensor(t0buf[ch % 2], pb,
                                                           sb_ds[:, ch:ch + 1],
                                                           t0buf[ch % 2],
                                                           OP.mult, OP.add)
                            inc(ins, "fin", 1)
                            w(dve, "fin", CNT["fin"])
                            wp(dve, B0, "resld", ch, 16)
                            chain(dve, dve.tensor_add(t0buf[ch % 2], t0buf[ch % 2],
                                                      resbuf[ch % 2]))
                            wp(dve, B0, "xld", ch, 16)
                            ins = dve.tensor_add(xbuf[ch % 2], t0buf[ch % 2], xbuf[ch % 2])
                            inc(ins, "obuf", 1)
                            ln_pop(LF, "dve2", dve)
                            ln_pop(LF, "dve2", dve)
                ln_drain(LF, "dve2", dve)

            @blk.gpsimd
            def _(pl):
                w(pl, "halo_st", B0["halo_st"])
                for hf in range(2):
                    for icl in range(c.FCH):
                        j = hf * c.FCH + icl
                        if j >= 2:
                            w(pl, "gmm", B0["gmm"] + j - 1)
                        g_, u_ = gu_ring[j % 2]
                        ic = hf * c.FCH + icl
                        srcg = gate_w_t[:, TS(ic, 128)].rearrange("(c p) n -> p c n", p=128)
                        srcu = up_w_t[:, TS(ic, 128)].rearrange("(c p) n -> p c n", p=128)
                        incp(pl.dma_start(sb3(g_, 128), srcg), "gld", j)
                        if j >= 2:
                            w(pl, "umm", B0["umm"] + j - 1)
                        incp(pl.dma_start(sb3(u_, 128), srcu), "uld", j)
                    if hf == 0:
                        for ch in range(c.DC):
                            if ch >= 2:
                                w(pl, "dnmm", B0["dnmm"] + ch - 1)
                            fh = c.FCH // 2
                            for a, nm in ((0, "dld"), (1, "dld2")):
                                srcd = down_w_t[a * c.FF4: (a + 1) * c.FF4,
                                                TS(ch, 128)].rearrange(
                                    "(c p) n -> p c n", p=128)
                                dst = dn_ring[ch % 2][:, a * fh * 128:(a + 1) * fh * 128]
                                incp(pl.dma_start(sb3(dst, 128), srcd), nm, ch)
                wp_all(pl, B0, "resst")
                for ch in range(c.DC):
                    j = c.DC + ch
                    w(pl, "dnmm", B0["dnmm"] + j - 1)
                    fh = c.FCH // 2
                    for a, nm in ((0, "dld"), (1, "dld2")):
                        srcd = down_w_t[c.FF2 + a * c.FF4: c.FF2 + (a + 1) * c.FF4,
                                        TS(ch, 128)].rearrange(
                            "(c p) n -> p c n", p=128)
                        dst = dn_ring[j % 2][:, a * fh * 128:(a + 1) * fh * 128]
                        incp(pl.dma_start(sb3(dst, 128), srcd), nm, j)
                    if ch >= 2:
                        w(pl, "obuf", B0["obuf"] + ch - 1)
                        wp(pl, B0, "outst", ch - 2, 16)
                    else:
                        nm = f"outst{ch % 2}"
                        if B0[nm]:
                            w(pl, nm, B0[nm])
                    incp(pl.dma_start(resbuf[ch % 2], s_res[ch]), "resld", ch)
                    src = x_fm[TS(ch, 128), tok0: tok0 + c.TILE]
                    incp(pl.dma_start(xbuf[ch % 2], src), "xld", ch)
                    if ch >= 1:
                        w(pl, "obuf", B0["obuf"] + ch)
                        incp(pl.dma_start(out_fm[TS(ch - 1, 128), TS(t, c.TILE)],
                                          xbuf[(ch - 1) % 2]), "outst", ch - 1)
                w(pl, "obuf", B0["obuf"] + c.DC)
                incp(pl.dma_start(out_fm[TS(c.DC - 1, 128), TS(t, c.TILE)],
                                  xbuf[(c.DC - 1) % 2]), "outst", c.DC - 1)

    with nc.Block() as blk:
        @blk.gpsimd
        def _(pl):
            for name in ["outst0", "outst1", "resst0", "resst1", "halo_st",
                         "wst0", "wst1", "gld0", "gld1", "uld0", "uld1",
                         "dld0", "dld1", "dld20", "dld21",
                         "resld0", "resld1", "xld0", "xld1"]:
                if CNT[name]:
                    pl.wait_ge(S[name], CNT[name])

        @blk.sync
        def _(sp):
            for name in ["wld0", "wld1", "xtm0", "lnx0", "lnx1", "halo_ld",
                         "ld_const"]:
                if CNT[name]:
                    sp.wait_ge(S[name], CNT[name])

        @blk.scalar
        def _(act):
            for name in ["pxtm0", "plnx0", "plnx1", "prowsd", "rowsd"]:
                if CNT[name]:
                    act.wait_ge(S[name], CNT[name])

    return nc


def make_in_maps(cfg, inputs, n_cores=N_CORES):
    c = cfg
    x = np.asarray(inputs["L0_attn_x"], dtype=np.float32)
    B, T, D = x.shape
    n_seq_shards = n_cores // B
    SEQ = T // n_seq_shards
    assert SEQ == c.TOK and D == c.D

    def pack_slabs(wt, nslab):
        arr = wt.reshape(c.DC, 128, nslab, c.SLAB)
        return np.ascontiguousarray(
            arr.transpose(1, 2, 0, 3).reshape(128, nslab * c.DC * c.SLAB))

    w_attn_t = pack_slabs(np.asarray(inputs["L0_attn_w_attn"], np.float32).T,
                          c.NSLAB_A)
    w_o_t = pack_slabs(np.asarray(inputs["L0_attn_w_o"], np.float32).T, c.NSLAB_O)
    gate_w_t = np.ascontiguousarray(np.asarray(inputs["L0_ffn_gate_w"], np.int32).T)
    up_w_t = np.ascontiguousarray(np.asarray(inputs["L0_ffn_up_w"], np.int32).T)
    down_w_t = np.ascontiguousarray(np.asarray(inputs["L0_ffn_down_w"], np.int32).T)
    gate_s = np.ascontiguousarray(np.asarray(inputs["L0_ffn_gate_scale"], np.float32).reshape(c.FC, 128).T)
    up_s = np.ascontiguousarray(np.asarray(inputs["L0_ffn_up_scale"], np.float32).reshape(c.FC, 128).T)
    down_s = np.ascontiguousarray(np.asarray(inputs["L0_ffn_down_scale"], np.float32).reshape(c.DC, 128).T)

    jj = np.arange(c.W)[:, None]
    ii = np.arange(c.W)[None, :]
    m_prev = (jj >= ii).astype(np.float32)
    m_cur = (jj <= ii).astype(np.float32)
    m_zero = np.zeros((c.W, c.W), np.float32)
    ident = np.eye(128, dtype=np.float32)

    in_maps = []
    for core in range(n_cores):
        b = core // n_seq_shards
        sh = core % n_seq_shards
        xt = np.zeros((c.TOKH, c.D), np.float32)
        t0 = sh * SEQ
        if sh > 0:
            xt[:] = x[b, t0 - c.W: t0 + SEQ]
        else:
            xt[c.W:] = x[b, 0:SEQ]
        x_tm_a = np.ascontiguousarray(xt)
        x_fm_a = np.ascontiguousarray(xt.T)
        m_first = m_prev if sh > 0 else m_zero
        import ml_dtypes
        masks = np.ascontiguousarray(
            np.concatenate([m_prev, m_cur, m_first], axis=1).astype(ml_dtypes.bfloat16))
        in_maps.append({
            "x_fm": x_fm_a, "x_tm": x_tm_a,
            "w_attn_t": w_attn_t, "w_o_t": w_o_t,
            "gate_w_t": gate_w_t, "up_w_t": up_w_t, "down_w_t": down_w_t,
            "gate_s": gate_s, "up_s": up_s, "down_s": down_s,
            "masks": masks, "ident": ident,
        })
    return in_maps


def kernel(**inputs):
    cfg = CFG()
    nc = build(cfg)
    in_maps = make_in_maps(cfg, inputs)
    res = run_bass_kernel_spmd(nc, in_maps, core_ids=list(range(N_CORES)))
    x = np.asarray(inputs["L0_attn_x"])
    B, T, D = x.shape
    n_seq_shards = N_CORES // B
    SEQ = T // n_seq_shards
    out = np.empty((B, T, D), np.float32)
    for core in range(N_CORES):
        b = core // n_seq_shards
        sh = core % n_seq_shards
        o = np.asarray(res.results[core]["out_fm"])
        out[b, sh * SEQ:(sh + 1) * SEQ, :] = o.T
    return out


# version 15
# speedup vs baseline: 1.1244x; 1.1244x over previous
import numpy as np
import concourse.bass as bass
import concourse.mybir as mybir
from concourse.bass_utils import run_bass_kernel_spmd

F32 = mybir.dt.float32
BF16 = mybir.dt.bfloat16
I32 = mybir.dt.int32
AF = mybir.ActivationFunctionType
OP = mybir.AluOpType

N_CORES = 8


class CFG:
    def __init__(self, D=2048, FF=8192, TOK=2048, TILE=512, W=128, SLAB=256):
        self.D, self.FF, self.TOK, self.TILE, self.W = D, FF, TOK, TILE, W
        self.SLAB = SLAB
        self.DC = D // 128
        self.FC = FF // 128
        self.FCH = self.FC // 2
        self.FF2 = FF // 2
        self.FF4 = FF // 4
        self.NT = TOK // TILE
        self.NB = TILE // W
        self.NSUB = TILE // 128
        self.TOKH = TOK + W
        self.NSLAB_A = 3 * D // SLAB
        self.NSLAB_O = D // SLAB
        self.SJ = SLAB // 128
        self.NQ = D // SLAB


def TS(i, n):
    return slice(i * n, (i + 1) * n)


def build(cfg: CFG):
    c = cfg
    nc = bass.Bass(target_bir_lowering=False, trn_type="TRN2")

    x_fm = nc.dram_tensor("x_fm", [c.D, c.TOKH], F32, kind="ExternalInput").ap()
    x_tm = nc.dram_tensor("x_tm", [c.TOKH, c.D], F32, kind="ExternalInput").ap()
    w_attn_t = nc.dram_tensor("w_attn_t", [128, c.NSLAB_A * c.DC * c.SLAB], F32,
                              kind="ExternalInput").ap()
    w_o_t = nc.dram_tensor("w_o_t", [128, c.NSLAB_O * c.DC * c.SLAB], F32,
                           kind="ExternalInput").ap()
    gate_w_t = nc.dram_tensor("gate_w_t", [c.D, c.FF], I32, kind="ExternalInput").ap()
    up_w_t = nc.dram_tensor("up_w_t", [c.D, c.FF], I32, kind="ExternalInput").ap()
    down_w_t = nc.dram_tensor("down_w_t", [c.FF, c.D], I32, kind="ExternalInput").ap()
    gate_s_in = nc.dram_tensor("gate_s", [128, c.FC], F32, kind="ExternalInput").ap()
    up_s_in = nc.dram_tensor("up_s", [128, c.FC], F32, kind="ExternalInput").ap()
    down_s_in = nc.dram_tensor("down_s", [128, c.DC], F32, kind="ExternalInput").ap()
    masks_in = nc.dram_tensor("masks", [128, 3 * c.W], BF16, kind="ExternalInput").ap()
    ident_in = nc.dram_tensor("ident", [128, 128], F32, kind="ExternalInput").ap()
    out_fm = nc.dram_tensor("out_fm", [c.D, c.TOK], F32, kind="ExternalOutput").ap()

    s_wah = nc.dram_tensor("s_wah", [c.NSLAB_A, 128, c.DC * c.SLAB], BF16, kind="Internal").ap()
    s_wal = nc.dram_tensor("s_wal", [c.NSLAB_A, 128, c.DC * c.SLAB], BF16, kind="Internal").ap()
    s_woh = nc.dram_tensor("s_woh", [c.NSLAB_O, 128, c.DC * c.SLAB], BF16, kind="Internal").ap()
    s_wol = nc.dram_tensor("s_wol", [c.NSLAB_O, 128, c.DC * c.SLAB], BF16, kind="Internal").ap()
    s_res = nc.dram_tensor("s_res", [c.DC, 128, c.TILE], F32, kind="Internal").ap()
    s_rows = nc.dram_tensor("s_rows", [2 * 4, 128], F32, kind="Internal").ap()
    s_khalo_h = nc.dram_tensor("s_khalo_h", [128, c.DC * c.W], BF16, kind="Internal").ap()
    s_khalo_l = nc.dram_tensor("s_khalo_l", [128, c.DC * c.W], BF16, kind="Internal").ap()
    s_vhalo_h = nc.dram_tensor("s_vhalo_h", [128, c.D], BF16, kind="Internal").ap()
    s_vhalo_l = nc.dram_tensor("s_vhalo_l", [128, c.D], BF16, kind="Internal").ap()

    cur = [nc.sbuf_base + 64]
    ded = {}

    def salloc(name, shape, dtype, offset=None):
        if offset is None:
            off = (cur[0] + 63) // 64 * 64
            cur[0] = off + shape[-1] * mybir.dt.size(dtype)
        else:
            off = offset
        t = nc.alloc_sbuf_tensor_at(name, shape, dtype, offset=off)
        ded[name] = off
        return t.ap()

    sb_masks = salloc("sb_masks", [128, 3 * c.W], BF16)
    sb_gs = salloc("sb_gs", [128, c.FC], F32)
    sb_us = salloc("sb_us", [128, c.FC], F32)
    sb_ds = salloc("sb_ds", [128, c.DC], F32)
    sb_ident = salloc("sb_ident", [128, 128], F32)
    sb_ones = salloc("sb_ones", [2, 128], BF16)
    sb_ones_col = salloc("sb_ones_col", [128, 1], BF16)
    sb_neg128 = salloc("sb_neg128", [1, 128], BF16)
    sb_stat = salloc("sb_stat", [128, 8 * c.NSUB], F32)
    sb_pk = salloc("sb_pk", [128, 2 * c.NSUB], F32)
    sb_t8 = salloc("sb_t8", [2 * c.NSUB, 128], F32)
    sb_mu_bh = salloc("sb_mu_bh", [1, c.TILE], BF16)
    sb_mu_bl = salloc("sb_mu_bl", [1, c.TILE], BF16)
    sb_r_bh = salloc("sb_r_bh", [1, c.TILE], BF16)
    sb_r_bl = salloc("sb_r_bl", [1, c.TILE], BF16)
    sb_ah = salloc("sb_ah", [128, 2 * c.W], BF16)
    sb_al = salloc("sb_al", [128, 2 * c.W], BF16)
    sb_ah2 = salloc("sb_ah2", [128, 2 * c.W], BF16)
    sb_al2 = salloc("sb_al2", [128, 2 * c.W], BF16)
    sb_rbuf = salloc("sb_rbuf", [128, 2 * c.W], F32)
    sb_ambuf = salloc("sb_ambuf", [128, 2 * c.W], F32)
    sumh_h = salloc("sumh_h", [1, c.TILE], BF16, offset=ded["sb_rbuf"])
    sumh_l = salloc("sumh_l", [1, c.TILE], BF16, offset=ded["sb_ambuf"])
    reschunk = [salloc(f"reschunk{i}", [128, c.TILE], F32) for i in range(2)]

    small_end = (cur[0] + 63) // 64 * 64

    xn_half = c.DC * c.TILE * 2
    h_half = c.FCH * c.TILE * 2
    XN_B = small_end
    A_B = XN_B + 2 * xn_half
    prep_end = 2 * xn_half + 2 * (2 * c.DC * c.SLAB * 2)
    halo_base = max(4 * xn_half, prep_end)
    A_SZ = max(6 * xn_half,
               2 * h_half + 2 * (2 * c.DC * 256) + 2 * c.FCH * 256,
               halo_base + 2 * c.DC * c.W * 2 + 2 * c.D * 2,
               2 * (2 * c.DC * c.SLAB * 2) + 2 * c.FCH * 256)
    B_B = A_B + A_SZ
    B_SZ = max(c.DC * c.TILE * 4, 10 * c.TILE * 4 + c.D * 4 + 2 * c.NSUB * 512,
               2 * 4 * max(c.DC * c.SLAB, c.FCH * 128, c.DC * 128))
    C_B = B_B + B_SZ
    C_SZ = max(c.DC * c.TILE * 4,
               2 * (2 * c.DC * c.SLAB * 2),
               2 * c.DC * c.W * 2 + 2 * c.D * 2)
    TOP = C_B + C_SZ
    assert TOP <= nc.sbuf_top - 64, (TOP, nc.sbuf_top)

    xnh = salloc("xnh", [128, c.DC * c.TILE], BF16, offset=XN_B)
    xnl = salloc("xnl", [128, c.DC * c.TILE], BF16, offset=XN_B + xn_half)

    qh = salloc("qh", [128, c.DC * c.TILE], BF16, offset=A_B + 0 * xn_half)
    ql = salloc("ql", [128, c.DC * c.TILE], BF16, offset=A_B + 1 * xn_half)
    kh = salloc("kh", [128, c.DC * c.TILE], BF16, offset=A_B + 2 * xn_half)
    kl = salloc("kl", [128, c.DC * c.TILE], BF16, offset=A_B + 3 * xn_half)
    vh = salloc("vh", [128, c.NSUB * c.D], BF16, offset=A_B + 4 * xn_half)
    vl = salloc("vl", [128, c.NSUB * c.D], BF16, offset=A_B + 5 * xn_half)
    hh = salloc("hh", [128, c.FCH * c.TILE], BF16, offset=A_B)
    hl = salloc("hl", [128, c.FCH * c.TILE], BF16, offset=A_B + h_half)
    off = A_B + 2 * h_half
    gu_ring = []
    for i in range(2):
        g = salloc(f"gbuf{i}", [128, c.DC * 128], BF16, offset=off); off += c.DC * 256
        u = salloc(f"ubuf{i}", [128, c.DC * 128], BF16, offset=off); off += c.DC * 256
        gu_ring.append((g, u))
    dn_ring = []
    for i in range(2):
        d = salloc(f"dbuf{i}", [128, c.FCH * 128], BF16, offset=off); off += c.FCH * 256
        dn_ring.append(d)
    assert off <= A_B + A_SZ
    off = A_B + 2 * xn_half
    prep_pair = []
    for i in range(2):
        ph = salloc(f"preph{i}", [128, c.DC * c.SLAB], BF16, offset=off); off += c.DC * c.SLAB * 2
        pl_ = salloc(f"prepl{i}", [128, c.DC * c.SLAB], BF16, offset=off); off += c.DC * c.SLAB * 2
        prep_pair.append((ph, pl_))
    assert off <= A_B + A_SZ
    hoff = A_B + halo_base
    khalo_pr_h = salloc("khalo_pr_h", [128, c.DC * c.W], BF16, offset=hoff); hoff += c.DC * c.W * 2
    khalo_pr_l = salloc("khalo_pr_l", [128, c.DC * c.W], BF16, offset=hoff); hoff += c.DC * c.W * 2
    vhalo_pr_h = salloc("vhalo_pr_h", [128, c.D], BF16, offset=hoff); hoff += c.D * 2
    vhalo_pr_l = salloc("vhalo_pr_l", [128, c.D], BF16, offset=hoff)

    oh = salloc("oh", [128, c.DC * c.TILE], BF16, offset=B_B)
    ol = salloc("ol", [128, c.DC * c.TILE], BF16, offset=B_B + xn_half)
    off = B_B
    prep_in = []
    PREP_N = c.DC * c.SLAB
    for i in range(2):
        t_ = salloc(f"prepin{i}", [128, PREP_N], F32, offset=off)
        off += PREP_N * 4
        prep_in.append(t_)
    assert off <= B_B + B_SZ
    off = B_B
    sgbuf, hf32, t0buf, resbuf, xbuf = [], [], [], [], []
    for i in range(2):
        sgbuf.append(salloc(f"sgbuf{i}", [128, c.TILE], F32, offset=off)); off += c.TILE * 4
        hf32.append(salloc(f"hf32_{i}", [128, c.TILE], F32, offset=off)); off += c.TILE * 4
        t0buf.append(salloc(f"t0buf{i}", [128, c.TILE], F32, offset=off)); off += c.TILE * 4
        resbuf.append(salloc(f"resbuf{i}", [128, c.TILE], F32, offset=off)); off += c.TILE * 4
        xbuf.append(salloc(f"xbuf{i}", [128, c.TILE], F32, offset=off)); off += c.TILE * 4
    statin_b = salloc("statin_b", [128, c.D], F32, offset=off); off += c.D * 4
    xring = [statin_b[:, TS(i, c.TILE)] for i in range(2)]
    sb_rowsg = salloc("sb_rowsg", [1, 2 * c.NSUB * 128], F32, offset=off)
    off += 2 * c.NSUB * 512
    assert off <= B_B + B_SZ, (off, B_B + B_SZ)

    accum = salloc("accum", [128, c.DC * c.TILE], F32, offset=C_B)
    off = C_B
    ws_ring = []
    for i in range(2):
        wh_ = salloc(f"wsh{i}", [128, c.DC * c.SLAB], BF16, offset=off); off += c.DC * c.SLAB * 2
        wl_ = salloc(f"wsl{i}", [128, c.DC * c.SLAB], BF16, offset=off); off += c.DC * c.SLAB * 2
        ws_ring.append((wh_, wl_))
    assert off <= C_B + C_SZ
    off = C_B
    khalo_sb_h = salloc("khalo_sb_h", [128, c.DC * c.W], BF16, offset=off); off += c.DC * c.W * 2
    khalo_sb_l = salloc("khalo_sb_l", [128, c.DC * c.W], BF16, offset=off); off += c.DC * c.W * 2
    vhalo_sb_h = salloc("vhalo_sb_h", [128, c.D], BF16, offset=off); off += c.D * 2
    vhalo_sb_l = salloc("vhalo_sb_l", [128, c.D], BF16, offset=off); off += c.D * 2
    assert off <= C_B + C_SZ

    P = [nc.alloc_psum_tensor(f"pb{i}", [128, 512], F32).ap() for i in range(8)]
    p_mm = [P[0][:, 0:c.TILE], P[1][:, 0:c.TILE], P[2][:, 0:c.TILE], P[3][:, 0:c.TILE]]
    p_at2 = [P[4][:, 0:256], P[2][:, 0:256]]
    p_o = [P[5][:, 0:128], P[6][:, 0:128]]
    p_bc_r = P[5]
    p_bc_mu = P[6]
    p_tr = P[7]
    p_sumh = P[4][0:1, 0:c.TILE]

    from contextlib import ExitStack
    stack = ExitStack()
    nc._kernel_stack = stack
    S, CNT = {}, {}
    for n in (["ld_const", "prep_done", "shmm", "shev",
               "xfm_ld", "stat", "vpe", "sqrt", "pack", "tmm", "rows",
               "xnf", "xnh_c",
               "qkmm", "qkhi", "qkev", "vmm", "vhi", "vev",
               "halo_ld", "halo_st", "atmm", "at_relu", "at_ev", "omm", "ohi", "oev",
               "womm", "wocp",
               "gmm", "umm", "sg", "hf32s", "hhi", "hev",
               "dnmm", "acc", "fin", "t0s", "obuf", "dvch", "scch", "t8c", "rowsd"]
              + ["xnl_c", "pxtm0", "prowsd"]
              + [f"{p}{i}" for p in ("wld", "wst", "wsth", "xtm", "gld", "uld",
                                     "lnx", "plnx",
                                     "dld", "dld2", "resld", "xld", "resst", "outst")
                 for i in range(2)]):
        S[n] = stack.enter_context(nc.semaphore(n))
        CNT[n] = 0

    def inc(ins, name, n=1):
        ins.then_inc(S[name], n)
        CNT[name] += n

    def w(eng, name, val):
        eng.wait_ge(S[name], val)

    def incp(ins, prefix, idx, n=16):
        inc(ins, f"{prefix}{idx % 2}", n)

    def wp(eng, B0, prefix, idx, per):
        nm = f"{prefix}{idx % 2}"
        eng.wait_ge(S[nm], B0[nm] + per * (idx // 2 + 1))

    def wp_all(eng, B0, prefix):
        for i in range(2):
            if CNT[f"{prefix}{i}"]:
                eng.wait_ge(S[f"{prefix}{i}"], CNT[f"{prefix}{i}"])

    def chain(eng, ins, name="dvch"):
        inc(ins, name, 1)
        eng.wait_ge(S[name], CNT[name])

    def hbm_cols(t, c0, c1):
        return t[:, c0:c1].rearrange("(c p) n -> p c n", p=128)

    def sb3(t, width):
        return t.rearrange("p (c n) -> p c n", n=width)

    D_inv = 1.0 / c.D
    EPS = 1e-5

    def ln_vec_stats(dve, s):
        ns = c.NSUB
        s1 = sb_stat[:, s:s + 1]
        s2 = sb_stat[:, ns + s:ns + s + 1]
        mu = sb_stat[:, 2 * ns + s:2 * ns + s + 1]
        e2 = sb_stat[:, 3 * ns + s:3 * ns + s + 1]
        msq = sb_stat[:, 4 * ns + s:4 * ns + s + 1]
        vpe = sb_stat[:, 5 * ns + s:5 * ns + s + 1]
        dve.tensor_scalar_mul(mu, s1, D_inv)
        chain(dve, dve.tensor_scalar_mul(e2, s2, D_inv))
        chain(dve, dve.scalar_tensor_tensor(msq, mu, -1.0, mu, OP.mult, OP.mult))
        ins = dve.scalar_tensor_tensor(vpe, e2, EPS, msq, OP.add, OP.add)
        inc(ins, "vpe", 1)

    def ln_vec_newton(dve, s):
        ns = c.NSUB
        mu = sb_stat[:, 2 * ns + s:2 * ns + s + 1]
        vpe = sb_stat[:, 5 * ns + s:5 * ns + s + 1]
        r0 = sb_stat[:, 6 * ns + s:6 * ns + s + 1]
        tn = sb_stat[:, 7 * ns + s:7 * ns + s + 1]
        chain(dve, dve.reciprocal(r0, tn))
        for _ in range(2):
            chain(dve, dve.tensor_mul(tn, r0, r0))
            chain(dve, dve.tensor_mul(tn, tn, vpe))
            chain(dve, dve.tensor_scalar(tn, tn, -0.5, 1.5, OP.mult, OP.add))
            chain(dve, dve.tensor_mul(r0, r0, tn))
        chain(dve, dve.tensor_mul(sb_pk[:, s:s + 1], mu, r0))
        ins = dve.tensor_copy(sb_pk[:, ns + s:ns + s + 1], r0)
        inc(ins, "pack", 1)

    def ln_ops(B0, n_sub, wid, xrow0, xcol0, xd_h, xd_l, overwrite_gate=None,
               defer_xn=False, dsem=("xtm0", "lnx", "rowsd")):
        L = {"dma": [], "act": [], "dve": [], "pe": [], "pl": []}

        def sp_xtm(s):
            def f(sp):
                if s >= 1:
                    w(sp, "stat", B0["stat"] + 2 * s)
                ins = sp.dma_start(statin_b, x_tm[xrow0 + s * 128: xrow0 + (s + 1) * 128, :])
                inc(ins, dsem[0], 16)
            return f

        def sp_x(dc):
            def f(sp):
                if dc < 2:
                    w(sp, "stat", B0["stat"] + 2 * n_sub)
                else:
                    w(sp, "xnl_c", B0["xnl_c"] + dc - 1)
                ins = sp.dma_start(xring[dc % 2][:, 0:wid],
                                   x_fm[TS(dc, 128), xcol0: xcol0 + wid])
                incp(ins, dsem[1], dc)
            return f

        def sp_rowsd(sp):
            w(sp, "t8c", B0["t8c"] + 1)
            ins = sp.dma_start(s_rows[0:2 * c.NSUB, :], sb_t8)
            inc(ins, dsem[2], 16)
            w(sp, dsem[2], B0[dsem[2]] + 16)
            dst_ = sb_rowsg.rearrange("p (a n) -> p a n", n=128)
            src_ = s_rows[0:2 * c.NSUB, :].rearrange("a n -> a () n") \
                if False else s_rows[0:2 * c.NSUB, :]
            ins = sp.dma_start(dst_, src_)
            inc(ins, dsem[2], 16)

        L["dma"] = ([sp_xtm(s) for s in range(n_sub)]
                    + [sp_x(0), sp_x(1), sp_rowsd]
                    + [sp_x(dc) for dc in range(2, c.DC)])

        def act_stats(s):
            def f(act):
                w(act, dsem[0], B0[dsem[0]] + 16 * (s + 1))
                if s >= 1:
                    w(act, "stat", B0["stat"] + 2 * s)
                ins = act.activation(statin_b, statin_b, AF.Copy,
                                     accum_out=sb_stat[:, s:s + 1])
                inc(ins, "stat", 1)
                w(act, "stat", B0["stat"] + 2 * s + 1)
                ins = act.activation(statin_b, statin_b, AF.Square,
                                     accum_out=sb_stat[:, c.NSUB + s:c.NSUB + s + 1])
                inc(ins, "stat", 1)
            return f

        def act_sqrt(s):
            def f(act):
                w(act, "vpe", B0["vpe"] + s + 1)
                ins = act.activation(sb_stat[:, 7 * c.NSUB + s:7 * c.NSUB + s + 1],
                                     sb_stat[:, 5 * c.NSUB + s:5 * c.NSUB + s + 1], AF.Sqrt)
                inc(ins, "sqrt", 1)
            return f

        L["act"] = ([None] + [act_stats(s) for s in range(n_sub)]
                    + [act_sqrt(s) for s in range(n_sub)])
        L["act2"] = []

        def dve_vstats(s):
            def f(dve):
                w(dve, "stat", B0["stat"] + 2 * (s + 1))
                ln_vec_stats(dve, s)
            return f

        def dve_newton(s):
            def f(dve):
                w(dve, "sqrt", B0["sqrt"] + s + 1)
                ln_vec_newton(dve, s)
            return f

        def dve_t8c(dve):
            w(dve, "tmm", B0["tmm"] + 1)
            ins = dve.tensor_copy(sb_t8, p_tr[0:2 * c.NSUB, 0:128])
            inc(ins, "t8c", 1)

        def dve_rowpair(dve):
            w(dve, dsem[2], B0[dsem[2]] + 32)
            mur = sb_rowsg[:, 0:wid]
            rr = sb_rowsg[:, c.NSUB * 128: c.NSUB * 128 + wid]
            dve.tensor_copy(sb_mu_bh[:, 0:wid], mur)
            chain(dve, dve.tensor_copy(sb_r_bh[:, 0:wid], rr))
            dve.tensor_sub(sb_mu_bl[:, 0:wid], mur, sb_mu_bh[:, 0:wid])
            ins = dve.tensor_sub(sb_r_bl[:, 0:wid], rr, sb_r_bh[:, 0:wid])
            inc(ins, "rows", 1)

        def dve_xn(dc):
            def f(dve):
                if dc == 0:
                    w(dve, "tmm", B0["tmm"] + 3)
                    if overwrite_gate is not None:
                        w(dve, overwrite_gate[0], overwrite_gate[1])
                wp(dve, B0, dsem[1], dc, 16)
                xc = xring[dc % 2][:, 0:wid]
                chain(dve, dve.tensor_mul(xc, xc, p_bc_r[:, 0:wid]))
                chain(dve, dve.tensor_sub(xc, xc, p_bc_mu[:, 0:wid]))
                chain(dve, dve.tensor_copy(xd_h[:, dc * c.TILE: dc * c.TILE + wid], xc))
                ins = dve.tensor_sub(xd_l[:, dc * c.TILE: dc * c.TILE + wid], xc,
                                     xd_h[:, dc * c.TILE: dc * c.TILE + wid])
                inc(ins, "xnl_c", 1)
            return f

        tail = [dve_xn(dc) for dc in range(c.DC)]
        head = [None] * 2 + [dve_vstats(s) for s in range(n_sub)] \
            + [dve_newton(s) for s in range(n_sub)]
        head += [None] * (12 - len(head)) + [dve_t8c, None, dve_rowpair]
        if defer_xn:
            L["dve"] = head
            L["dve2"] = tail
        else:
            L["dve"] = head + tail
            L["dve2"] = []

        def pe_tr(pe):
            w(pe, "pack", B0["pack"] + n_sub)
            ins = pe.transpose(p_tr[0:2 * c.NSUB, 0:128], sb_pk, sb_ident)
            inc(ins, "tmm", 1)

        def pe_bc(pe):
            w(pe, "rows", B0["rows"] + 1)
            pe.matmul(p_bc_mu[:, 0:wid], sb_ones[0:1, :], sb_mu_bh[:, 0:wid],
                      start=True, stop=False)
            ins = pe.matmul(p_bc_mu[:, 0:wid], sb_ones[0:1, :], sb_mu_bl[:, 0:wid],
                            start=False, stop=True)
            inc(ins, "tmm", 1)
            pe.matmul(p_bc_r[:, 0:wid], sb_ones[0:1, :], sb_r_bh[:, 0:wid],
                      start=True, stop=False)
            ins = pe.matmul(p_bc_r[:, 0:wid], sb_ones[0:1, :], sb_r_bl[:, 0:wid],
                            start=False, stop=True)
            inc(ins, "tmm", 1)

        L["pe"] = [pe_tr, pe_bc]
        L["pl"] = []
        return L

    def ln_pop(L, eng_name, eng):
        if L is not None and L[eng_name]:
            f = L[eng_name].pop(0)
            if f is not None:
                f(eng)

    def ln_drain(L, eng_name, eng):
        if L is None:
            return
        while L[eng_name]:
            f = L[eng_name].pop(0)
            if f is not None:
                f(eng)

    B0 = dict(CNT)
    with nc.Block() as blk:
        @blk.sync
        def _(sp):
            for dst, src in ((sb_masks, masks_in), (sb_gs, gate_s_in), (sb_us, up_s_in),
                             (sb_ds, down_s_in), (sb_ident, ident_in)):
                inc(sp.dma_start(dst, src), "ld_const", 16)
            sp.wait_ge(S["ld_const"], CNT["ld_const"])

        @blk.vector
        def _(dve):
            dve.memset(sb_ones, 1.0)
            dve.memset(sb_ones_col, 1.0)
            dve.memset(sb_neg128, -128.0)
            dve.memset(sb_pk, 0.0)
            dve.memset(sb_stat, 0.0)

    NSL = c.NSLAB_A + c.NSLAB_O
    B0 = dict(CNT)
    with nc.Block() as blk:
        HSL = c.DC // 2 * c.SLAB

        def prep_src(s, h):
            wt = w_attn_t if s < c.NSLAB_A else w_o_t
            s0 = s if s < c.NSLAB_A else s - c.NSLAB_A
            base = s0 * c.DC * c.SLAB + h * HSL
            return wt[:, base: base + HSL]

        def hi_dst(s):
            return s_wah[s] if s < c.NSLAB_A else s_woh[s - c.NSLAB_A]

        def hi_store(eng, s, h):
            phi = prep_pair[s % 2][0]
            dh = hi_dst(s)
            incp(eng.dma_start(dh[:, h * HSL:(h + 1) * HSL],
                               phi[:, h * HSL:(h + 1) * HSL]), "wsth", s)

        @blk.sync
        def _(sp):
            for s in range(NSL):
                if s >= 2:
                    w(sp, "prep_done", B0["prep_done"] + s - 1)
                dst = prep_in[s % 2][:, 0:HSL]
                incp(sp.dma_start(dst, prep_src(s, 0)), "wld", s)
                if s >= 2:
                    hi_store(sp, s - 2, 0)
            w(sp, "prep_done", B0["prep_done"] + NSL)
            for s in (NSL - 2, NSL - 1):
                hi_store(sp, s, 0)

        @blk.scalar
        def _(act):
            for s in range(NSL):
                if s >= 2:
                    w(act, "prep_done", B0["prep_done"] + s - 1)
                dst = prep_in[s % 2][:, HSL:2 * HSL]
                incp(act.dma_start(dst, prep_src(s, 1)), "wld", s)
                if s >= 2:
                    hi_store(act, s - 2, 1)
            w(act, "prep_done", B0["prep_done"] + NSL)
            for s in (NSL - 2, NSL - 1):
                hi_store(act, s, 1)

        @blk.vector
        def _(dve):
            for s in range(NSL):
                wp(dve, B0, "wld", s, 32)
                if s >= 2:
                    wp(dve, B0, "wst", s - 2, 16)
                    wp(dve, B0, "wsth", s - 2, 32)
                pin = prep_in[s % 2][:, 0:c.DC * c.SLAB]
                phi, plo = prep_pair[s % 2]
                if s < c.NQ:
                    chain(dve, dve.tensor_scalar_mul(phi, pin, 1.0 / c.W))
                    ins = dve.scalar_tensor_tensor(plo, pin, 1.0 / c.W, phi,
                                                   OP.mult, OP.subtract)
                else:
                    chain(dve, dve.tensor_copy(phi, pin))
                    ins = dve.tensor_sub(plo, pin, phi)
                inc(ins, "prep_done", 1)

        @blk.gpsimd
        def _(pl):
            for s in range(NSL):
                w(pl, "prep_done", B0["prep_done"] + s + 1)
                plo = prep_pair[s % 2][1]
                dl = s_wal[s] if s < c.NSLAB_A else s_wol[s - c.NSLAB_A]
                incp(pl.dma_start(dl, plo), "wst", s)
            for i in range(2):
                n_attn = (c.NSLAB_A + 1 - i) // 2
                w(pl, f"wst{i}", B0[f"wst{i}"] + 16 * n_attn)
                w(pl, f"wsth{i}", B0[f"wsth{i}"] + 32 * n_attn)

    WST_FULL = {n: CNT[n] for n in ("wst0", "wst1", "wsth0", "wsth1")}

    B0 = dict(CNT)
    with nc.Block() as blk:
        LH = ln_ops(B0, 1, c.W, 0, 0, qh, ql)

        @blk.sync
        def _(sp):
            ln_drain(LH, "dma", sp)

        @blk.scalar
        def _(act):
            ln_drain(LH, "act", act)

        @blk.vector
        def _(dve):
            ln_drain(LH, "dve", dve)

        @blk.tensor
        def _(pe):
            ln_drain(LH, "pe", pe)


    B0 = dict(CNT)
    with nc.Block() as blk:
        KV = list(range(c.NQ, c.NSLAB_A))
        NK = c.NQ
        L0 = ln_ops(B0, c.NSUB, c.TILE, c.W, c.W, xnh, xnl,
                    dsem=("pxtm0", "plnx", "prowsd"))

        @blk.sync
        def _(sp):
            for i, s in enumerate(KV):
                if i >= 2:
                    pi = i - 2
                    if pi < NK:
                        w(sp, "qkmm", B0["qkmm"] + (pi + 1) * c.SJ)
                    else:
                        w(sp, "vmm", B0["vmm"] + (pi - NK + 1))
                wh_, wl_ = ws_ring[i % 2]
                incp(sp.dma_start(wh_, s_wah[s]), "wld", i)
                incp(sp.dma_start(wl_, s_wal[s]), "wld", i)

        @blk.tensor
        def _(pe):
            for i, s in enumerate(KV):
                wp(pe, B0, "wld", i, 32)
                wh_, wl_ = ws_ring[i % 2]
                if i < NK:
                    for j in range(c.SJ):
                        oc = i * c.SJ + j
                        if oc >= 2:
                            w(pe, "qkev", B0["qkev"] + oc - 1)
                        pb = p_mm[oc % 2]
                        nmm = 0
                        for dc in range(c.DC):
                            lt_h = wh_[:, dc * c.SLAB + j * 128: dc * c.SLAB + (j + 1) * 128]
                            lt_l = wl_[:, dc * c.SLAB + j * 128: dc * c.SLAB + (j + 1) * 128]
                            for lt, rh in ((lt_h, qh), (lt_h, ql), (lt_l, qh)):
                                ins = pe.matmul(pb[:, 0:c.W], lt,
                                                rh[:, dc * c.TILE: dc * c.TILE + c.W],
                                                start=(nmm == 0), stop=(nmm == 3 * c.DC - 1))
                                nmm += 1
                        inc(ins, "qkmm", 1)
                else:
                    g = i - NK
                    if g >= 2:
                        w(pe, "vev", B0["vev"] + g - 1)
                    pb = p_mm[2 + g % 2][:, 0:c.SLAB]
                    nmm = 0
                    for dc in range(c.DC):
                        xh_ = qh[:, dc * c.TILE: dc * c.TILE + c.W]
                        xl_ = ql[:, dc * c.TILE: dc * c.TILE + c.W]
                        wh_c = wh_[:, TS(dc, c.SLAB)]
                        wl_c = wl_[:, TS(dc, c.SLAB)]
                        for lt, rh in ((xh_, wh_c), (xh_, wl_c), (xl_, wh_c)):
                            ins = pe.matmul(pb, lt, rh, start=(nmm == 0),
                                            stop=(nmm == 3 * c.DC - 1))
                            nmm += 1
                    inc(ins, "vmm", 1)
                if i in (5, 8):
                    ln_pop(L0, "pe", pe)
            ln_drain(L0, "pe", pe)

        @blk.scalar
        def _(act):
            for oc in range(NK * c.SJ):
                w(act, "qkmm", B0["qkmm"] + oc + 1)
                ins = act.copy(khalo_pr_h[:, TS(oc, c.W)], p_mm[oc % 2][:, 0:c.W])
                inc(ins, "qkhi", 1)
                ln_pop(L0, "act", act)
            for g in range(c.NSLAB_A - 2 * c.NQ):
                w(act, "vmm", B0["vmm"] + g + 1)
                ins = act.copy(vhalo_pr_h[:, TS(g, c.SLAB)], p_mm[2 + g % 2][:, 0:c.SLAB])
                inc(ins, "vhi", 1)
                ln_pop(L0, "act", act)
            ln_drain(L0, "act", act)

        @blk.vector
        def _(dve):
            for oc in range(NK * c.SJ):
                w(dve, "qkhi", B0["qkhi"] + oc + 1)
                ins = dve.tensor_sub(khalo_pr_l[:, TS(oc, c.W)], p_mm[oc % 2][:, 0:c.W],
                                     khalo_pr_h[:, TS(oc, c.W)])
                inc(ins, "qkev", 1)
                ln_pop(L0, "dve", dve)
            for g in range(c.NSLAB_A - 2 * c.NQ):
                w(dve, "vhi", B0["vhi"] + g + 1)
                ins = dve.tensor_sub(vhalo_pr_l[:, TS(g, c.SLAB)],
                                     p_mm[2 + g % 2][:, 0:c.SLAB],
                                     vhalo_pr_h[:, TS(g, c.SLAB)])
                inc(ins, "vev", 1)
                ln_pop(L0, "dve", dve)
            ln_drain(L0, "dve", dve)

        @blk.gpsimd
        def _(pl):
            ln_drain(L0, "dma", pl)
            w(pl, "qkev", B0["qkev"] + NK * c.SJ)
            w(pl, "vev", B0["vev"] + (c.NSLAB_A - 2 * c.NQ))
            inc(pl.dma_start(s_khalo_h, khalo_pr_h), "halo_st", 16)
            inc(pl.dma_start(s_khalo_l, khalo_pr_l), "halo_st", 16)
            inc(pl.dma_start(s_vhalo_h, vhalo_pr_h), "halo_st", 16)
            inc(pl.dma_start(s_vhalo_l, vhalo_pr_l), "halo_st", 16)

    for t in range(c.NT):
        GL0 = {}
        tok0 = c.W + t * c.TILE
        xnt_h, xnt_l = xnh, xnl
        qt_h, qt_l = qh, ql

        B0 = dict(CNT)
        with nc.Block() as blk:
            @blk.sync
            def _(sp):
                for s in range(c.NSLAB_A):
                    if s >= 2:
                        sc = s - 2
                        if sc < 2 * c.NQ:
                            w(sp, "qkmm", B0["qkmm"] + (sc + 1) * c.SJ)
                        else:
                            w(sp, "vmm", B0["vmm"] + (sc - 2 * c.NQ + 1) * c.NSUB)
                    wh_, wl_ = ws_ring[s % 2]
                    incp(sp.dma_start(wh_, s_wah[s]), "wld", s)
                    incp(sp.dma_start(wl_, s_wal[s]), "wld", s)
                s_le = c.NSLAB_A - 2
                w(sp, "vmm", B0["vmm"] + (s_le - 2 * c.NQ + 1) * c.NSUB)
                w(sp, "halo_st", B0["halo_st"])
                inc(sp.dma_start(khalo_sb_h, s_khalo_h), "halo_ld", 16)
                inc(sp.dma_start(khalo_sb_l, s_khalo_l), "halo_ld", 16)
                inc(sp.dma_start(vhalo_sb_h, s_vhalo_h), "halo_ld", 16)
                inc(sp.dma_start(vhalo_sb_l, s_vhalo_l), "halo_ld", 16)

            @blk.tensor
            def _(pe):
                for s in range(c.NSLAB_A):
                    wp(pe, B0, "wld", s, 32)
                    wh_, wl_ = ws_ring[s % 2]
                    if s < 2 * c.NQ:
                        for j in range(c.SJ):
                            oc = s * c.SJ + j
                            if oc >= 2:
                                w(pe, "qkev", B0["qkev"] + oc - 1)
                            pb = p_mm[oc % 2]
                            nmm = 0
                            for dc in range(c.DC):
                                lt_h = wh_[:, dc * c.SLAB + j * 128: dc * c.SLAB + (j + 1) * 128]
                                lt_l = wl_[:, dc * c.SLAB + j * 128: dc * c.SLAB + (j + 1) * 128]
                                for lt, rh in ((lt_h, xnt_h), (lt_h, xnt_l), (lt_l, xnt_h)):
                                    ins = pe.matmul(pb, lt, rh[:, TS(dc, c.TILE)],
                                                    start=(nmm == 0), stop=(nmm == 3 * c.DC - 1))
                                    nmm += 1
                            inc(ins, "qkmm", 1)
                    else:
                        sv = s - 2 * c.NQ
                        for ts_ in range(c.NSUB):
                            g = sv * c.NSUB + ts_
                            if g >= 2:
                                w(pe, "vev", B0["vev"] + g - 1)
                            pb = p_mm[2 + g % 2][:, 0:c.SLAB]
                            nmm = 0
                            for dc in range(c.DC):
                                xh_ = xnt_h[:, dc * c.TILE + ts_ * 128: dc * c.TILE + (ts_ + 1) * 128]
                                xl_ = xnt_l[:, dc * c.TILE + ts_ * 128: dc * c.TILE + (ts_ + 1) * 128]
                                wh_c = wh_[:, TS(dc, c.SLAB)]
                                wl_c = wl_[:, TS(dc, c.SLAB)]
                                for lt, rh in ((xh_, wh_c), (xh_, wl_c), (xl_, wh_c)):
                                    ins = pe.matmul(pb, lt, rh, start=(nmm == 0),
                                                    stop=(nmm == 3 * c.DC - 1))
                                    nmm += 1
                            inc(ins, "vmm", 1)

            @blk.scalar
            def _(act):
                w(act, "halo_st", B0["halo_st"])
                if t == 0:
                    for n in WST_FULL:
                        w(act, n, WST_FULL[n])
                for s in range(c.NSLAB_A):
                    if s < 2 * c.NQ:
                        for j in range(c.SJ):
                            oc = s * c.SJ + j
                            w(act, "qkmm", B0["qkmm"] + oc + 1)
                            dst = qt_h if oc < c.DC else kh
                            occ = oc if oc < c.DC else oc - c.DC
                            ins = act.copy(dst[:, TS(occ, c.TILE)], p_mm[oc % 2])
                            inc(ins, "qkhi", 1)
                    else:
                        sv = s - 2 * c.NQ
                        for ts_ in range(c.NSUB):
                            g = sv * c.NSUB + ts_
                            w(act, "vmm", B0["vmm"] + g + 1)
                            sl = slice(ts_ * c.D + sv * c.SLAB, ts_ * c.D + (sv + 1) * c.SLAB)
                            ins = act.copy(vh[:, sl], p_mm[2 + g % 2][:, 0:c.SLAB])
                            inc(ins, "vhi", 1)

            @blk.vector
            def _(dve):
                w(dve, "halo_st", B0["halo_st"])
                if t == 0:
                    for n in WST_FULL:
                        w(dve, n, WST_FULL[n])
                for s in range(c.NSLAB_A):
                    if s < 2 * c.NQ:
                        for j in range(c.SJ):
                            oc = s * c.SJ + j
                            w(dve, "qkhi", B0["qkhi"] + oc + 1)
                            dsth = qt_h if oc < c.DC else kh
                            dstl = qt_l if oc < c.DC else kl
                            occ = oc if oc < c.DC else oc - c.DC
                            ins = dve.tensor_sub(dstl[:, TS(occ, c.TILE)], p_mm[oc % 2],
                                                 dsth[:, TS(occ, c.TILE)])
                            inc(ins, "qkev", 1)
                    else:
                        sv = s - 2 * c.NQ
                        for ts_ in range(c.NSUB):
                            g = sv * c.NSUB + ts_
                            w(dve, "vhi", B0["vhi"] + g + 1)
                            sl = slice(ts_ * c.D + sv * c.SLAB, ts_ * c.D + (sv + 1) * c.SLAB)
                            ins = dve.tensor_sub(vl[:, sl], p_mm[2 + g % 2][:, 0:c.SLAB], vh[:, sl])
                            inc(ins, "vev", 1)

        B0 = dict(CNT)
        with nc.Block() as blk:
            @blk.tensor
            def _(pe):
                w(pe, "halo_ld", B0["halo_ld"])

                def at_group(b):
                    if b >= 2:
                        w(pe, "at_ev", B0["at_ev"] + 2 * (b - 1))
                    nmm = 0
                    for half in range(2):
                        pb = p_at2[b % 2][:, half * 128:(half + 1) * 128]
                        for dc in range(c.DC):
                            if half == 0:
                                if b == 0:
                                    kh_ = khalo_sb_h[:, TS(dc, c.W)]
                                    kl_ = khalo_sb_l[:, TS(dc, c.W)]
                                else:
                                    kh_ = kh[:, dc * c.TILE + (b - 1) * c.W: dc * c.TILE + b * c.W]
                                    kl_ = kl[:, dc * c.TILE + (b - 1) * c.W: dc * c.TILE + b * c.W]
                            else:
                                kh_ = kh[:, dc * c.TILE + b * c.W: dc * c.TILE + (b + 1) * c.W]
                                kl_ = kl[:, dc * c.TILE + b * c.W: dc * c.TILE + (b + 1) * c.W]
                            qh_ = qt_h[:, dc * c.TILE + b * c.W: dc * c.TILE + (b + 1) * c.W]
                            ql_ = qt_l[:, dc * c.TILE + b * c.W: dc * c.TILE + (b + 1) * c.W]
                            for lt, rh in ((kh_, qh_), (kh_, ql_), (kl_, qh_)):
                                ins = pe.matmul(pb, lt, rh, start=(nmm == 0),
                                                stop=(nmm == 6 * c.DC - 1))
                                nmm += 1
                    inc(ins, "atmm", 1)

                def o_groups(b):
                    w(pe, "at_ev", B0["at_ev"] + 2 * b + 2)
                    for dc in range(c.DC):
                        g = b * c.DC + dc
                        if g >= 2:
                            w(pe, "oev", B0["oev"] + g - 1)
                        pb = p_o[g % 2]
                        nmm = 0
                        for half in range(2):
                            if half == 0:
                                if b == 0:
                                    vh_ = vhalo_sb_h[:, TS(dc, 128)]
                                    vl_ = vhalo_sb_l[:, TS(dc, 128)]
                                else:
                                    vh_ = vh[:, (b - 1) * c.D + dc * 128:(b - 1) * c.D + (dc + 1) * 128]
                                    vl_ = vl[:, (b - 1) * c.D + dc * 128:(b - 1) * c.D + (dc + 1) * 128]
                            else:
                                vh_ = vh[:, b * c.D + dc * 128: b * c.D + (dc + 1) * 128]
                                vl_ = vl[:, b * c.D + dc * 128: b * c.D + (dc + 1) * 128]
                            ah_ = (sb_ah if b % 2 == 0 else sb_ah2)[:, TS(half, c.W)]
                            al_ = (sb_al if b % 2 == 0 else sb_al2)[:, TS(half, c.W)]
                            for lt, rh in ((vh_, ah_), (vh_, al_), (vl_, ah_)):
                                ins = pe.matmul(pb, lt, rh, start=(nmm == 0), stop=(nmm == 5))
                                nmm += 1
                        inc(ins, "omm", 1)

                for b in range(c.NB):
                    at_group(b)
                    if b >= 1:
                        o_groups(b - 1)
                o_groups(c.NB - 1)

            @blk.scalar
            def _(act):
                wp_all(act, B0, "outst")
                for b in range(c.NB):
                    w(act, "atmm", B0["atmm"] + b + 1)
                    if b >= 1:
                        w(act, "at_ev", B0["at_ev"] + 2 * b)
                    for half in range(2):
                        ins = act.activation(sb_rbuf[:, TS(half, c.W)],
                                             p_at2[b % 2][:, half * 128:(half + 1) * 128],
                                             AF.Relu)
                        inc(ins, "at_relu", 1)
                    for dc in range(c.DC):
                        g = b * c.DC + dc
                        w(act, "omm", B0["omm"] + g + 1)
                        ins = act.copy(oh[:, dc * c.TILE + b * c.W: dc * c.TILE + (b + 1) * c.W],
                                       p_o[g % 2])
                        inc(ins, "ohi", 1)

            @blk.vector
            def _(dve):
                wp_all(dve, B0, "outst")
                for b in range(c.NB):
                    w(dve, "at_relu", B0["at_relu"] + 2 * b + 2)
                    if b >= 2:
                        w(dve, "omm", B0["omm"] + (b - 1) * c.DC)
                    for half in range(2):
                        pb = p_at2[b % 2][:, half * 128:(half + 1) * 128]
                        ins = dve.scalar_tensor_tensor(sb_ambuf[:, TS(half, c.W)], pb, 1.0,
                                                       sb_rbuf[:, TS(half, c.W)],
                                                       OP.mult, OP.mult)
                    chain(dve, ins)
                    for half in range(2):
                        if half == 0:
                            msk = sb_masks[:, 2 * c.W:3 * c.W] if (t == 0 and b == 0) \
                                else sb_masks[:, 0:c.W]
                        else:
                            msk = sb_masks[:, c.W:2 * c.W]
                        ins = dve.tensor_mul(sb_ambuf[:, TS(half, c.W)],
                                             sb_ambuf[:, TS(half, c.W)], msk)
                    chain(dve, ins)
                    ah_b = sb_ah if b % 2 == 0 else sb_ah2
                    al_b = sb_al if b % 2 == 0 else sb_al2
                    for half in range(2):
                        ins = dve.tensor_copy(ah_b[:, TS(half, c.W)],
                                              sb_ambuf[:, TS(half, c.W)])
                    chain(dve, ins)
                    for half in range(2):
                        ins = dve.tensor_sub(al_b[:, TS(half, c.W)],
                                             sb_ambuf[:, TS(half, c.W)],
                                             ah_b[:, TS(half, c.W)])
                        inc(ins, "at_ev", 1)
                    for dc in range(c.DC):
                        g = b * c.DC + dc
                        w(dve, "ohi", B0["ohi"] + g + 1)
                        ins = dve.tensor_sub(ol[:, dc * c.TILE + b * c.W: dc * c.TILE + (b + 1) * c.W],
                                             p_o[g % 2],
                                             oh[:, dc * c.TILE + b * c.W: dc * c.TILE + (b + 1) * c.W])
                        inc(ins, "oev", 1)

            @blk.gpsimd
            def _(pl):
                w(pl, "halo_ld", B0["halo_ld"])
                src = sb3(kh, c.TILE)[:, :, c.TILE - c.W:c.TILE]
                inc(pl.dma_start(sb3(s_khalo_h, c.W), src), "halo_st", 16)
                src = sb3(kl, c.TILE)[:, :, c.TILE - c.W:c.TILE]
                inc(pl.dma_start(sb3(s_khalo_l, c.W), src), "halo_st", 16)
                inc(pl.dma_start(s_vhalo_h, vh[:, (c.NSUB - 1) * c.D: c.NSUB * c.D]), "halo_st", 16)
                inc(pl.dma_start(s_vhalo_l, vl[:, (c.NSUB - 1) * c.D: c.NSUB * c.D]), "halo_st", 16)

        B0 = dict(CNT)
        with nc.Block() as blk:
            @blk.sync
            def _(sp):
                if t == 0:
                    for n in WST_FULL:
                        w(sp, n, WST_FULL[n])
                for s in range(c.NSLAB_O):
                    if s >= 2:
                        w(sp, "womm", B0["womm"] + (s - 1) * c.SJ)
                    wh_, wl_ = ws_ring[s % 2]
                    incp(sp.dma_start(wh_, s_woh[s]), "wld", s)
                    incp(sp.dma_start(wl_, s_wol[s]), "wld", s)

            @blk.tensor
            def _(pe):
                for s in range(c.NSLAB_O):
                    wp(pe, B0, "wld", s, 32)
                    wh_, wl_ = ws_ring[s % 2]
                    for j in range(c.SJ):
                        oc = s * c.SJ + j
                        if oc >= 2:
                            w(pe, "wocp", B0["wocp"] + oc - 1)
                        pb = p_mm[oc % 2]
                        nmm = 0
                        for dc in range(c.DC):
                            lt_h = wh_[:, dc * c.SLAB + j * 128: dc * c.SLAB + (j + 1) * 128]
                            lt_l = wl_[:, dc * c.SLAB + j * 128: dc * c.SLAB + (j + 1) * 128]
                            for lt, rh in ((lt_h, oh), (lt_h, ol), (lt_l, oh)):
                                ins = pe.matmul(pb, lt, rh[:, TS(dc, c.TILE)],
                                                start=(nmm == 0), stop=(nmm == 3 * c.DC - 1))
                                nmm += 1
                        inc(ins, "womm", 1)

            @blk.scalar
            def _(act):
                for oc in range(c.DC):
                    w(act, "womm", B0["womm"] + oc + 1)
                    if oc >= 2:
                        wp(act, B0, "resst", oc - 2, 16)
                    else:
                        nm = f"resst{oc % 2}"
                        if B0[nm]:
                            w(act, nm, B0[nm])
                    ins = act.copy(reschunk[oc % 2], p_mm[oc % 2])
                    inc(ins, "wocp", 1)

            @blk.gpsimd
            def _(pl):
                w(pl, "halo_st", B0["halo_st"])
                GL0.update({n: CNT[n] for n in ("gld0", "gld1", "uld0", "uld1",
                                                "dld0", "dld1", "dld20", "dld21")})
                for j in (0, 1):
                    g_, u_ = gu_ring[j % 2]
                    srcg = gate_w_t[:, TS(j, 128)].rearrange("(c p) n -> p c n", p=128)
                    srcu = up_w_t[:, TS(j, 128)].rearrange("(c p) n -> p c n", p=128)
                    incp(pl.dma_start(sb3(g_, 128), srcg), "gld", j)
                    incp(pl.dma_start(sb3(u_, 128), srcu), "uld", j)
                fh = c.FCH // 2
                for ch in (0, 1):
                    for a, nm in ((0, "dld"), (1, "dld2")):
                        srcd = down_w_t[a * c.FF4:(a + 1) * c.FF4,
                                        TS(ch, 128)].rearrange("(c p) n -> p c n", p=128)
                        dst = dn_ring[ch % 2][:, a * fh * 128:(a + 1) * fh * 128]
                        incp(pl.dma_start(sb3(dst, 128), srcd), nm, ch)
                wp_all(pl, B0, "resld")
                for oc in range(c.DC):
                    w(pl, "wocp", B0["wocp"] + oc + 1)
                    incp(pl.dma_start(s_res[oc], reschunk[oc % 2]), "resst", oc)

        B0 = dict(CNT)
        with nc.Block() as blk:
            if t + 1 < c.NT:
                nxt = c.W + (t + 1) * c.TILE
                LF = ln_ops(B0, c.NSUB, c.TILE, nxt, nxt, xnh, xnl,
                            overwrite_gate=("umm", B0["umm"] + c.FC), defer_xn=True)
            else:
                LF = None

            @blk.sync
            def _(sp):
                ln_drain(LF, "dma", sp)

            @blk.tensor
            def _(pe):
                for hf in range(2):
                    for icl in range(c.FCH):
                        j = hf * c.FCH + icl
                        wp(pe, GL0, "gld", j, 16)
                        if hf == 1 and icl == 0:
                            w(pe, "acc", B0["acc"] + c.DC)
                        if j >= 2:
                            w(pe, "hev", B0["hev"] + j - 1)
                        g_, u_ = gu_ring[j % 2]
                        pg = p_mm[(j % 2) * 2]
                        pu = p_mm[(j % 2) * 2 + 1]
                        nmm = 0
                        for dc in range(c.DC):
                            lt = g_[:, TS(dc, 128)]
                            for rh in (xnt_h, xnt_l):
                                ins = pe.matmul(pg, lt, rh[:, TS(dc, c.TILE)],
                                                start=(nmm == 0), stop=(nmm == 2 * c.DC - 1))
                                nmm += 1
                        inc(ins, "gmm", 1)
                        wp(pe, GL0, "uld", j, 16)
                        nmm = 0
                        for dc in range(c.DC):
                            lt = u_[:, TS(dc, 128)]
                            for rh in (xnt_h, xnt_l):
                                ins = pe.matmul(pu, lt, rh[:, TS(dc, c.TILE)],
                                                start=(nmm == 0), stop=(nmm == 2 * c.DC - 1))
                                nmm += 1
                        inc(ins, "umm", 1)
                        if icl >= 2:
                            k = icl - 2
                            if k == 0 and hf == 1:
                                w(pe, "shev", B0["shev"] + 1)
                            pe.matmul(p_sumh, sb_ones_col, hh[:, TS(k, c.TILE)],
                                      start=(k == 0), stop=False)
                            pe.matmul(p_sumh, sb_ones_col, hl[:, TS(k, c.TILE)],
                                      start=False, stop=False)
                        if j in (11, 15):
                            ln_pop(LF, "pe", pe)
                    if hf == 1:
                        ln_drain(LF, "pe", pe)
                    w(pe, "hev", B0["hev"] + (hf + 1) * c.FCH)
                    for k in (c.FCH - 2, c.FCH - 1):
                        pe.matmul(p_sumh, sb_ones_col, hh[:, TS(k, c.TILE)],
                                  start=False, stop=False)
                        ins = pe.matmul(p_sumh, sb_ones_col, hl[:, TS(k, c.TILE)],
                                        start=False, stop=(k == c.FCH - 1))
                    inc(ins, "shmm", 1)
                    w(pe, "shev", B0["shev"] + hf + 1)
                    for ch in range(c.DC):
                        j = hf * c.DC + ch
                        wp(pe, GL0, "dld", j, 16)
                        if hf == 0:
                            if j >= 2:
                                w(pe, "acc", B0["acc"] + j - 1)
                        else:
                            if ch >= 2:
                                w(pe, "fin", B0["fin"] + ch - 1)
                            else:
                                w(pe, "acc", B0["acc"] + c.DC + ch - 1)
                        pb = p_mm[j % 2]
                        nmm = 0
                        for icl in range(c.FCH):
                            if icl == c.FCH // 2:
                                wp(pe, GL0, "dld2", j, 16)
                            lt = dn_ring[j % 2][:, TS(icl, 128)]
                            for rh in (hh, hl):
                                ins = pe.matmul(pb, lt, rh[:, TS(icl, c.TILE)],
                                                start=(nmm == 0), stop=False)
                                nmm += 1
                        pe.matmul(pb, sb_neg128, sumh_h[:, 0:c.TILE], start=False, stop=False)
                        ins = pe.matmul(pb, sb_neg128, sumh_l[:, 0:c.TILE], start=False, stop=True)
                        inc(ins, "dnmm", 1)

            @blk.scalar
            def _(act):
                w(act, "halo_st", B0["halo_st"])
                for hf in range(2):
                    for icl in range(c.FCH):
                        j = hf * c.FCH + icl
                        w(act, "gmm", B0["gmm"] + j + 1)
                        if j >= 2:
                            w(act, "hf32s", B0["hf32s"] + j - 1)
                        if hf == 1 and icl == 0:
                            w(act, "dnmm", B0["dnmm"] + c.DC)
                        pg = p_mm[(j % 2) * 2]
                        i2 = hf * c.FCH + icl
                        ins = act.activation(sgbuf[j % 2], pg, AF.Silu,
                                             scale=sb_gs[:, i2:i2 + 1])
                        inc(ins, "sg", 1)
                        w(act, "hf32s", B0["hf32s"] + j + 1)
                        ins = act.copy(hh[:, TS(icl, c.TILE)], hf32[j % 2])
                        inc(ins, "hhi", 1)
                        ln_pop(LF, "act", act)
                ln_drain(LF, "act", act)
                for ch in range(c.DC):
                    w(act, "acc", B0["acc"] + ch + 1)
                    if ch >= 2:
                        w(act, "obuf", B0["obuf"] + ch - 1)
                    ins = act.activation(t0buf[ch % 2], accum[:, TS(ch, c.TILE)], AF.Copy,
                                         scale=sb_ds[:, ch:ch + 1])
                    inc(ins, "t0s", 1)

            @blk.vector
            def _(dve):
                w(dve, "halo_st", B0["halo_st"])
                for hf in range(2):
                    for icl in range(c.FCH):
                        j = hf * c.FCH + icl
                        w(dve, "sg", B0["sg"] + j + 1)
                        w(dve, "umm", B0["umm"] + j + 1)
                        if j >= 2:
                            w(dve, "hhi", B0["hhi"] + j - 1)
                        if hf == 1 and icl == 0:
                            w(dve, "dnmm", B0["dnmm"] + c.DC)
                        pu = p_mm[(j % 2) * 2 + 1]
                        i2 = hf * c.FCH + icl
                        ins = dve.scalar_tensor_tensor(hf32[j % 2], pu, sb_us[:, i2:i2 + 1],
                                                       sgbuf[j % 2], OP.mult, OP.mult)
                        inc(ins, "hf32s", 1)
                        w(dve, "hhi", B0["hhi"] + j + 1)
                        ins = dve.tensor_sub(hl[:, TS(icl, c.TILE)], hf32[j % 2],
                                             hh[:, TS(icl, c.TILE)])
                        inc(ins, "hev", 1)
                        ln_pop(LF, "dve", dve)
                    if hf == 1:
                        ln_drain(LF, "dve", dve)
                    w(dve, "shmm", B0["shmm"] + hf + 1)
                    chain(dve, dve.tensor_copy(sumh_h[:, 0:c.TILE], p_sumh))
                    ins = dve.tensor_sub(sumh_l[:, 0:c.TILE], p_sumh, sumh_h[:, 0:c.TILE])
                    inc(ins, "shev", 1)
                    for ch in range(c.DC):
                        j = hf * c.DC + ch
                        w(dve, "dnmm", B0["dnmm"] + j + 1)
                        pb = p_mm[j % 2]
                        if hf == 0:
                            ins = dve.tensor_copy(accum[:, TS(ch, c.TILE)], pb)
                            inc(ins, "acc", 1)
                        else:
                            w(dve, "t0s", B0["t0s"] + ch + 1)
                            ins = dve.scalar_tensor_tensor(t0buf[ch % 2], pb,
                                                           sb_ds[:, ch:ch + 1],
                                                           t0buf[ch % 2],
                                                           OP.mult, OP.add)
                            inc(ins, "fin", 1)
                            w(dve, "fin", CNT["fin"])
                            wp(dve, B0, "resld", ch, 16)
                            chain(dve, dve.tensor_add(t0buf[ch % 2], t0buf[ch % 2],
                                                      resbuf[ch % 2]))
                            wp(dve, B0, "xld", ch, 16)
                            ins = dve.tensor_add(xbuf[ch % 2], t0buf[ch % 2], xbuf[ch % 2])
                            inc(ins, "obuf", 1)
                            ln_pop(LF, "dve2", dve)
                            ln_pop(LF, "dve2", dve)
                ln_drain(LF, "dve2", dve)

            @blk.gpsimd
            def _(pl):
                w(pl, "halo_st", B0["halo_st"])
                for hf in range(2):
                    for icl in range(c.FCH):
                        j = hf * c.FCH + icl
                        if j < 2:
                            continue
                        w(pl, "gmm", B0["gmm"] + j - 1)
                        g_, u_ = gu_ring[j % 2]
                        ic = hf * c.FCH + icl
                        srcg = gate_w_t[:, TS(ic, 128)].rearrange("(c p) n -> p c n", p=128)
                        srcu = up_w_t[:, TS(ic, 128)].rearrange("(c p) n -> p c n", p=128)
                        incp(pl.dma_start(sb3(g_, 128), srcg), "gld", j)
                        w(pl, "umm", B0["umm"] + j - 1)
                        incp(pl.dma_start(sb3(u_, 128), srcu), "uld", j)
                    if hf == 0:
                        for ch in range(2, c.DC):
                            w(pl, "dnmm", B0["dnmm"] + ch - 1)
                            fh = c.FCH // 2
                            for a, nm in ((0, "dld"), (1, "dld2")):
                                srcd = down_w_t[a * c.FF4: (a + 1) * c.FF4,
                                                TS(ch, 128)].rearrange(
                                    "(c p) n -> p c n", p=128)
                                dst = dn_ring[ch % 2][:, a * fh * 128:(a + 1) * fh * 128]
                                incp(pl.dma_start(sb3(dst, 128), srcd), nm, ch)
                wp_all(pl, B0, "resst")
                for ch in range(c.DC):
                    j = c.DC + ch
                    w(pl, "dnmm", B0["dnmm"] + j - 1)
                    fh = c.FCH // 2
                    for a, nm in ((0, "dld"), (1, "dld2")):
                        srcd = down_w_t[c.FF2 + a * c.FF4: c.FF2 + (a + 1) * c.FF4,
                                        TS(ch, 128)].rearrange(
                            "(c p) n -> p c n", p=128)
                        dst = dn_ring[j % 2][:, a * fh * 128:(a + 1) * fh * 128]
                        incp(pl.dma_start(sb3(dst, 128), srcd), nm, j)
                    if ch >= 2:
                        w(pl, "obuf", B0["obuf"] + ch - 1)
                        wp(pl, B0, "outst", ch - 2, 16)
                    else:
                        nm = f"outst{ch % 2}"
                        if B0[nm]:
                            w(pl, nm, B0[nm])
                    incp(pl.dma_start(resbuf[ch % 2], s_res[ch]), "resld", ch)
                    src = x_fm[TS(ch, 128), tok0: tok0 + c.TILE]
                    incp(pl.dma_start(xbuf[ch % 2], src), "xld", ch)
                    if ch >= 1:
                        w(pl, "obuf", B0["obuf"] + ch)
                        incp(pl.dma_start(out_fm[TS(ch - 1, 128), TS(t, c.TILE)],
                                          xbuf[(ch - 1) % 2]), "outst", ch - 1)
                w(pl, "obuf", B0["obuf"] + c.DC)
                incp(pl.dma_start(out_fm[TS(c.DC - 1, 128), TS(t, c.TILE)],
                                  xbuf[(c.DC - 1) % 2]), "outst", c.DC - 1)

    with nc.Block() as blk:
        @blk.gpsimd
        def _(pl):
            for name in ["outst0", "outst1", "resst0", "resst1", "halo_st",
                         "wst0", "wst1", "gld0", "gld1", "uld0", "uld1",
                         "dld0", "dld1", "dld20", "dld21",
                         "resld0", "resld1", "xld0", "xld1"]:
                if CNT[name]:
                    pl.wait_ge(S[name], CNT[name])

        @blk.sync
        def _(sp):
            for name in ["wld0", "wld1", "wsth0", "wsth1", "xtm0", "lnx0", "lnx1",
                         "halo_ld", "ld_const"]:
                if CNT[name]:
                    sp.wait_ge(S[name], CNT[name])

        @blk.scalar
        def _(act):
            for name in ["pxtm0", "plnx0", "plnx1", "prowsd", "rowsd"]:
                if CNT[name]:
                    act.wait_ge(S[name], CNT[name])

    return nc


def make_in_maps(cfg, inputs, n_cores=N_CORES):
    c = cfg
    x = np.asarray(inputs["L0_attn_x"], dtype=np.float32)
    B, T, D = x.shape
    n_seq_shards = n_cores // B
    SEQ = T // n_seq_shards
    assert SEQ == c.TOK and D == c.D

    def pack_slabs(wt, nslab):
        arr = wt.reshape(c.DC, 128, nslab, c.SLAB)
        return np.ascontiguousarray(
            arr.transpose(1, 2, 0, 3).reshape(128, nslab * c.DC * c.SLAB))

    w_attn_t = pack_slabs(np.asarray(inputs["L0_attn_w_attn"], np.float32).T,
                          c.NSLAB_A)
    w_o_t = pack_slabs(np.asarray(inputs["L0_attn_w_o"], np.float32).T, c.NSLAB_O)
    gate_w_t = np.ascontiguousarray(np.asarray(inputs["L0_ffn_gate_w"], np.int32).T)
    up_w_t = np.ascontiguousarray(np.asarray(inputs["L0_ffn_up_w"], np.int32).T)
    down_w_t = np.ascontiguousarray(np.asarray(inputs["L0_ffn_down_w"], np.int32).T)
    gate_s = np.ascontiguousarray(np.asarray(inputs["L0_ffn_gate_scale"], np.float32).reshape(c.FC, 128).T)
    up_s = np.ascontiguousarray(np.asarray(inputs["L0_ffn_up_scale"], np.float32).reshape(c.FC, 128).T)
    down_s = np.ascontiguousarray(np.asarray(inputs["L0_ffn_down_scale"], np.float32).reshape(c.DC, 128).T)

    jj = np.arange(c.W)[:, None]
    ii = np.arange(c.W)[None, :]
    m_prev = (jj >= ii).astype(np.float32)
    m_cur = (jj <= ii).astype(np.float32)
    m_zero = np.zeros((c.W, c.W), np.float32)
    ident = np.eye(128, dtype=np.float32)

    in_maps = []
    for core in range(n_cores):
        b = core // n_seq_shards
        sh = core % n_seq_shards
        xt = np.zeros((c.TOKH, c.D), np.float32)
        t0 = sh * SEQ
        if sh > 0:
            xt[:] = x[b, t0 - c.W: t0 + SEQ]
        else:
            xt[c.W:] = x[b, 0:SEQ]
        x_tm_a = np.ascontiguousarray(xt)
        x_fm_a = np.ascontiguousarray(xt.T)
        m_first = m_prev if sh > 0 else m_zero
        import ml_dtypes
        masks = np.ascontiguousarray(
            np.concatenate([m_prev, m_cur, m_first], axis=1).astype(ml_dtypes.bfloat16))
        in_maps.append({
            "x_fm": x_fm_a, "x_tm": x_tm_a,
            "w_attn_t": w_attn_t, "w_o_t": w_o_t,
            "gate_w_t": gate_w_t, "up_w_t": up_w_t, "down_w_t": down_w_t,
            "gate_s": gate_s, "up_s": up_s, "down_s": down_s,
            "masks": masks, "ident": ident,
        })
    return in_maps


def kernel(**inputs):
    cfg = CFG()
    nc = build(cfg)
    in_maps = make_in_maps(cfg, inputs)
    res = run_bass_kernel_spmd(nc, in_maps, core_ids=list(range(N_CORES)))
    x = np.asarray(inputs["L0_attn_x"])
    B, T, D = x.shape
    n_seq_shards = N_CORES // B
    SEQ = T // n_seq_shards
    out = np.empty((B, T, D), np.float32)
    for core in range(N_CORES):
        b = core // n_seq_shards
        sh = core % n_seq_shards
        o = np.asarray(res.results[core]["out_fm"])
        out[b, sh * SEQ:(sh + 1) * SEQ, :] = o.T
    return out


# version 16
# speedup vs baseline: 1.1332x; 1.0078x over previous
import numpy as np
import concourse.bass as bass
import concourse.mybir as mybir
from concourse.bass_utils import run_bass_kernel_spmd

F32 = mybir.dt.float32
BF16 = mybir.dt.bfloat16
I32 = mybir.dt.int32
AF = mybir.ActivationFunctionType
OP = mybir.AluOpType

N_CORES = 8


class CFG:
    def __init__(self, D=2048, FF=8192, TOK=2048, TILE=512, W=128, SLAB=256):
        self.D, self.FF, self.TOK, self.TILE, self.W = D, FF, TOK, TILE, W
        self.SLAB = SLAB
        self.DC = D // 128
        self.FC = FF // 128
        self.FCH = self.FC // 2
        self.FF2 = FF // 2
        self.FF4 = FF // 4
        self.NT = TOK // TILE
        self.NB = TILE // W
        self.NSUB = TILE // 128
        self.TOKH = TOK + W
        self.NSLAB_A = 3 * D // SLAB
        self.NSLAB_O = D // SLAB
        self.SJ = SLAB // 128
        self.NQ = D // SLAB


def TS(i, n):
    return slice(i * n, (i + 1) * n)


def build(cfg: CFG):
    c = cfg
    nc = bass.Bass(target_bir_lowering=False, trn_type="TRN2")

    x_fm = nc.dram_tensor("x_fm", [c.D, c.TOKH], F32, kind="ExternalInput").ap()
    x_tm = nc.dram_tensor("x_tm", [c.TOKH, c.D], F32, kind="ExternalInput").ap()
    w_attn_t = nc.dram_tensor("w_attn_t", [128, c.NSLAB_A * c.DC * c.SLAB], F32,
                              kind="ExternalInput").ap()
    w_o_t = nc.dram_tensor("w_o_t", [128, c.NSLAB_O * c.DC * c.SLAB], F32,
                           kind="ExternalInput").ap()
    gate_w_t = nc.dram_tensor("gate_w_t", [c.D, c.FF], I32, kind="ExternalInput").ap()
    up_w_t = nc.dram_tensor("up_w_t", [c.D, c.FF], I32, kind="ExternalInput").ap()
    down_w_t = nc.dram_tensor("down_w_t", [c.FF, c.D], I32, kind="ExternalInput").ap()
    gate_s_in = nc.dram_tensor("gate_s", [128, c.FC], F32, kind="ExternalInput").ap()
    up_s_in = nc.dram_tensor("up_s", [128, c.FC], F32, kind="ExternalInput").ap()
    down_s_in = nc.dram_tensor("down_s", [128, c.DC], F32, kind="ExternalInput").ap()
    masks_in = nc.dram_tensor("masks", [128, 3 * c.W], BF16, kind="ExternalInput").ap()
    ident_in = nc.dram_tensor("ident", [128, 128], F32, kind="ExternalInput").ap()
    out_fm = nc.dram_tensor("out_fm", [c.D, c.TOK], F32, kind="ExternalOutput").ap()

    s_wah = nc.dram_tensor("s_wah", [c.NSLAB_A, 128, c.DC * c.SLAB], BF16, kind="Internal").ap()
    s_wal = nc.dram_tensor("s_wal", [c.NSLAB_A, 128, c.DC * c.SLAB], BF16, kind="Internal").ap()
    s_woh = nc.dram_tensor("s_woh", [c.NSLAB_O, 128, c.DC * c.SLAB], BF16, kind="Internal").ap()
    s_wol = nc.dram_tensor("s_wol", [c.NSLAB_O, 128, c.DC * c.SLAB], BF16, kind="Internal").ap()
    s_res = nc.dram_tensor("s_res", [c.DC, 128, c.TILE], F32, kind="Internal").ap()
    s_rows = nc.dram_tensor("s_rows", [2 * 4, 128], F32, kind="Internal").ap()
    s_khalo_h = nc.dram_tensor("s_khalo_h", [128, c.DC * c.W], BF16, kind="Internal").ap()
    s_khalo_l = nc.dram_tensor("s_khalo_l", [128, c.DC * c.W], BF16, kind="Internal").ap()
    s_vhalo_h = nc.dram_tensor("s_vhalo_h", [128, c.D], BF16, kind="Internal").ap()
    s_vhalo_l = nc.dram_tensor("s_vhalo_l", [128, c.D], BF16, kind="Internal").ap()

    cur = [nc.sbuf_base + 64]
    ded = {}

    def salloc(name, shape, dtype, offset=None):
        if offset is None:
            off = (cur[0] + 63) // 64 * 64
            cur[0] = off + shape[-1] * mybir.dt.size(dtype)
        else:
            off = offset
        t = nc.alloc_sbuf_tensor_at(name, shape, dtype, offset=off)
        ded[name] = off
        return t.ap()

    sb_masks = salloc("sb_masks", [128, 3 * c.W], BF16)
    sb_gs = salloc("sb_gs", [128, c.FC], F32)
    sb_us = salloc("sb_us", [128, c.FC], F32)
    sb_ds = salloc("sb_ds", [128, c.DC], F32)
    sb_ident = salloc("sb_ident", [128, 128], F32)
    sb_ones = salloc("sb_ones", [2, 128], BF16)
    sb_ones_col = salloc("sb_ones_col", [128, 1], BF16)
    sb_neg128 = salloc("sb_neg128", [1, 128], BF16)
    sb_stat = salloc("sb_stat", [128, 8 * c.NSUB], F32)
    sb_pk = salloc("sb_pk", [128, 2 * c.NSUB], F32)
    sb_t8 = salloc("sb_t8", [2 * c.NSUB, 128], F32)
    sb_mu_bh = salloc("sb_mu_bh", [1, c.TILE], BF16)
    sb_mu_bl = salloc("sb_mu_bl", [1, c.TILE], BF16)
    sb_r_bh = salloc("sb_r_bh", [1, c.TILE], BF16)
    sb_r_bl = salloc("sb_r_bl", [1, c.TILE], BF16)
    sb_ah = salloc("sb_ah", [128, 2 * c.W], BF16)
    sb_al = salloc("sb_al", [128, 2 * c.W], BF16)
    sb_ah2 = salloc("sb_ah2", [128, 2 * c.W], BF16)
    sb_al2 = salloc("sb_al2", [128, 2 * c.W], BF16)
    sb_rbuf = salloc("sb_rbuf", [128, 2 * c.W], F32)
    sb_ambuf = salloc("sb_ambuf", [128, 2 * c.W], F32)
    sumh_h = salloc("sumh_h", [1, c.TILE], BF16, offset=ded["sb_rbuf"])
    sumh_l = salloc("sumh_l", [1, c.TILE], BF16, offset=ded["sb_ambuf"])
    reschunk = [salloc(f"reschunk{i}", [128, c.TILE], F32) for i in range(2)]

    small_end = (cur[0] + 63) // 64 * 64

    xn_half = c.DC * c.TILE * 2
    h_half = c.FCH * c.TILE * 2
    XN_B = small_end
    A_B = XN_B + 2 * xn_half
    prep_end = 2 * xn_half + 2 * (2 * c.DC * c.SLAB * 2)
    halo_base = max(4 * xn_half, prep_end)
    A_SZ = max(6 * xn_half,
               2 * h_half + 2 * (2 * c.DC * 256) + 2 * c.FCH * 256,
               halo_base + 2 * c.DC * c.W * 2 + 2 * c.D * 2,
               2 * (2 * c.DC * c.SLAB * 2) + 2 * c.FCH * 256)
    B_B = A_B + A_SZ
    B_SZ = max(c.DC * c.TILE * 4, 10 * c.TILE * 4 + c.D * 4 + 2 * c.NSUB * 512,
               2 * 4 * max(c.DC * c.SLAB, c.FCH * 128, c.DC * 128))
    C_B = B_B + B_SZ
    C_SZ = max(c.DC * c.TILE * 4,
               2 * (2 * c.DC * c.SLAB * 2),
               2 * c.DC * c.W * 2 + 2 * c.D * 2)
    TOP = C_B + C_SZ
    assert TOP <= nc.sbuf_top - 64, (TOP, nc.sbuf_top)

    xnh = salloc("xnh", [128, c.DC * c.TILE], BF16, offset=XN_B)
    xnl = salloc("xnl", [128, c.DC * c.TILE], BF16, offset=XN_B + xn_half)

    qh = salloc("qh", [128, c.DC * c.TILE], BF16, offset=A_B + 0 * xn_half)
    ql = salloc("ql", [128, c.DC * c.TILE], BF16, offset=A_B + 1 * xn_half)
    kh = salloc("kh", [128, c.DC * c.TILE], BF16, offset=A_B + 2 * xn_half)
    kl = salloc("kl", [128, c.DC * c.TILE], BF16, offset=A_B + 3 * xn_half)
    vh = salloc("vh", [128, c.NSUB * c.D], BF16, offset=A_B + 4 * xn_half)
    vl = salloc("vl", [128, c.NSUB * c.D], BF16, offset=A_B + 5 * xn_half)
    hh = salloc("hh", [128, c.FCH * c.TILE], BF16, offset=A_B)
    hl = salloc("hl", [128, c.FCH * c.TILE], BF16, offset=A_B + h_half)
    off = A_B + 2 * h_half
    gu_ring = []
    for i in range(2):
        g = salloc(f"gbuf{i}", [128, c.DC * 128], BF16, offset=off); off += c.DC * 256
        u = salloc(f"ubuf{i}", [128, c.DC * 128], BF16, offset=off); off += c.DC * 256
        gu_ring.append((g, u))
    dn_ring = []
    for i in range(2):
        d = salloc(f"dbuf{i}", [128, c.FCH * 128], BF16, offset=off); off += c.FCH * 256
        dn_ring.append(d)
    assert off <= A_B + A_SZ
    off = A_B + 2 * xn_half
    prep_pair = []
    for i in range(2):
        ph = salloc(f"preph{i}", [128, c.DC * c.SLAB], BF16, offset=off); off += c.DC * c.SLAB * 2
        pl_ = salloc(f"prepl{i}", [128, c.DC * c.SLAB], BF16, offset=off); off += c.DC * c.SLAB * 2
        prep_pair.append((ph, pl_))
    assert off <= A_B + A_SZ
    hoff = A_B + halo_base
    khalo_pr_h = salloc("khalo_pr_h", [128, c.DC * c.W], BF16, offset=hoff); hoff += c.DC * c.W * 2
    khalo_pr_l = salloc("khalo_pr_l", [128, c.DC * c.W], BF16, offset=hoff); hoff += c.DC * c.W * 2
    vhalo_pr_h = salloc("vhalo_pr_h", [128, c.D], BF16, offset=hoff); hoff += c.D * 2
    vhalo_pr_l = salloc("vhalo_pr_l", [128, c.D], BF16, offset=hoff)

    oh = salloc("oh", [128, c.DC * c.TILE], BF16, offset=B_B)
    ol = salloc("ol", [128, c.DC * c.TILE], BF16, offset=B_B + xn_half)
    off = B_B
    prep_in = []
    PREP_N = c.DC * c.SLAB
    for i in range(2):
        t_ = salloc(f"prepin{i}", [128, PREP_N], F32, offset=off)
        off += PREP_N * 4
        prep_in.append(t_)
    assert off <= B_B + B_SZ
    off = B_B
    sgbuf, hf32, t0buf, resbuf, xbuf = [], [], [], [], []
    for i in range(2):
        sgbuf.append(salloc(f"sgbuf{i}", [128, c.TILE], F32, offset=off)); off += c.TILE * 4
        hf32.append(salloc(f"hf32_{i}", [128, c.TILE], F32, offset=off)); off += c.TILE * 4
        t0buf.append(salloc(f"t0buf{i}", [128, c.TILE], F32, offset=off)); off += c.TILE * 4
        resbuf.append(salloc(f"resbuf{i}", [128, c.TILE], F32, offset=off)); off += c.TILE * 4
        xbuf.append(salloc(f"xbuf{i}", [128, c.TILE], F32, offset=off)); off += c.TILE * 4
    statin_b = salloc("statin_b", [128, c.D], F32, offset=off); off += c.D * 4
    xring = [statin_b[:, TS(i, c.TILE)] for i in range(2)]
    sb_rowsg = salloc("sb_rowsg", [1, 2 * c.NSUB * 128], F32, offset=off)
    off += 2 * c.NSUB * 512
    assert off <= B_B + B_SZ, (off, B_B + B_SZ)

    accum = salloc("accum", [128, c.DC * c.TILE], F32, offset=C_B)
    off = C_B
    ws_ring = []
    for i in range(2):
        wh_ = salloc(f"wsh{i}", [128, c.DC * c.SLAB], BF16, offset=off); off += c.DC * c.SLAB * 2
        wl_ = salloc(f"wsl{i}", [128, c.DC * c.SLAB], BF16, offset=off); off += c.DC * c.SLAB * 2
        ws_ring.append((wh_, wl_))
    assert off <= C_B + C_SZ
    off = C_B
    khalo_sb_h = salloc("khalo_sb_h", [128, c.DC * c.W], BF16, offset=off); off += c.DC * c.W * 2
    khalo_sb_l = salloc("khalo_sb_l", [128, c.DC * c.W], BF16, offset=off); off += c.DC * c.W * 2
    vhalo_sb_h = salloc("vhalo_sb_h", [128, c.D], BF16, offset=off); off += c.D * 2
    vhalo_sb_l = salloc("vhalo_sb_l", [128, c.D], BF16, offset=off); off += c.D * 2
    assert off <= C_B + C_SZ

    P = [nc.alloc_psum_tensor(f"pb{i}", [128, 512], F32).ap() for i in range(8)]
    p_mm = [P[0][:, 0:c.TILE], P[1][:, 0:c.TILE], P[2][:, 0:c.TILE], P[3][:, 0:c.TILE]]
    p_at2 = [P[4][:, 0:256], P[2][:, 0:256]]
    p_o = [P[5][:, 0:128], P[6][:, 0:128]]
    p_bc_r = P[5]
    p_bc_mu = P[6]
    p_tr = P[7]
    p_sumh = P[4][0:1, 0:c.TILE]

    from contextlib import ExitStack
    stack = ExitStack()
    nc._kernel_stack = stack
    S, CNT = {}, {}
    for n in (["ld_const", "prep_done", "shmm", "shev",
               "xfm_ld", "stat", "vpe", "sqrt", "pack", "tmm", "rows",
               "xnf", "xnh_c",
               "qkmm", "qkhi", "qkev", "vmm", "vhi", "vev",
               "halo_ld", "halo_st", "atmm", "at_relu", "at_ev", "omm", "ohi", "oev",
               "womm", "wocp",
               "gmm", "umm", "sg", "hf32s", "hhi", "hev",
               "dnmm", "acc", "fin", "t0s", "obuf", "dvch", "scch", "t8c", "rowsd"]
              + ["xnl_c", "pxtm0", "prowsd"]
              + [f"{p}{i}" for p in ("wld", "wst", "wsth", "xtm", "gld", "uld",
                                     "lnx", "plnx",
                                     "dld", "dld2", "resld", "xld", "resst", "outst")
                 for i in range(2)]):
        S[n] = stack.enter_context(nc.semaphore(n))
        CNT[n] = 0

    def inc(ins, name, n=1):
        ins.then_inc(S[name], n)
        CNT[name] += n

    def w(eng, name, val):
        eng.wait_ge(S[name], val)

    def incp(ins, prefix, idx, n=16):
        inc(ins, f"{prefix}{idx % 2}", n)

    def wp(eng, B0, prefix, idx, per):
        nm = f"{prefix}{idx % 2}"
        eng.wait_ge(S[nm], B0[nm] + per * (idx // 2 + 1))

    def wp_all(eng, B0, prefix):
        for i in range(2):
            if CNT[f"{prefix}{i}"]:
                eng.wait_ge(S[f"{prefix}{i}"], CNT[f"{prefix}{i}"])

    def chain(eng, ins, name="dvch"):
        inc(ins, name, 1)
        eng.wait_ge(S[name], CNT[name])

    def hbm_cols(t, c0, c1):
        return t[:, c0:c1].rearrange("(c p) n -> p c n", p=128)

    def sb3(t, width):
        return t.rearrange("p (c n) -> p c n", n=width)

    D_inv = 1.0 / c.D
    EPS = 1e-5

    def ln_vec_stats(dve, s):
        ns = c.NSUB
        s1 = sb_stat[:, s:s + 1]
        s2 = sb_stat[:, ns + s:ns + s + 1]
        mu = sb_stat[:, 2 * ns + s:2 * ns + s + 1]
        e2 = sb_stat[:, 3 * ns + s:3 * ns + s + 1]
        msq = sb_stat[:, 4 * ns + s:4 * ns + s + 1]
        vpe = sb_stat[:, 5 * ns + s:5 * ns + s + 1]
        dve.tensor_scalar_mul(mu, s1, D_inv)
        chain(dve, dve.tensor_scalar_mul(e2, s2, D_inv))
        chain(dve, dve.scalar_tensor_tensor(msq, mu, -1.0, mu, OP.mult, OP.mult))
        ins = dve.scalar_tensor_tensor(vpe, e2, EPS, msq, OP.add, OP.add)
        inc(ins, "vpe", 1)

    def ln_vec_newton(dve, s):
        ns = c.NSUB
        mu = sb_stat[:, 2 * ns + s:2 * ns + s + 1]
        vpe = sb_stat[:, 5 * ns + s:5 * ns + s + 1]
        r0 = sb_stat[:, 6 * ns + s:6 * ns + s + 1]
        tn = sb_stat[:, 7 * ns + s:7 * ns + s + 1]
        chain(dve, dve.reciprocal(r0, tn))
        for _ in range(2):
            chain(dve, dve.tensor_mul(tn, r0, r0))
            chain(dve, dve.tensor_mul(tn, tn, vpe))
            chain(dve, dve.tensor_scalar(tn, tn, -0.5, 1.5, OP.mult, OP.add))
            chain(dve, dve.tensor_mul(r0, r0, tn))
        chain(dve, dve.tensor_mul(sb_pk[:, s:s + 1], mu, r0))
        ins = dve.tensor_copy(sb_pk[:, ns + s:ns + s + 1], r0)
        inc(ins, "pack", 1)

    def ln_ops(B0, n_sub, wid, xrow0, xcol0, xd_h, xd_l, overwrite_gate=None,
               defer_xn=False, dsem=("xtm0", "lnx", "rowsd")):
        L = {"dma": [], "act": [], "dve": [], "pe": [], "pl": []}

        def sp_xtm(s):
            def f(sp):
                if s >= 1:
                    w(sp, "stat", B0["stat"] + 2 * s)
                ins = sp.dma_start(statin_b, x_tm[xrow0 + s * 128: xrow0 + (s + 1) * 128, :])
                inc(ins, dsem[0], 16)
            return f

        def sp_x(dc):
            def f(sp):
                if dc < 2:
                    w(sp, "stat", B0["stat"] + 2 * n_sub)
                else:
                    w(sp, "xnl_c", B0["xnl_c"] + dc - 1)
                ins = sp.dma_start(xring[dc % 2][:, 0:wid],
                                   x_fm[TS(dc, 128), xcol0: xcol0 + wid])
                incp(ins, dsem[1], dc)
            return f

        def sp_rowsd(sp):
            w(sp, "t8c", B0["t8c"] + 1)
            ins = sp.dma_start(s_rows[0:2 * c.NSUB, :], sb_t8)
            inc(ins, dsem[2], 16)
            w(sp, dsem[2], B0[dsem[2]] + 16)
            dst_ = sb_rowsg.rearrange("p (a n) -> p a n", n=128)
            src_ = s_rows[0:2 * c.NSUB, :].rearrange("a n -> a () n") \
                if False else s_rows[0:2 * c.NSUB, :]
            ins = sp.dma_start(dst_, src_)
            inc(ins, dsem[2], 16)

        L["dma"] = ([sp_xtm(s) for s in range(n_sub)]
                    + [sp_x(0), sp_x(1), sp_rowsd]
                    + [sp_x(dc) for dc in range(2, c.DC)])

        def act_stats(s):
            def f(act):
                w(act, dsem[0], B0[dsem[0]] + 16 * (s + 1))
                if s >= 1:
                    w(act, "stat", B0["stat"] + 2 * s)
                ins = act.activation(statin_b, statin_b, AF.Copy,
                                     accum_out=sb_stat[:, s:s + 1])
                inc(ins, "stat", 1)
                w(act, "stat", B0["stat"] + 2 * s + 1)
                ins = act.activation(statin_b, statin_b, AF.Square,
                                     accum_out=sb_stat[:, c.NSUB + s:c.NSUB + s + 1])
                inc(ins, "stat", 1)
            return f

        def act_sqrt(s):
            def f(act):
                w(act, "vpe", B0["vpe"] + s + 1)
                ins = act.activation(sb_stat[:, 7 * c.NSUB + s:7 * c.NSUB + s + 1],
                                     sb_stat[:, 5 * c.NSUB + s:5 * c.NSUB + s + 1], AF.Sqrt)
                inc(ins, "sqrt", 1)
            return f

        L["act"] = ([None] + [act_stats(s) for s in range(n_sub)]
                    + [act_sqrt(s) for s in range(n_sub)])
        L["act2"] = []

        def dve_vstats(s):
            def f(dve):
                w(dve, "stat", B0["stat"] + 2 * (s + 1))
                ln_vec_stats(dve, s)
            return f

        def dve_newton(s):
            def f(dve):
                w(dve, "sqrt", B0["sqrt"] + s + 1)
                ln_vec_newton(dve, s)
            return f

        def dve_t8c(dve):
            w(dve, "tmm", B0["tmm"] + 1)
            ins = dve.tensor_copy(sb_t8, p_tr[0:2 * c.NSUB, 0:128])
            inc(ins, "t8c", 1)

        def dve_rowpair(dve):
            w(dve, dsem[2], B0[dsem[2]] + 32)
            mur = sb_rowsg[:, 0:wid]
            rr = sb_rowsg[:, c.NSUB * 128: c.NSUB * 128 + wid]
            dve.tensor_copy(sb_mu_bh[:, 0:wid], mur)
            chain(dve, dve.tensor_copy(sb_r_bh[:, 0:wid], rr))
            dve.tensor_sub(sb_mu_bl[:, 0:wid], mur, sb_mu_bh[:, 0:wid])
            ins = dve.tensor_sub(sb_r_bl[:, 0:wid], rr, sb_r_bh[:, 0:wid])
            inc(ins, "rows", 1)

        def dve_xn(dc):
            def f(dve):
                if dc == 0:
                    w(dve, "tmm", B0["tmm"] + 3)
                    if overwrite_gate is not None:
                        w(dve, overwrite_gate[0], overwrite_gate[1])
                wp(dve, B0, dsem[1], dc, 16)
                xc = xring[dc % 2][:, 0:wid]
                chain(dve, dve.tensor_mul(xc, xc, p_bc_r[:, 0:wid]))
                chain(dve, dve.tensor_sub(xc, xc, p_bc_mu[:, 0:wid]))
                chain(dve, dve.tensor_copy(xd_h[:, dc * c.TILE: dc * c.TILE + wid], xc))
                ins = dve.tensor_sub(xd_l[:, dc * c.TILE: dc * c.TILE + wid], xc,
                                     xd_h[:, dc * c.TILE: dc * c.TILE + wid])
                inc(ins, "xnl_c", 1)
            return f

        tail = [dve_xn(dc) for dc in range(c.DC)]
        head = [None] * 2 + [dve_vstats(s) for s in range(n_sub)] \
            + [dve_newton(s) for s in range(n_sub)]
        head += [None] * (12 - len(head)) + [dve_t8c, None, dve_rowpair]
        if defer_xn:
            L["dve"] = head
            L["dve2"] = tail
        else:
            L["dve"] = head + tail
            L["dve2"] = []

        def pe_tr(pe):
            w(pe, "pack", B0["pack"] + n_sub)
            ins = pe.transpose(p_tr[0:2 * c.NSUB, 0:128], sb_pk, sb_ident)
            inc(ins, "tmm", 1)

        def pe_bc(pe):
            w(pe, "rows", B0["rows"] + 1)
            pe.matmul(p_bc_mu[:, 0:wid], sb_ones[0:1, :], sb_mu_bh[:, 0:wid],
                      start=True, stop=False)
            ins = pe.matmul(p_bc_mu[:, 0:wid], sb_ones[0:1, :], sb_mu_bl[:, 0:wid],
                            start=False, stop=True)
            inc(ins, "tmm", 1)
            pe.matmul(p_bc_r[:, 0:wid], sb_ones[0:1, :], sb_r_bh[:, 0:wid],
                      start=True, stop=False)
            ins = pe.matmul(p_bc_r[:, 0:wid], sb_ones[0:1, :], sb_r_bl[:, 0:wid],
                            start=False, stop=True)
            inc(ins, "tmm", 1)

        L["pe"] = [pe_tr, pe_bc]
        L["pl"] = []
        return L

    def ln_pop(L, eng_name, eng):
        if L is not None and L[eng_name]:
            f = L[eng_name].pop(0)
            if f is not None:
                f(eng)

    def ln_drain(L, eng_name, eng):
        if L is None:
            return
        while L[eng_name]:
            f = L[eng_name].pop(0)
            if f is not None:
                f(eng)

    B0 = dict(CNT)
    with nc.Block() as blk:
        @blk.sync
        def _(sp):
            for dst, src in ((sb_masks, masks_in), (sb_gs, gate_s_in), (sb_us, up_s_in),
                             (sb_ds, down_s_in), (sb_ident, ident_in)):
                inc(sp.dma_start(dst, src), "ld_const", 16)
            sp.wait_ge(S["ld_const"], CNT["ld_const"])

        @blk.vector
        def _(dve):
            dve.memset(sb_ones, 1.0)
            dve.memset(sb_ones_col, 1.0)
            dve.memset(sb_neg128, -128.0)
            dve.memset(sb_pk, 0.0)
            dve.memset(sb_stat, 0.0)

    NSL = c.NSLAB_A + c.NSLAB_O
    B0 = dict(CNT)
    with nc.Block() as blk:
        HSL = c.DC // 2 * c.SLAB

        def prep_src(s, h):
            wt = w_attn_t if s < c.NSLAB_A else w_o_t
            s0 = s if s < c.NSLAB_A else s - c.NSLAB_A
            base = s0 * c.DC * c.SLAB + h * HSL
            return wt[:, base: base + HSL]

        def hi_dst(s):
            return s_wah[s] if s < c.NSLAB_A else s_woh[s - c.NSLAB_A]

        def hi_store(eng, s, h):
            phi = prep_pair[s % 2][0]
            dh = hi_dst(s)
            incp(eng.dma_start(dh[:, h * HSL:(h + 1) * HSL],
                               phi[:, h * HSL:(h + 1) * HSL]), "wsth", s)

        @blk.sync
        def _(sp):
            for s in range(NSL):
                if s >= 2:
                    w(sp, "prep_done", B0["prep_done"] + s - 1)
                dst = prep_in[s % 2][:, 0:HSL]
                incp(sp.dma_start(dst, prep_src(s, 0)), "wld", s)
                if s >= 2:
                    hi_store(sp, s - 2, 0)
            w(sp, "prep_done", B0["prep_done"] + NSL)
            for s in (NSL - 2, NSL - 1):
                hi_store(sp, s, 0)

        @blk.scalar
        def _(act):
            for s in range(NSL):
                if s >= 2:
                    w(act, "prep_done", B0["prep_done"] + s - 1)
                dst = prep_in[s % 2][:, HSL:2 * HSL]
                incp(act.dma_start(dst, prep_src(s, 1)), "wld", s)
                if s >= 2:
                    hi_store(act, s - 2, 1)
            w(act, "prep_done", B0["prep_done"] + NSL)
            for s in (NSL - 2, NSL - 1):
                hi_store(act, s, 1)

        @blk.vector
        def _(dve):
            for s in range(NSL):
                wp(dve, B0, "wld", s, 32)
                if s >= 2:
                    wp(dve, B0, "wst", s - 2, 16)
                    wp(dve, B0, "wsth", s - 2, 32)
                pin = prep_in[s % 2][:, 0:c.DC * c.SLAB]
                phi, plo = prep_pair[s % 2]
                if s < c.NQ:
                    chain(dve, dve.tensor_scalar_mul(phi, pin, 1.0 / c.W))
                    ins = dve.scalar_tensor_tensor(plo, pin, 1.0 / c.W, phi,
                                                   OP.mult, OP.subtract)
                else:
                    chain(dve, dve.tensor_copy(phi, pin))
                    ins = dve.tensor_sub(plo, pin, phi)
                inc(ins, "prep_done", 1)

        @blk.gpsimd
        def _(pl):
            for s in range(NSL):
                w(pl, "prep_done", B0["prep_done"] + s + 1)
                plo = prep_pair[s % 2][1]
                dl = s_wal[s] if s < c.NSLAB_A else s_wol[s - c.NSLAB_A]
                incp(pl.dma_start(dl, plo), "wst", s)
            for i in range(2):
                n_attn = (c.NSLAB_A + 1 - i) // 2
                w(pl, f"wst{i}", B0[f"wst{i}"] + 16 * n_attn)
                w(pl, f"wsth{i}", B0[f"wsth{i}"] + 32 * n_attn)

    WST_FULL = {n: CNT[n] for n in ("wst0", "wst1", "wsth0", "wsth1")}

    B0 = dict(CNT)
    with nc.Block() as blk:
        LH = ln_ops(B0, 1, c.W, 0, 0, qh, ql)

        @blk.sync
        def _(sp):
            ln_drain(LH, "dma", sp)

        @blk.scalar
        def _(act):
            ln_drain(LH, "act", act)

        @blk.vector
        def _(dve):
            ln_drain(LH, "dve", dve)

        @blk.tensor
        def _(pe):
            ln_drain(LH, "pe", pe)


    B0 = dict(CNT)
    with nc.Block() as blk:
        KV = list(range(c.NQ, c.NSLAB_A))
        NK = c.NQ
        L0 = ln_ops(B0, c.NSUB, c.TILE, c.W, c.W, xnh, xnl,
                    dsem=("pxtm0", "plnx", "prowsd"))

        @blk.sync
        def _(sp):
            for i, s in enumerate(KV):
                if i >= 2:
                    pi = i - 2
                    if pi < NK:
                        w(sp, "qkmm", B0["qkmm"] + (pi + 1) * c.SJ)
                    else:
                        w(sp, "vmm", B0["vmm"] + (pi - NK + 1))
                wh_, wl_ = ws_ring[i % 2]
                incp(sp.dma_start(wh_, s_wah[s]), "wld", i)
                incp(sp.dma_start(wl_, s_wal[s]), "wld", i)

        @blk.tensor
        def _(pe):
            for i, s in enumerate(KV):
                wp(pe, B0, "wld", i, 32)
                wh_, wl_ = ws_ring[i % 2]
                if i < NK:
                    for j in range(c.SJ):
                        oc = i * c.SJ + j
                        if oc >= 2:
                            w(pe, "qkev", B0["qkev"] + oc - 1)
                        pb = p_mm[oc % 2]
                        nmm = 0
                        for dc in range(c.DC):
                            lt_h = wh_[:, dc * c.SLAB + j * 128: dc * c.SLAB + (j + 1) * 128]
                            lt_l = wl_[:, dc * c.SLAB + j * 128: dc * c.SLAB + (j + 1) * 128]
                            for lt, rh in ((lt_h, qh), (lt_h, ql), (lt_l, qh)):
                                ins = pe.matmul(pb[:, 0:c.W], lt,
                                                rh[:, dc * c.TILE: dc * c.TILE + c.W],
                                                start=(nmm == 0), stop=(nmm == 3 * c.DC - 1))
                                nmm += 1
                        inc(ins, "qkmm", 1)
                else:
                    g = i - NK
                    if g >= 2:
                        w(pe, "vev", B0["vev"] + g - 1)
                    pb = p_mm[2 + g % 2][:, 0:c.SLAB]
                    nmm = 0
                    for dc in range(c.DC):
                        xh_ = qh[:, dc * c.TILE: dc * c.TILE + c.W]
                        xl_ = ql[:, dc * c.TILE: dc * c.TILE + c.W]
                        wh_c = wh_[:, TS(dc, c.SLAB)]
                        wl_c = wl_[:, TS(dc, c.SLAB)]
                        for lt, rh in ((xh_, wh_c), (xh_, wl_c), (xl_, wh_c)):
                            ins = pe.matmul(pb, lt, rh, start=(nmm == 0),
                                            stop=(nmm == 3 * c.DC - 1))
                            nmm += 1
                    inc(ins, "vmm", 1)
                if i in (5, 8):
                    ln_pop(L0, "pe", pe)
            ln_drain(L0, "pe", pe)

        @blk.scalar
        def _(act):
            for oc in range(NK * c.SJ):
                w(act, "qkmm", B0["qkmm"] + oc + 1)
                ins = act.copy(khalo_pr_h[:, TS(oc, c.W)], p_mm[oc % 2][:, 0:c.W])
                inc(ins, "qkhi", 1)
                ln_pop(L0, "act", act)
            for g in range(c.NSLAB_A - 2 * c.NQ):
                w(act, "vmm", B0["vmm"] + g + 1)
                ins = act.copy(vhalo_pr_h[:, TS(g, c.SLAB)], p_mm[2 + g % 2][:, 0:c.SLAB])
                inc(ins, "vhi", 1)
                ln_pop(L0, "act", act)
            ln_drain(L0, "act", act)

        @blk.vector
        def _(dve):
            for oc in range(NK * c.SJ):
                w(dve, "qkhi", B0["qkhi"] + oc + 1)
                ins = dve.tensor_sub(khalo_pr_l[:, TS(oc, c.W)], p_mm[oc % 2][:, 0:c.W],
                                     khalo_pr_h[:, TS(oc, c.W)])
                inc(ins, "qkev", 1)
                ln_pop(L0, "dve", dve)
            for g in range(c.NSLAB_A - 2 * c.NQ):
                w(dve, "vhi", B0["vhi"] + g + 1)
                ins = dve.tensor_sub(vhalo_pr_l[:, TS(g, c.SLAB)],
                                     p_mm[2 + g % 2][:, 0:c.SLAB],
                                     vhalo_pr_h[:, TS(g, c.SLAB)])
                inc(ins, "vev", 1)
                ln_pop(L0, "dve", dve)
            ln_drain(L0, "dve", dve)

        @blk.gpsimd
        def _(pl):
            ln_drain(L0, "dma", pl)
            w(pl, "qkev", B0["qkev"] + NK * c.SJ)
            w(pl, "vev", B0["vev"] + (c.NSLAB_A - 2 * c.NQ))
            inc(pl.dma_start(s_khalo_h, khalo_pr_h), "halo_st", 16)
            inc(pl.dma_start(s_khalo_l, khalo_pr_l), "halo_st", 16)
            inc(pl.dma_start(s_vhalo_h, vhalo_pr_h), "halo_st", 16)
            inc(pl.dma_start(s_vhalo_l, vhalo_pr_l), "halo_st", 16)

    QL0_prev = None
    for t in range(c.NT):
        GL0 = {}
        WOL0 = {}
        QL0_next = {}
        tok0 = c.W + t * c.TILE
        xnt_h, xnt_l = xnh, xnl
        qt_h, qt_l = qh, ql

        B0 = dict(CNT)
        QBASE = QL0_prev if QL0_prev is not None else B0
        with nc.Block() as blk:
            @blk.sync
            def _(sp):
                for s in range(c.NSLAB_A):
                    if QL0_prev is not None and s < 2:
                        continue
                    if s >= 2:
                        sc = s - 2
                        if sc < 2 * c.NQ:
                            w(sp, "qkmm", B0["qkmm"] + (sc + 1) * c.SJ)
                        else:
                            w(sp, "vmm", B0["vmm"] + (sc - 2 * c.NQ + 1) * c.NSUB)
                    wh_, wl_ = ws_ring[s % 2]
                    incp(sp.dma_start(wh_, s_wah[s]), "wld", s)
                    incp(sp.dma_start(wl_, s_wal[s]), "wld", s)
                s_le = c.NSLAB_A - 2
                w(sp, "vmm", B0["vmm"] + (s_le - 2 * c.NQ + 1) * c.NSUB)
                w(sp, "halo_st", B0["halo_st"])
                inc(sp.dma_start(khalo_sb_h, s_khalo_h), "halo_ld", 16)
                inc(sp.dma_start(khalo_sb_l, s_khalo_l), "halo_ld", 16)
                inc(sp.dma_start(vhalo_sb_h, s_vhalo_h), "halo_ld", 16)
                inc(sp.dma_start(vhalo_sb_l, s_vhalo_l), "halo_ld", 16)

            @blk.tensor
            def _(pe):
                for s in range(c.NSLAB_A):
                    wp(pe, QBASE, "wld", s, 32)
                    wh_, wl_ = ws_ring[s % 2]
                    if s < 2 * c.NQ:
                        for j in range(c.SJ):
                            oc = s * c.SJ + j
                            if oc >= 2:
                                w(pe, "qkev", B0["qkev"] + oc - 1)
                            pb = p_mm[oc % 2]
                            nmm = 0
                            for dc in range(c.DC):
                                lt_h = wh_[:, dc * c.SLAB + j * 128: dc * c.SLAB + (j + 1) * 128]
                                lt_l = wl_[:, dc * c.SLAB + j * 128: dc * c.SLAB + (j + 1) * 128]
                                for lt, rh in ((lt_h, xnt_h), (lt_h, xnt_l), (lt_l, xnt_h)):
                                    ins = pe.matmul(pb, lt, rh[:, TS(dc, c.TILE)],
                                                    start=(nmm == 0), stop=(nmm == 3 * c.DC - 1))
                                    nmm += 1
                            inc(ins, "qkmm", 1)
                    else:
                        sv = s - 2 * c.NQ
                        for ts_ in range(c.NSUB):
                            g = sv * c.NSUB + ts_
                            if g >= 2:
                                w(pe, "vev", B0["vev"] + g - 1)
                            pb = p_mm[2 + g % 2][:, 0:c.SLAB]
                            nmm = 0
                            for dc in range(c.DC):
                                xh_ = xnt_h[:, dc * c.TILE + ts_ * 128: dc * c.TILE + (ts_ + 1) * 128]
                                xl_ = xnt_l[:, dc * c.TILE + ts_ * 128: dc * c.TILE + (ts_ + 1) * 128]
                                wh_c = wh_[:, TS(dc, c.SLAB)]
                                wl_c = wl_[:, TS(dc, c.SLAB)]
                                for lt, rh in ((xh_, wh_c), (xh_, wl_c), (xl_, wh_c)):
                                    ins = pe.matmul(pb, lt, rh, start=(nmm == 0),
                                                    stop=(nmm == 3 * c.DC - 1))
                                    nmm += 1
                            inc(ins, "vmm", 1)

            @blk.scalar
            def _(act):
                w(act, "halo_st", B0["halo_st"])
                if t == 0:
                    for n in WST_FULL:
                        w(act, n, WST_FULL[n])
                for s in range(c.NSLAB_A):
                    if s < 2 * c.NQ:
                        for j in range(c.SJ):
                            oc = s * c.SJ + j
                            w(act, "qkmm", B0["qkmm"] + oc + 1)
                            dst = qt_h if oc < c.DC else kh
                            occ = oc if oc < c.DC else oc - c.DC
                            ins = act.copy(dst[:, TS(occ, c.TILE)], p_mm[oc % 2])
                            inc(ins, "qkhi", 1)
                    else:
                        sv = s - 2 * c.NQ
                        for ts_ in range(c.NSUB):
                            g = sv * c.NSUB + ts_
                            w(act, "vmm", B0["vmm"] + g + 1)
                            sl = slice(ts_ * c.D + sv * c.SLAB, ts_ * c.D + (sv + 1) * c.SLAB)
                            ins = act.copy(vh[:, sl], p_mm[2 + g % 2][:, 0:c.SLAB])
                            inc(ins, "vhi", 1)

            @blk.vector
            def _(dve):
                w(dve, "halo_st", B0["halo_st"])
                if t == 0:
                    for n in WST_FULL:
                        w(dve, n, WST_FULL[n])
                for s in range(c.NSLAB_A):
                    if s < 2 * c.NQ:
                        for j in range(c.SJ):
                            oc = s * c.SJ + j
                            w(dve, "qkhi", B0["qkhi"] + oc + 1)
                            dsth = qt_h if oc < c.DC else kh
                            dstl = qt_l if oc < c.DC else kl
                            occ = oc if oc < c.DC else oc - c.DC
                            ins = dve.tensor_sub(dstl[:, TS(occ, c.TILE)], p_mm[oc % 2],
                                                 dsth[:, TS(occ, c.TILE)])
                            inc(ins, "qkev", 1)
                    else:
                        sv = s - 2 * c.NQ
                        for ts_ in range(c.NSUB):
                            g = sv * c.NSUB + ts_
                            w(dve, "vhi", B0["vhi"] + g + 1)
                            sl = slice(ts_ * c.D + sv * c.SLAB, ts_ * c.D + (sv + 1) * c.SLAB)
                            ins = dve.tensor_sub(vl[:, sl], p_mm[2 + g % 2][:, 0:c.SLAB], vh[:, sl])
                            inc(ins, "vev", 1)

        B0 = dict(CNT)
        with nc.Block() as blk:
            @blk.tensor
            def _(pe):
                w(pe, "halo_ld", B0["halo_ld"])

                def at_group(b):
                    if b >= 2:
                        w(pe, "at_ev", B0["at_ev"] + 2 * (b - 1))
                    nmm = 0
                    for half in range(2):
                        pb = p_at2[b % 2][:, half * 128:(half + 1) * 128]
                        for dc in range(c.DC):
                            if half == 0:
                                if b == 0:
                                    kh_ = khalo_sb_h[:, TS(dc, c.W)]
                                    kl_ = khalo_sb_l[:, TS(dc, c.W)]
                                else:
                                    kh_ = kh[:, dc * c.TILE + (b - 1) * c.W: dc * c.TILE + b * c.W]
                                    kl_ = kl[:, dc * c.TILE + (b - 1) * c.W: dc * c.TILE + b * c.W]
                            else:
                                kh_ = kh[:, dc * c.TILE + b * c.W: dc * c.TILE + (b + 1) * c.W]
                                kl_ = kl[:, dc * c.TILE + b * c.W: dc * c.TILE + (b + 1) * c.W]
                            qh_ = qt_h[:, dc * c.TILE + b * c.W: dc * c.TILE + (b + 1) * c.W]
                            ql_ = qt_l[:, dc * c.TILE + b * c.W: dc * c.TILE + (b + 1) * c.W]
                            for lt, rh in ((kh_, qh_), (kh_, ql_), (kl_, qh_)):
                                ins = pe.matmul(pb, lt, rh, start=(nmm == 0),
                                                stop=(nmm == 6 * c.DC - 1))
                                nmm += 1
                    inc(ins, "atmm", 1)

                def o_groups(b):
                    w(pe, "at_ev", B0["at_ev"] + 2 * b + 2)
                    for dc in range(c.DC):
                        g = b * c.DC + dc
                        if g >= 2:
                            w(pe, "oev", B0["oev"] + g - 1)
                        pb = p_o[g % 2]
                        nmm = 0
                        for half in range(2):
                            if half == 0:
                                if b == 0:
                                    vh_ = vhalo_sb_h[:, TS(dc, 128)]
                                    vl_ = vhalo_sb_l[:, TS(dc, 128)]
                                else:
                                    vh_ = vh[:, (b - 1) * c.D + dc * 128:(b - 1) * c.D + (dc + 1) * 128]
                                    vl_ = vl[:, (b - 1) * c.D + dc * 128:(b - 1) * c.D + (dc + 1) * 128]
                            else:
                                vh_ = vh[:, b * c.D + dc * 128: b * c.D + (dc + 1) * 128]
                                vl_ = vl[:, b * c.D + dc * 128: b * c.D + (dc + 1) * 128]
                            ah_ = (sb_ah if b % 2 == 0 else sb_ah2)[:, TS(half, c.W)]
                            al_ = (sb_al if b % 2 == 0 else sb_al2)[:, TS(half, c.W)]
                            for lt, rh in ((vh_, ah_), (vh_, al_), (vl_, ah_)):
                                ins = pe.matmul(pb, lt, rh, start=(nmm == 0), stop=(nmm == 5))
                                nmm += 1
                        inc(ins, "omm", 1)

                for b in range(c.NB):
                    at_group(b)
                    if b >= 1:
                        o_groups(b - 1)
                o_groups(c.NB - 1)

            @blk.scalar
            def _(act):
                wp_all(act, B0, "outst")
                for b in range(c.NB):
                    w(act, "atmm", B0["atmm"] + b + 1)
                    if b >= 1:
                        w(act, "at_ev", B0["at_ev"] + 2 * b)
                    for half in range(2):
                        ins = act.activation(sb_rbuf[:, TS(half, c.W)],
                                             p_at2[b % 2][:, half * 128:(half + 1) * 128],
                                             AF.Relu)
                        inc(ins, "at_relu", 1)
                    for dc in range(c.DC):
                        g = b * c.DC + dc
                        w(act, "omm", B0["omm"] + g + 1)
                        ins = act.copy(oh[:, dc * c.TILE + b * c.W: dc * c.TILE + (b + 1) * c.W],
                                       p_o[g % 2])
                        inc(ins, "ohi", 1)

            @blk.vector
            def _(dve):
                wp_all(dve, B0, "outst")
                for b in range(c.NB):
                    w(dve, "at_relu", B0["at_relu"] + 2 * b + 2)
                    if b >= 2:
                        w(dve, "omm", B0["omm"] + (b - 1) * c.DC)
                    for half in range(2):
                        pb = p_at2[b % 2][:, half * 128:(half + 1) * 128]
                        ins = dve.scalar_tensor_tensor(sb_ambuf[:, TS(half, c.W)], pb, 1.0,
                                                       sb_rbuf[:, TS(half, c.W)],
                                                       OP.mult, OP.mult)
                    chain(dve, ins)
                    for half in range(2):
                        if half == 0:
                            msk = sb_masks[:, 2 * c.W:3 * c.W] if (t == 0 and b == 0) \
                                else sb_masks[:, 0:c.W]
                        else:
                            msk = sb_masks[:, c.W:2 * c.W]
                        ins = dve.tensor_mul(sb_ambuf[:, TS(half, c.W)],
                                             sb_ambuf[:, TS(half, c.W)], msk)
                    chain(dve, ins)
                    ah_b = sb_ah if b % 2 == 0 else sb_ah2
                    al_b = sb_al if b % 2 == 0 else sb_al2
                    for half in range(2):
                        ins = dve.tensor_copy(ah_b[:, TS(half, c.W)],
                                              sb_ambuf[:, TS(half, c.W)])
                    chain(dve, ins)
                    for half in range(2):
                        ins = dve.tensor_sub(al_b[:, TS(half, c.W)],
                                             sb_ambuf[:, TS(half, c.W)],
                                             ah_b[:, TS(half, c.W)])
                        inc(ins, "at_ev", 1)
                    for dc in range(c.DC):
                        g = b * c.DC + dc
                        w(dve, "ohi", B0["ohi"] + g + 1)
                        ins = dve.tensor_sub(ol[:, dc * c.TILE + b * c.W: dc * c.TILE + (b + 1) * c.W],
                                             p_o[g % 2],
                                             oh[:, dc * c.TILE + b * c.W: dc * c.TILE + (b + 1) * c.W])
                        inc(ins, "oev", 1)

            @blk.sync
            def _(sp):
                if t == 0:
                    for n in WST_FULL:
                        w(sp, n, WST_FULL[n])
                WOL0.update({n: CNT[n] for n in ("wld0", "wld1")})
                w(sp, "omm", B0["omm"] + c.DC)
                incp(sp.dma_start(ws_ring[0][0], s_woh[0]), "wld", 0)
                incp(sp.dma_start(ws_ring[0][1], s_wol[0]), "wld", 0)
                incp(sp.dma_start(ws_ring[1][0], s_woh[1]), "wld", 1)
                incp(sp.dma_start(ws_ring[1][1], s_wol[1]), "wld", 1)

            @blk.gpsimd
            def _(pl):
                w(pl, "halo_ld", B0["halo_ld"])
                src = sb3(kh, c.TILE)[:, :, c.TILE - c.W:c.TILE]
                inc(pl.dma_start(sb3(s_khalo_h, c.W), src), "halo_st", 16)
                src = sb3(kl, c.TILE)[:, :, c.TILE - c.W:c.TILE]
                inc(pl.dma_start(sb3(s_khalo_l, c.W), src), "halo_st", 16)
                inc(pl.dma_start(s_vhalo_h, vh[:, (c.NSUB - 1) * c.D: c.NSUB * c.D]), "halo_st", 16)
                inc(pl.dma_start(s_vhalo_l, vl[:, (c.NSUB - 1) * c.D: c.NSUB * c.D]), "halo_st", 16)

        B0 = dict(CNT)
        with nc.Block() as blk:
            @blk.sync
            def _(sp):
                if t == 0:
                    for n in WST_FULL:
                        w(sp, n, WST_FULL[n])
                for s in range(2, c.NSLAB_O):
                    w(sp, "womm", B0["womm"] + (s - 1) * c.SJ)
                    wh_, wl_ = ws_ring[s % 2]
                    incp(sp.dma_start(wh_, s_woh[s]), "wld", s)
                    incp(sp.dma_start(wl_, s_wol[s]), "wld", s)

            @blk.tensor
            def _(pe):
                for s in range(c.NSLAB_O):
                    wp(pe, WOL0, "wld", s, 32)
                    wh_, wl_ = ws_ring[s % 2]
                    for j in range(c.SJ):
                        oc = s * c.SJ + j
                        if oc >= 2:
                            w(pe, "wocp", B0["wocp"] + oc - 1)
                        pb = p_mm[oc % 2]
                        nmm = 0
                        for dc in range(c.DC):
                            lt_h = wh_[:, dc * c.SLAB + j * 128: dc * c.SLAB + (j + 1) * 128]
                            lt_l = wl_[:, dc * c.SLAB + j * 128: dc * c.SLAB + (j + 1) * 128]
                            for lt, rh in ((lt_h, oh), (lt_h, ol), (lt_l, oh)):
                                ins = pe.matmul(pb, lt, rh[:, TS(dc, c.TILE)],
                                                start=(nmm == 0), stop=(nmm == 3 * c.DC - 1))
                                nmm += 1
                        inc(ins, "womm", 1)

            @blk.scalar
            def _(act):
                for oc in range(c.DC):
                    w(act, "womm", B0["womm"] + oc + 1)
                    if oc >= 2:
                        wp(act, B0, "resst", oc - 2, 16)
                    else:
                        nm = f"resst{oc % 2}"
                        if B0[nm]:
                            w(act, nm, B0[nm])
                    ins = act.copy(reschunk[oc % 2], p_mm[oc % 2])
                    inc(ins, "wocp", 1)

            @blk.gpsimd
            def _(pl):
                w(pl, "halo_st", B0["halo_st"])
                GL0.update({n: CNT[n] for n in ("gld0", "gld1", "uld0", "uld1",
                                                "dld0", "dld1", "dld20", "dld21")})
                for j in (0, 1):
                    g_, u_ = gu_ring[j % 2]
                    srcg = gate_w_t[:, TS(j, 128)].rearrange("(c p) n -> p c n", p=128)
                    srcu = up_w_t[:, TS(j, 128)].rearrange("(c p) n -> p c n", p=128)
                    incp(pl.dma_start(sb3(g_, 128), srcg), "gld", j)
                    incp(pl.dma_start(sb3(u_, 128), srcu), "uld", j)
                fh = c.FCH // 2
                for ch in (0, 1):
                    for a, nm in ((0, "dld"), (1, "dld2")):
                        srcd = down_w_t[a * c.FF4:(a + 1) * c.FF4,
                                        TS(ch, 128)].rearrange("(c p) n -> p c n", p=128)
                        dst = dn_ring[ch % 2][:, a * fh * 128:(a + 1) * fh * 128]
                        incp(pl.dma_start(sb3(dst, 128), srcd), nm, ch)
                wp_all(pl, B0, "resld")
                for oc in range(c.DC):
                    w(pl, "wocp", B0["wocp"] + oc + 1)
                    incp(pl.dma_start(s_res[oc], reschunk[oc % 2]), "resst", oc)

        B0 = dict(CNT)
        with nc.Block() as blk:
            if t + 1 < c.NT:
                nxt = c.W + (t + 1) * c.TILE
                LF = ln_ops(B0, c.NSUB, c.TILE, nxt, nxt, xnh, xnl,
                            overwrite_gate=("umm", B0["umm"] + c.FC), defer_xn=True)
            else:
                LF = None

            @blk.sync
            def _(sp):
                ln_drain(LF, "dma", sp)
                if t + 1 < c.NT:
                    QL0_next.update({n: CNT[n] for n in ("wld0", "wld1")})
                    nch = c.DC * c.SLAB // c.TILE
                    w(sp, "t0s", B0["t0s"] + min(c.DC, nch))
                    incp(sp.dma_start(ws_ring[0][0], s_wah[0]), "wld", 0)
                    incp(sp.dma_start(ws_ring[0][1], s_wal[0]), "wld", 0)
                    w(sp, "t0s", B0["t0s"] + min(c.DC, 2 * nch))
                    incp(sp.dma_start(ws_ring[1][0], s_wah[1]), "wld", 1)
                    incp(sp.dma_start(ws_ring[1][1], s_wal[1]), "wld", 1)

            @blk.tensor
            def _(pe):
                for hf in range(2):
                    for icl in range(c.FCH):
                        j = hf * c.FCH + icl
                        wp(pe, GL0, "gld", j, 16)
                        if hf == 1 and icl == 0:
                            w(pe, "acc", B0["acc"] + c.DC)
                        if j >= 2:
                            w(pe, "hev", B0["hev"] + j - 1)
                        g_, u_ = gu_ring[j % 2]
                        pg = p_mm[(j % 2) * 2]
                        pu = p_mm[(j % 2) * 2 + 1]
                        nmm = 0
                        for dc in range(c.DC):
                            lt = g_[:, TS(dc, 128)]
                            for rh in (xnt_h, xnt_l):
                                ins = pe.matmul(pg, lt, rh[:, TS(dc, c.TILE)],
                                                start=(nmm == 0), stop=(nmm == 2 * c.DC - 1))
                                nmm += 1
                        inc(ins, "gmm", 1)
                        wp(pe, GL0, "uld", j, 16)
                        nmm = 0
                        for dc in range(c.DC):
                            lt = u_[:, TS(dc, 128)]
                            for rh in (xnt_h, xnt_l):
                                ins = pe.matmul(pu, lt, rh[:, TS(dc, c.TILE)],
                                                start=(nmm == 0), stop=(nmm == 2 * c.DC - 1))
                                nmm += 1
                        inc(ins, "umm", 1)
                        if icl >= 2:
                            k = icl - 2
                            if k == 0 and hf == 1:
                                w(pe, "shev", B0["shev"] + 1)
                            pe.matmul(p_sumh, sb_ones_col, hh[:, TS(k, c.TILE)],
                                      start=(k == 0), stop=False)
                            pe.matmul(p_sumh, sb_ones_col, hl[:, TS(k, c.TILE)],
                                      start=False, stop=False)
                        if j in (11, 15):
                            ln_pop(LF, "pe", pe)
                    if hf == 1:
                        ln_drain(LF, "pe", pe)
                    w(pe, "hev", B0["hev"] + (hf + 1) * c.FCH)
                    for k in (c.FCH - 2, c.FCH - 1):
                        pe.matmul(p_sumh, sb_ones_col, hh[:, TS(k, c.TILE)],
                                  start=False, stop=False)
                        ins = pe.matmul(p_sumh, sb_ones_col, hl[:, TS(k, c.TILE)],
                                        start=False, stop=(k == c.FCH - 1))
                    inc(ins, "shmm", 1)
                    w(pe, "shev", B0["shev"] + hf + 1)
                    for ch in range(c.DC):
                        j = hf * c.DC + ch
                        wp(pe, GL0, "dld", j, 16)
                        if hf == 0:
                            if j >= 2:
                                w(pe, "acc", B0["acc"] + j - 1)
                        else:
                            if ch >= 2:
                                w(pe, "fin", B0["fin"] + ch - 1)
                            else:
                                w(pe, "acc", B0["acc"] + c.DC + ch - 1)
                        pb = p_mm[j % 2]
                        nmm = 0
                        for icl in range(c.FCH):
                            if icl == c.FCH // 2:
                                wp(pe, GL0, "dld2", j, 16)
                            lt = dn_ring[j % 2][:, TS(icl, 128)]
                            for rh in (hh, hl):
                                ins = pe.matmul(pb, lt, rh[:, TS(icl, c.TILE)],
                                                start=(nmm == 0), stop=False)
                                nmm += 1
                        pe.matmul(pb, sb_neg128, sumh_h[:, 0:c.TILE], start=False, stop=False)
                        ins = pe.matmul(pb, sb_neg128, sumh_l[:, 0:c.TILE], start=False, stop=True)
                        inc(ins, "dnmm", 1)

            @blk.scalar
            def _(act):
                w(act, "halo_st", B0["halo_st"])
                for hf in range(2):
                    for icl in range(c.FCH):
                        j = hf * c.FCH + icl
                        w(act, "gmm", B0["gmm"] + j + 1)
                        if j >= 2:
                            w(act, "hf32s", B0["hf32s"] + j - 1)
                        if hf == 1 and icl == 0:
                            w(act, "dnmm", B0["dnmm"] + c.DC)
                        pg = p_mm[(j % 2) * 2]
                        i2 = hf * c.FCH + icl
                        ins = act.activation(sgbuf[j % 2], pg, AF.Silu,
                                             scale=sb_gs[:, i2:i2 + 1])
                        inc(ins, "sg", 1)
                        w(act, "hf32s", B0["hf32s"] + j + 1)
                        ins = act.copy(hh[:, TS(icl, c.TILE)], hf32[j % 2])
                        inc(ins, "hhi", 1)
                        ln_pop(LF, "act", act)
                ln_drain(LF, "act", act)
                for ch in range(c.DC):
                    w(act, "acc", B0["acc"] + ch + 1)
                    if ch >= 2:
                        w(act, "obuf", B0["obuf"] + ch - 1)
                    ins = act.activation(t0buf[ch % 2], accum[:, TS(ch, c.TILE)], AF.Copy,
                                         scale=sb_ds[:, ch:ch + 1])
                    inc(ins, "t0s", 1)

            @blk.vector
            def _(dve):
                w(dve, "halo_st", B0["halo_st"])
                for hf in range(2):
                    for icl in range(c.FCH):
                        j = hf * c.FCH + icl
                        w(dve, "sg", B0["sg"] + j + 1)
                        w(dve, "umm", B0["umm"] + j + 1)
                        if j >= 2:
                            w(dve, "hhi", B0["hhi"] + j - 1)
                        if hf == 1 and icl == 0:
                            w(dve, "dnmm", B0["dnmm"] + c.DC)
                        pu = p_mm[(j % 2) * 2 + 1]
                        i2 = hf * c.FCH + icl
                        ins = dve.scalar_tensor_tensor(hf32[j % 2], pu, sb_us[:, i2:i2 + 1],
                                                       sgbuf[j % 2], OP.mult, OP.mult)
                        inc(ins, "hf32s", 1)
                        w(dve, "hhi", B0["hhi"] + j + 1)
                        ins = dve.tensor_sub(hl[:, TS(icl, c.TILE)], hf32[j % 2],
                                             hh[:, TS(icl, c.TILE)])
                        inc(ins, "hev", 1)
                        ln_pop(LF, "dve", dve)
                    if hf == 1:
                        ln_drain(LF, "dve", dve)
                    w(dve, "shmm", B0["shmm"] + hf + 1)
                    chain(dve, dve.tensor_copy(sumh_h[:, 0:c.TILE], p_sumh))
                    ins = dve.tensor_sub(sumh_l[:, 0:c.TILE], p_sumh, sumh_h[:, 0:c.TILE])
                    inc(ins, "shev", 1)
                    for ch in range(c.DC):
                        j = hf * c.DC + ch
                        w(dve, "dnmm", B0["dnmm"] + j + 1)
                        pb = p_mm[j % 2]
                        if hf == 0:
                            ins = dve.tensor_copy(accum[:, TS(ch, c.TILE)], pb)
                            inc(ins, "acc", 1)
                        else:
                            w(dve, "t0s", B0["t0s"] + ch + 1)
                            ins = dve.scalar_tensor_tensor(t0buf[ch % 2], pb,
                                                           sb_ds[:, ch:ch + 1],
                                                           t0buf[ch % 2],
                                                           OP.mult, OP.add)
                            inc(ins, "fin", 1)
                            w(dve, "fin", CNT["fin"])
                            wp(dve, B0, "resld", ch, 16)
                            chain(dve, dve.tensor_add(t0buf[ch % 2], t0buf[ch % 2],
                                                      resbuf[ch % 2]))
                            wp(dve, B0, "xld", ch, 16)
                            ins = dve.tensor_add(xbuf[ch % 2], t0buf[ch % 2], xbuf[ch % 2])
                            inc(ins, "obuf", 1)
                            ln_pop(LF, "dve2", dve)
                            ln_pop(LF, "dve2", dve)
                ln_drain(LF, "dve2", dve)

            @blk.gpsimd
            def _(pl):
                w(pl, "halo_st", B0["halo_st"])
                for hf in range(2):
                    for icl in range(c.FCH):
                        j = hf * c.FCH + icl
                        if j < 2:
                            continue
                        w(pl, "gmm", B0["gmm"] + j - 1)
                        g_, u_ = gu_ring[j % 2]
                        ic = hf * c.FCH + icl
                        srcg = gate_w_t[:, TS(ic, 128)].rearrange("(c p) n -> p c n", p=128)
                        srcu = up_w_t[:, TS(ic, 128)].rearrange("(c p) n -> p c n", p=128)
                        incp(pl.dma_start(sb3(g_, 128), srcg), "gld", j)
                        w(pl, "umm", B0["umm"] + j - 1)
                        incp(pl.dma_start(sb3(u_, 128), srcu), "uld", j)
                    if hf == 0:
                        for ch in range(2, c.DC):
                            w(pl, "dnmm", B0["dnmm"] + ch - 1)
                            fh = c.FCH // 2
                            for a, nm in ((0, "dld"), (1, "dld2")):
                                srcd = down_w_t[a * c.FF4: (a + 1) * c.FF4,
                                                TS(ch, 128)].rearrange(
                                    "(c p) n -> p c n", p=128)
                                dst = dn_ring[ch % 2][:, a * fh * 128:(a + 1) * fh * 128]
                                incp(pl.dma_start(sb3(dst, 128), srcd), nm, ch)
                wp_all(pl, B0, "resst")
                for ch in range(c.DC):
                    j = c.DC + ch
                    w(pl, "dnmm", B0["dnmm"] + j - 1)
                    fh = c.FCH // 2
                    for a, nm in ((0, "dld"), (1, "dld2")):
                        srcd = down_w_t[c.FF2 + a * c.FF4: c.FF2 + (a + 1) * c.FF4,
                                        TS(ch, 128)].rearrange(
                            "(c p) n -> p c n", p=128)
                        dst = dn_ring[j % 2][:, a * fh * 128:(a + 1) * fh * 128]
                        incp(pl.dma_start(sb3(dst, 128), srcd), nm, j)
                    if ch >= 2:
                        w(pl, "obuf", B0["obuf"] + ch - 1)
                        wp(pl, B0, "outst", ch - 2, 16)
                    else:
                        nm = f"outst{ch % 2}"
                        if B0[nm]:
                            w(pl, nm, B0[nm])
                    incp(pl.dma_start(resbuf[ch % 2], s_res[ch]), "resld", ch)
                    src = x_fm[TS(ch, 128), tok0: tok0 + c.TILE]
                    incp(pl.dma_start(xbuf[ch % 2], src), "xld", ch)
                    if ch >= 1:
                        w(pl, "obuf", B0["obuf"] + ch)
                        incp(pl.dma_start(out_fm[TS(ch - 1, 128), TS(t, c.TILE)],
                                          xbuf[(ch - 1) % 2]), "outst", ch - 1)
                w(pl, "obuf", B0["obuf"] + c.DC)
                incp(pl.dma_start(out_fm[TS(c.DC - 1, 128), TS(t, c.TILE)],
                                  xbuf[(c.DC - 1) % 2]), "outst", c.DC - 1)

        QL0_prev = QL0_next if t + 1 < c.NT else None

    with nc.Block() as blk:
        @blk.gpsimd
        def _(pl):
            for name in ["outst0", "outst1", "resst0", "resst1", "halo_st",
                         "wst0", "wst1", "gld0", "gld1", "uld0", "uld1",
                         "dld0", "dld1", "dld20", "dld21",
                         "resld0", "resld1", "xld0", "xld1"]:
                if CNT[name]:
                    pl.wait_ge(S[name], CNT[name])

        @blk.sync
        def _(sp):
            for name in ["wld0", "wld1", "wsth0", "wsth1", "xtm0", "lnx0", "lnx1",
                         "halo_ld", "ld_const"]:
                if CNT[name]:
                    sp.wait_ge(S[name], CNT[name])

        @blk.scalar
        def _(act):
            for name in ["pxtm0", "plnx0", "plnx1", "prowsd", "rowsd"]:
                if CNT[name]:
                    act.wait_ge(S[name], CNT[name])

    return nc


def make_in_maps(cfg, inputs, n_cores=N_CORES):
    c = cfg
    x = np.asarray(inputs["L0_attn_x"], dtype=np.float32)
    B, T, D = x.shape
    n_seq_shards = n_cores // B
    SEQ = T // n_seq_shards
    assert SEQ == c.TOK and D == c.D

    def pack_slabs(wt, nslab):
        arr = wt.reshape(c.DC, 128, nslab, c.SLAB)
        return np.ascontiguousarray(
            arr.transpose(1, 2, 0, 3).reshape(128, nslab * c.DC * c.SLAB))

    w_attn_t = pack_slabs(np.asarray(inputs["L0_attn_w_attn"], np.float32).T,
                          c.NSLAB_A)
    w_o_t = pack_slabs(np.asarray(inputs["L0_attn_w_o"], np.float32).T, c.NSLAB_O)
    gate_w_t = np.ascontiguousarray(np.asarray(inputs["L0_ffn_gate_w"], np.int32).T)
    up_w_t = np.ascontiguousarray(np.asarray(inputs["L0_ffn_up_w"], np.int32).T)
    down_w_t = np.ascontiguousarray(np.asarray(inputs["L0_ffn_down_w"], np.int32).T)
    gate_s = np.ascontiguousarray(np.asarray(inputs["L0_ffn_gate_scale"], np.float32).reshape(c.FC, 128).T)
    up_s = np.ascontiguousarray(np.asarray(inputs["L0_ffn_up_scale"], np.float32).reshape(c.FC, 128).T)
    down_s = np.ascontiguousarray(np.asarray(inputs["L0_ffn_down_scale"], np.float32).reshape(c.DC, 128).T)

    jj = np.arange(c.W)[:, None]
    ii = np.arange(c.W)[None, :]
    m_prev = (jj >= ii).astype(np.float32)
    m_cur = (jj <= ii).astype(np.float32)
    m_zero = np.zeros((c.W, c.W), np.float32)
    ident = np.eye(128, dtype=np.float32)

    in_maps = []
    for core in range(n_cores):
        b = core // n_seq_shards
        sh = core % n_seq_shards
        xt = np.zeros((c.TOKH, c.D), np.float32)
        t0 = sh * SEQ
        if sh > 0:
            xt[:] = x[b, t0 - c.W: t0 + SEQ]
        else:
            xt[c.W:] = x[b, 0:SEQ]
        x_tm_a = np.ascontiguousarray(xt)
        x_fm_a = np.ascontiguousarray(xt.T)
        m_first = m_prev if sh > 0 else m_zero
        import ml_dtypes
        masks = np.ascontiguousarray(
            np.concatenate([m_prev, m_cur, m_first], axis=1).astype(ml_dtypes.bfloat16))
        in_maps.append({
            "x_fm": x_fm_a, "x_tm": x_tm_a,
            "w_attn_t": w_attn_t, "w_o_t": w_o_t,
            "gate_w_t": gate_w_t, "up_w_t": up_w_t, "down_w_t": down_w_t,
            "gate_s": gate_s, "up_s": up_s, "down_s": down_s,
            "masks": masks, "ident": ident,
        })
    return in_maps


def kernel(**inputs):
    cfg = CFG()
    nc = build(cfg)
    in_maps = make_in_maps(cfg, inputs)
    res = run_bass_kernel_spmd(nc, in_maps, core_ids=list(range(N_CORES)))
    x = np.asarray(inputs["L0_attn_x"])
    B, T, D = x.shape
    n_seq_shards = N_CORES // B
    SEQ = T // n_seq_shards
    out = np.empty((B, T, D), np.float32)
    for core in range(N_CORES):
        b = core // n_seq_shards
        sh = core % n_seq_shards
        o = np.asarray(res.results[core]["out_fm"])
        out[b, sh * SEQ:(sh + 1) * SEQ, :] = o.T
    return out


# version 17 (final)
# speedup vs baseline: 1.1550x; 1.0193x over previous
import numpy as np
import concourse.bass as bass
import concourse.mybir as mybir
from concourse.bass_utils import run_bass_kernel_spmd

F32 = mybir.dt.float32
BF16 = mybir.dt.bfloat16
I32 = mybir.dt.int32
AF = mybir.ActivationFunctionType
OP = mybir.AluOpType

N_CORES = 8


class CFG:
    def __init__(self, D=2048, FF=8192, TOK=2048, TILE=512, W=128, SLAB=256):
        self.D, self.FF, self.TOK, self.TILE, self.W = D, FF, TOK, TILE, W
        self.SLAB = SLAB
        self.DC = D // 128
        self.FC = FF // 128
        self.FCH = self.FC // 2
        self.FF2 = FF // 2
        self.FF4 = FF // 4
        self.NT = TOK // TILE
        self.NB = TILE // W
        self.NSUB = TILE // 128
        self.TOKH = TOK + W
        self.NSLAB_A = 3 * D // SLAB
        self.NSLAB_O = D // SLAB
        self.SJ = SLAB // 128
        self.NQ = D // SLAB


def TS(i, n):
    return slice(i * n, (i + 1) * n)


def build(cfg: CFG):
    c = cfg
    nc = bass.Bass(target_bir_lowering=False, trn_type="TRN2")

    x_fm = nc.dram_tensor("x_fm", [c.D, c.TOKH], F32, kind="ExternalInput").ap()
    x_tm = nc.dram_tensor("x_tm", [c.TOKH, c.D], F32, kind="ExternalInput").ap()
    w_attn_t = nc.dram_tensor("w_attn_t", [128, c.NSLAB_A * c.DC * c.SLAB], F32,
                              kind="ExternalInput").ap()
    w_o_t = nc.dram_tensor("w_o_t", [128, c.NSLAB_O * c.DC * c.SLAB], F32,
                           kind="ExternalInput").ap()
    gate_w_t = nc.dram_tensor("gate_w_t", [c.D, c.FF], I32, kind="ExternalInput").ap()
    up_w_t = nc.dram_tensor("up_w_t", [c.D, c.FF], I32, kind="ExternalInput").ap()
    down_w_t = nc.dram_tensor("down_w_t", [c.FF, c.D], I32, kind="ExternalInput").ap()
    gate_s_in = nc.dram_tensor("gate_s", [128, c.FC], F32, kind="ExternalInput").ap()
    up_s_in = nc.dram_tensor("up_s", [128, c.FC], F32, kind="ExternalInput").ap()
    down_s_in = nc.dram_tensor("down_s", [128, c.DC], F32, kind="ExternalInput").ap()
    masks_in = nc.dram_tensor("masks", [128, 3 * c.W], BF16, kind="ExternalInput").ap()
    ident_in = nc.dram_tensor("ident", [128, 128], F32, kind="ExternalInput").ap()
    out_fm = nc.dram_tensor("out_fm", [c.D, c.TOK], F32, kind="ExternalOutput").ap()

    s_wah = nc.dram_tensor("s_wah", [c.NSLAB_A, 128, c.DC * c.SLAB], BF16, kind="Internal").ap()
    s_wal = nc.dram_tensor("s_wal", [c.NSLAB_A, 128, c.DC * c.SLAB], BF16, kind="Internal").ap()
    s_woh = nc.dram_tensor("s_woh", [c.NSLAB_O, 128, c.DC * c.SLAB], BF16, kind="Internal").ap()
    s_wol = nc.dram_tensor("s_wol", [c.NSLAB_O, 128, c.DC * c.SLAB], BF16, kind="Internal").ap()
    s_res = nc.dram_tensor("s_res", [c.DC, 128, c.TILE], F32, kind="Internal").ap()
    s_rows = nc.dram_tensor("s_rows", [2 * 4, 128], F32, kind="Internal").ap()
    s_khalo_h = nc.dram_tensor("s_khalo_h", [128, c.DC * c.W], BF16, kind="Internal").ap()
    s_khalo_l = nc.dram_tensor("s_khalo_l", [128, c.DC * c.W], BF16, kind="Internal").ap()
    s_vhalo_h = nc.dram_tensor("s_vhalo_h", [128, c.D], BF16, kind="Internal").ap()
    s_vhalo_l = nc.dram_tensor("s_vhalo_l", [128, c.D], BF16, kind="Internal").ap()

    cur = [nc.sbuf_base + 64]
    ded = {}

    def salloc(name, shape, dtype, offset=None):
        if offset is None:
            off = (cur[0] + 63) // 64 * 64
            cur[0] = off + shape[-1] * mybir.dt.size(dtype)
        else:
            off = offset
        t = nc.alloc_sbuf_tensor_at(name, shape, dtype, offset=off)
        ded[name] = off
        return t.ap()

    sb_masks = salloc("sb_masks", [128, 3 * c.W], BF16)
    sb_gs = salloc("sb_gs", [128, c.FC], F32)
    sb_us = salloc("sb_us", [128, c.FC], F32)
    sb_ds = salloc("sb_ds", [128, c.DC], F32)
    sb_ident = salloc("sb_ident", [128, 128], F32)
    sb_ones = salloc("sb_ones", [2, 128], BF16)
    sb_ones_col = salloc("sb_ones_col", [128, 1], BF16)
    sb_neg128 = salloc("sb_neg128", [1, 128], BF16)
    sb_stat = salloc("sb_stat", [128, 8 * c.NSUB], F32)
    sb_pk = salloc("sb_pk", [128, 2 * c.NSUB], F32)
    sb_t8 = salloc("sb_t8", [2 * c.NSUB, 128], F32)
    sb_mu_bh = salloc("sb_mu_bh", [1, c.TILE], BF16)
    sb_mu_bl = salloc("sb_mu_bl", [1, c.TILE], BF16)
    sb_r_bh = salloc("sb_r_bh", [1, c.TILE], BF16)
    sb_r_bl = salloc("sb_r_bl", [1, c.TILE], BF16)
    sb_ah = salloc("sb_ah", [128, 2 * c.W], BF16)
    sb_al = salloc("sb_al", [128, 2 * c.W], BF16)
    sb_ah2 = salloc("sb_ah2", [128, 2 * c.W], BF16)
    sb_al2 = salloc("sb_al2", [128, 2 * c.W], BF16)
    sb_rbuf = salloc("sb_rbuf", [128, 2 * c.W], F32)
    sb_ambuf = salloc("sb_ambuf", [128, 2 * c.W], F32)
    sumh_h = salloc("sumh_h", [1, c.TILE], BF16, offset=ded["sb_rbuf"])
    sumh_l = salloc("sumh_l", [1, c.TILE], BF16, offset=ded["sb_ambuf"])
    reschunk = [salloc(f"reschunk{i}", [128, c.TILE], F32) for i in range(2)]

    small_end = (cur[0] + 63) // 64 * 64

    xn_half = c.DC * c.TILE * 2
    h_half = c.FCH * c.TILE * 2
    XN_B = small_end
    A_B = XN_B + 2 * xn_half
    prep_end = 2 * xn_half + 2 * (2 * c.DC * c.SLAB * 2)
    halo_base = max(4 * xn_half, prep_end)
    A_SZ = max(6 * xn_half,
               2 * h_half + 2 * (2 * c.DC * 256) + 2 * c.FCH * 256,
               halo_base + 2 * c.DC * c.W * 2 + 2 * c.D * 2,
               2 * (2 * c.DC * c.SLAB * 2) + 2 * c.FCH * 256)
    B_B = A_B + A_SZ
    B_SZ = max(c.DC * c.TILE * 4, 10 * c.TILE * 4 + c.D * 4 + 2 * c.NSUB * 512,
               2 * 4 * max(c.DC * c.SLAB, c.FCH * 128, c.DC * 128))
    C_B = B_B + B_SZ
    C_SZ = max(c.DC * c.TILE * 4,
               2 * (2 * c.DC * c.SLAB * 2),
               2 * c.DC * c.W * 2 + 2 * c.D * 2)
    TOP = C_B + C_SZ
    assert TOP <= nc.sbuf_top - 64, (TOP, nc.sbuf_top)

    xnh = salloc("xnh", [128, c.DC * c.TILE], BF16, offset=XN_B)
    xnl = salloc("xnl", [128, c.DC * c.TILE], BF16, offset=XN_B + xn_half)

    qh = salloc("qh", [128, c.DC * c.TILE], BF16, offset=A_B + 0 * xn_half)
    ql = salloc("ql", [128, c.DC * c.TILE], BF16, offset=A_B + 1 * xn_half)
    kh = salloc("kh", [128, c.DC * c.TILE], BF16, offset=A_B + 2 * xn_half)
    kl = salloc("kl", [128, c.DC * c.TILE], BF16, offset=A_B + 3 * xn_half)
    vh = salloc("vh", [128, c.NSUB * c.D], BF16, offset=A_B + 4 * xn_half)
    vl = salloc("vl", [128, c.NSUB * c.D], BF16, offset=A_B + 5 * xn_half)
    hh = salloc("hh", [128, c.FCH * c.TILE], BF16, offset=A_B)
    hl = salloc("hl", [128, c.FCH * c.TILE], BF16, offset=A_B + h_half)
    off = A_B + 2 * h_half
    gu_ring = []
    for i in range(2):
        g = salloc(f"gbuf{i}", [128, c.DC * 128], BF16, offset=off); off += c.DC * 256
        u = salloc(f"ubuf{i}", [128, c.DC * 128], BF16, offset=off); off += c.DC * 256
        gu_ring.append((g, u))
    dn_ring = []
    for i in range(2):
        d = salloc(f"dbuf{i}", [128, c.FCH * 128], BF16, offset=off); off += c.FCH * 256
        dn_ring.append(d)
    assert off <= A_B + A_SZ
    off = A_B + 2 * xn_half
    prep_pair = []
    for i in range(2):
        ph = salloc(f"preph{i}", [128, c.DC * c.SLAB], BF16, offset=off); off += c.DC * c.SLAB * 2
        pl_ = salloc(f"prepl{i}", [128, c.DC * c.SLAB], BF16, offset=off); off += c.DC * c.SLAB * 2
        prep_pair.append((ph, pl_))
    assert off <= A_B + A_SZ
    hoff = A_B + halo_base
    khalo_pr_h = salloc("khalo_pr_h", [128, c.DC * c.W], BF16, offset=hoff); hoff += c.DC * c.W * 2
    khalo_pr_l = salloc("khalo_pr_l", [128, c.DC * c.W], BF16, offset=hoff); hoff += c.DC * c.W * 2
    vhalo_pr_h = salloc("vhalo_pr_h", [128, c.D], BF16, offset=hoff); hoff += c.D * 2
    vhalo_pr_l = salloc("vhalo_pr_l", [128, c.D], BF16, offset=hoff)

    oh = salloc("oh", [128, c.DC * c.TILE], BF16, offset=B_B)
    ol = salloc("ol", [128, c.DC * c.TILE], BF16, offset=B_B + xn_half)
    off = B_B
    prep_in = []
    PREP_N = c.DC * c.SLAB
    for i in range(2):
        t_ = salloc(f"prepin{i}", [128, PREP_N], F32, offset=off)
        off += PREP_N * 4
        prep_in.append(t_)
    assert off <= B_B + B_SZ
    off = B_B
    sgbuf, hf32, t0buf, resbuf, xbuf = [], [], [], [], []
    for i in range(2):
        sgbuf.append(salloc(f"sgbuf{i}", [128, c.TILE], F32, offset=off)); off += c.TILE * 4
        hf32.append(salloc(f"hf32_{i}", [128, c.TILE], F32, offset=off)); off += c.TILE * 4
        t0buf.append(salloc(f"t0buf{i}", [128, c.TILE], F32, offset=off)); off += c.TILE * 4
        resbuf.append(salloc(f"resbuf{i}", [128, c.TILE], F32, offset=off)); off += c.TILE * 4
        xbuf.append(salloc(f"xbuf{i}", [128, c.TILE], F32, offset=off)); off += c.TILE * 4
    statin_b = salloc("statin_b", [128, c.D], F32, offset=off); off += c.D * 4
    xring = [statin_b[:, TS(i, c.TILE)] for i in range(2)]
    sb_rowsg = salloc("sb_rowsg", [1, 2 * c.NSUB * 128], F32, offset=off)
    off += 2 * c.NSUB * 512
    assert off <= B_B + B_SZ, (off, B_B + B_SZ)

    accum = salloc("accum", [128, c.DC * c.TILE], F32, offset=C_B)
    off = C_B
    ws_ring = []
    for i in range(2):
        wh_ = salloc(f"wsh{i}", [128, c.DC * c.SLAB], BF16, offset=off); off += c.DC * c.SLAB * 2
        wl_ = salloc(f"wsl{i}", [128, c.DC * c.SLAB], BF16, offset=off); off += c.DC * c.SLAB * 2
        ws_ring.append((wh_, wl_))
    assert off <= C_B + C_SZ
    off = C_B
    khalo_sb_h = salloc("khalo_sb_h", [128, c.DC * c.W], BF16, offset=off); off += c.DC * c.W * 2
    khalo_sb_l = salloc("khalo_sb_l", [128, c.DC * c.W], BF16, offset=off); off += c.DC * c.W * 2
    vhalo_sb_h = salloc("vhalo_sb_h", [128, c.D], BF16, offset=off); off += c.D * 2
    vhalo_sb_l = salloc("vhalo_sb_l", [128, c.D], BF16, offset=off); off += c.D * 2
    assert off <= C_B + C_SZ

    P = [nc.alloc_psum_tensor(f"pb{i}", [128, 512], F32).ap() for i in range(8)]
    p_mm = [P[0][:, 0:c.TILE], P[1][:, 0:c.TILE], P[2][:, 0:c.TILE], P[3][:, 0:c.TILE]]
    p_at2 = [P[4][:, 0:256], P[2][:, 0:256]]
    p_o = [P[5][:, 0:128], P[6][:, 0:128], P[3][:, 0:128], P[7][:, 0:128]]
    p_bc_r = P[5]
    p_bc_mu = P[6]
    p_tr = P[7]
    p_sumh = P[4][0:1, 0:c.TILE]

    from contextlib import ExitStack
    stack = ExitStack()
    nc._kernel_stack = stack
    S, CNT = {}, {}
    for n in (["ld_const", "prep_done", "shmm", "shev",
               "xfm_ld", "stat", "vpe", "sqrt", "pack", "tmm", "rows",
               "xnf", "xnh_c",
               "qkmm", "qkhi", "qkev", "vmm", "vhi", "vev",
               "halo_ld", "halo_st", "atmm", "at_relu", "at_ev", "omm", "ohi", "oev",
               "womm", "wocp",
               "gmm", "umm", "sg", "hf32s", "hhi", "hev",
               "dnmm", "acc", "fin", "t0s", "obuf", "dvch", "scch", "t8c", "rowsd"]
              + ["xnl_c", "pxtm0", "prowsd"]
              + [f"{p}{i}" for p in ("wld", "wst", "wsth", "xtm", "gld", "uld",
                                     "lnx", "plnx",
                                     "dld", "dld2", "resld", "xld", "resst", "outst")
                 for i in range(2)]):
        S[n] = stack.enter_context(nc.semaphore(n))
        CNT[n] = 0

    def inc(ins, name, n=1):
        ins.then_inc(S[name], n)
        CNT[name] += n

    def w(eng, name, val):
        eng.wait_ge(S[name], val)

    def incp(ins, prefix, idx, n=16):
        inc(ins, f"{prefix}{idx % 2}", n)

    def wp(eng, B0, prefix, idx, per):
        nm = f"{prefix}{idx % 2}"
        eng.wait_ge(S[nm], B0[nm] + per * (idx // 2 + 1))

    def wp_all(eng, B0, prefix):
        for i in range(2):
            if CNT[f"{prefix}{i}"]:
                eng.wait_ge(S[f"{prefix}{i}"], CNT[f"{prefix}{i}"])

    def chain(eng, ins, name="dvch"):
        inc(ins, name, 1)
        eng.wait_ge(S[name], CNT[name])

    def hbm_cols(t, c0, c1):
        return t[:, c0:c1].rearrange("(c p) n -> p c n", p=128)

    def sb3(t, width):
        return t.rearrange("p (c n) -> p c n", n=width)

    D_inv = 1.0 / c.D
    EPS = 1e-5

    def ln_vec_stats(dve, s):
        ns = c.NSUB
        s1 = sb_stat[:, s:s + 1]
        s2 = sb_stat[:, ns + s:ns + s + 1]
        mu = sb_stat[:, 2 * ns + s:2 * ns + s + 1]
        e2 = sb_stat[:, 3 * ns + s:3 * ns + s + 1]
        msq = sb_stat[:, 4 * ns + s:4 * ns + s + 1]
        vpe = sb_stat[:, 5 * ns + s:5 * ns + s + 1]
        dve.tensor_scalar_mul(mu, s1, D_inv)
        chain(dve, dve.tensor_scalar_mul(e2, s2, D_inv))
        chain(dve, dve.scalar_tensor_tensor(msq, mu, -1.0, mu, OP.mult, OP.mult))
        ins = dve.scalar_tensor_tensor(vpe, e2, EPS, msq, OP.add, OP.add)
        inc(ins, "vpe", 1)

    def ln_vec_newton(dve, s):
        ns = c.NSUB
        mu = sb_stat[:, 2 * ns + s:2 * ns + s + 1]
        vpe = sb_stat[:, 5 * ns + s:5 * ns + s + 1]
        r0 = sb_stat[:, 6 * ns + s:6 * ns + s + 1]
        tn = sb_stat[:, 7 * ns + s:7 * ns + s + 1]
        chain(dve, dve.reciprocal(r0, tn))
        for _ in range(2):
            chain(dve, dve.tensor_mul(tn, r0, r0))
            chain(dve, dve.tensor_mul(tn, tn, vpe))
            chain(dve, dve.tensor_scalar(tn, tn, -0.5, 1.5, OP.mult, OP.add))
            chain(dve, dve.tensor_mul(r0, r0, tn))
        chain(dve, dve.tensor_mul(sb_pk[:, s:s + 1], mu, r0))
        ins = dve.tensor_copy(sb_pk[:, ns + s:ns + s + 1], r0)
        inc(ins, "pack", 1)

    def ln_ops(B0, n_sub, wid, xrow0, xcol0, xd_h, xd_l, overwrite_gate=None,
               defer_xn=False, dsem=("xtm0", "lnx", "rowsd")):
        L = {"dma": [], "act": [], "dve": [], "pe": [], "pl": []}

        def sp_xtm(s):
            def f(sp):
                if s >= 1:
                    w(sp, "stat", B0["stat"] + 2 * s)
                ins = sp.dma_start(statin_b, x_tm[xrow0 + s * 128: xrow0 + (s + 1) * 128, :])
                inc(ins, dsem[0], 16)
            return f

        def sp_x(dc):
            def f(sp):
                if dc < 2:
                    w(sp, "stat", B0["stat"] + 2 * n_sub)
                else:
                    w(sp, "xnl_c", B0["xnl_c"] + dc - 1)
                ins = sp.dma_start(xring[dc % 2][:, 0:wid],
                                   x_fm[TS(dc, 128), xcol0: xcol0 + wid])
                incp(ins, dsem[1], dc)
            return f

        def sp_rowsd(sp):
            w(sp, "t8c", B0["t8c"] + 1)
            ins = sp.dma_start(s_rows[0:2 * c.NSUB, :], sb_t8)
            inc(ins, dsem[2], 16)
            w(sp, dsem[2], B0[dsem[2]] + 16)
            dst_ = sb_rowsg.rearrange("p (a n) -> p a n", n=128)
            src_ = s_rows[0:2 * c.NSUB, :].rearrange("a n -> a () n") \
                if False else s_rows[0:2 * c.NSUB, :]
            ins = sp.dma_start(dst_, src_)
            inc(ins, dsem[2], 16)

        L["dma"] = ([sp_xtm(s) for s in range(n_sub)]
                    + [sp_x(0), sp_x(1), sp_rowsd]
                    + [sp_x(dc) for dc in range(2, c.DC)])

        def act_stats(s):
            def f(act):
                w(act, dsem[0], B0[dsem[0]] + 16 * (s + 1))
                if s >= 1:
                    w(act, "stat", B0["stat"] + 2 * s)
                ins = act.activation(statin_b, statin_b, AF.Copy,
                                     accum_out=sb_stat[:, s:s + 1])
                inc(ins, "stat", 1)
                w(act, "stat", B0["stat"] + 2 * s + 1)
                ins = act.activation(statin_b, statin_b, AF.Square,
                                     accum_out=sb_stat[:, c.NSUB + s:c.NSUB + s + 1])
                inc(ins, "stat", 1)
            return f

        def act_sqrt(s):
            def f(act):
                w(act, "vpe", B0["vpe"] + s + 1)
                ins = act.activation(sb_stat[:, 7 * c.NSUB + s:7 * c.NSUB + s + 1],
                                     sb_stat[:, 5 * c.NSUB + s:5 * c.NSUB + s + 1], AF.Sqrt)
                inc(ins, "sqrt", 1)
            return f

        L["act"] = ([None] + [act_stats(s) for s in range(n_sub)]
                    + [act_sqrt(s) for s in range(n_sub)])
        L["act2"] = []

        def dve_vstats(s):
            def f(dve):
                w(dve, "stat", B0["stat"] + 2 * (s + 1))
                ln_vec_stats(dve, s)
            return f

        def dve_newton(s):
            def f(dve):
                w(dve, "sqrt", B0["sqrt"] + s + 1)
                ln_vec_newton(dve, s)
            return f

        def dve_t8c(dve):
            w(dve, "tmm", B0["tmm"] + 1)
            ins = dve.tensor_copy(sb_t8, p_tr[0:2 * c.NSUB, 0:128])
            inc(ins, "t8c", 1)

        def dve_rowpair(dve):
            w(dve, dsem[2], B0[dsem[2]] + 32)
            mur = sb_rowsg[:, 0:wid]
            rr = sb_rowsg[:, c.NSUB * 128: c.NSUB * 128 + wid]
            dve.tensor_copy(sb_mu_bh[:, 0:wid], mur)
            chain(dve, dve.tensor_copy(sb_r_bh[:, 0:wid], rr))
            dve.tensor_sub(sb_mu_bl[:, 0:wid], mur, sb_mu_bh[:, 0:wid])
            ins = dve.tensor_sub(sb_r_bl[:, 0:wid], rr, sb_r_bh[:, 0:wid])
            inc(ins, "rows", 1)

        def dve_xn(dc):
            def f(dve):
                if dc == 0:
                    w(dve, "tmm", B0["tmm"] + 3)
                    if overwrite_gate is not None:
                        w(dve, overwrite_gate[0], overwrite_gate[1])
                wp(dve, B0, dsem[1], dc, 16)
                xc = xring[dc % 2][:, 0:wid]
                chain(dve, dve.tensor_mul(xc, xc, p_bc_r[:, 0:wid]))
                chain(dve, dve.tensor_sub(xc, xc, p_bc_mu[:, 0:wid]))
                chain(dve, dve.tensor_copy(xd_h[:, dc * c.TILE: dc * c.TILE + wid], xc))
                ins = dve.tensor_sub(xd_l[:, dc * c.TILE: dc * c.TILE + wid], xc,
                                     xd_h[:, dc * c.TILE: dc * c.TILE + wid])
                inc(ins, "xnl_c", 1)
            return f

        tail = [dve_xn(dc) for dc in range(c.DC)]
        head = [None] * 2 + [dve_vstats(s) for s in range(n_sub)] \
            + [dve_newton(s) for s in range(n_sub)]
        head += [None] * (12 - len(head)) + [dve_t8c, None, dve_rowpair]
        if defer_xn:
            L["dve"] = head
            L["dve2"] = tail
        else:
            L["dve"] = head + tail
            L["dve2"] = []

        def pe_tr(pe):
            w(pe, "pack", B0["pack"] + n_sub)
            ins = pe.transpose(p_tr[0:2 * c.NSUB, 0:128], sb_pk, sb_ident)
            inc(ins, "tmm", 1)

        def pe_bc(pe):
            w(pe, "rows", B0["rows"] + 1)
            pe.matmul(p_bc_mu[:, 0:wid], sb_ones[0:1, :], sb_mu_bh[:, 0:wid],
                      start=True, stop=False)
            ins = pe.matmul(p_bc_mu[:, 0:wid], sb_ones[0:1, :], sb_mu_bl[:, 0:wid],
                            start=False, stop=True)
            inc(ins, "tmm", 1)
            pe.matmul(p_bc_r[:, 0:wid], sb_ones[0:1, :], sb_r_bh[:, 0:wid],
                      start=True, stop=False)
            ins = pe.matmul(p_bc_r[:, 0:wid], sb_ones[0:1, :], sb_r_bl[:, 0:wid],
                            start=False, stop=True)
            inc(ins, "tmm", 1)

        L["pe"] = [pe_tr, pe_bc]
        L["pl"] = []
        return L

    def ln_pop(L, eng_name, eng):
        if L is not None and L[eng_name]:
            f = L[eng_name].pop(0)
            if f is not None:
                f(eng)

    def ln_drain(L, eng_name, eng):
        if L is None:
            return
        while L[eng_name]:
            f = L[eng_name].pop(0)
            if f is not None:
                f(eng)

    B0 = dict(CNT)
    with nc.Block() as blk:
        @blk.sync
        def _(sp):
            for dst, src in ((sb_masks, masks_in), (sb_gs, gate_s_in), (sb_us, up_s_in),
                             (sb_ds, down_s_in), (sb_ident, ident_in)):
                inc(sp.dma_start(dst, src), "ld_const", 16)
            sp.wait_ge(S["ld_const"], CNT["ld_const"])

        @blk.vector
        def _(dve):
            dve.memset(sb_ones, 1.0)
            dve.memset(sb_ones_col, 1.0)
            dve.memset(sb_neg128, -128.0)
            dve.memset(sb_pk, 0.0)
            dve.memset(sb_stat, 0.0)

    NSL = c.NSLAB_A + c.NSLAB_O
    B0 = dict(CNT)
    with nc.Block() as blk:
        HSL = c.DC // 2 * c.SLAB

        def prep_src(s, h):
            wt = w_attn_t if s < c.NSLAB_A else w_o_t
            s0 = s if s < c.NSLAB_A else s - c.NSLAB_A
            base = s0 * c.DC * c.SLAB + h * HSL
            return wt[:, base: base + HSL]

        def hi_dst(s):
            return s_wah[s] if s < c.NSLAB_A else s_woh[s - c.NSLAB_A]

        def hi_store(eng, s, h):
            phi = prep_pair[s % 2][0]
            dh = hi_dst(s)
            incp(eng.dma_start(dh[:, h * HSL:(h + 1) * HSL],
                               phi[:, h * HSL:(h + 1) * HSL]), "wsth", s)

        @blk.sync
        def _(sp):
            for s in range(NSL):
                if s >= 2:
                    w(sp, "prep_done", B0["prep_done"] + s - 1)
                dst = prep_in[s % 2][:, 0:HSL]
                incp(sp.dma_start(dst, prep_src(s, 0)), "wld", s)
                if s >= 2:
                    hi_store(sp, s - 2, 0)
            w(sp, "prep_done", B0["prep_done"] + NSL)
            for s in (NSL - 2, NSL - 1):
                hi_store(sp, s, 0)

        @blk.scalar
        def _(act):
            for s in range(NSL):
                if s >= 2:
                    w(act, "prep_done", B0["prep_done"] + s - 1)
                dst = prep_in[s % 2][:, HSL:2 * HSL]
                incp(act.dma_start(dst, prep_src(s, 1)), "wld", s)
                if s >= 2:
                    hi_store(act, s - 2, 1)
            w(act, "prep_done", B0["prep_done"] + NSL)
            for s in (NSL - 2, NSL - 1):
                hi_store(act, s, 1)

        @blk.vector
        def _(dve):
            for s in range(NSL):
                wp(dve, B0, "wld", s, 32)
                if s >= 2:
                    wp(dve, B0, "wst", s - 2, 16)
                    wp(dve, B0, "wsth", s - 2, 32)
                pin = prep_in[s % 2][:, 0:c.DC * c.SLAB]
                phi, plo = prep_pair[s % 2]
                if s < c.NQ:
                    chain(dve, dve.tensor_scalar_mul(phi, pin, 1.0 / c.W))
                    ins = dve.scalar_tensor_tensor(plo, pin, 1.0 / c.W, phi,
                                                   OP.mult, OP.subtract)
                else:
                    chain(dve, dve.tensor_copy(phi, pin))
                    ins = dve.tensor_sub(plo, pin, phi)
                inc(ins, "prep_done", 1)

        @blk.gpsimd
        def _(pl):
            for s in range(NSL):
                w(pl, "prep_done", B0["prep_done"] + s + 1)
                plo = prep_pair[s % 2][1]
                dl = s_wal[s] if s < c.NSLAB_A else s_wol[s - c.NSLAB_A]
                incp(pl.dma_start(dl, plo), "wst", s)
            for i in range(2):
                n_attn = (c.NSLAB_A + 1 - i) // 2
                w(pl, f"wst{i}", B0[f"wst{i}"] + 16 * n_attn)
                w(pl, f"wsth{i}", B0[f"wsth{i}"] + 32 * n_attn)

    WST_FULL = {n: CNT[n] for n in ("wst0", "wst1", "wsth0", "wsth1")}

    B0 = dict(CNT)
    with nc.Block() as blk:
        LH = ln_ops(B0, 1, c.W, 0, 0, qh, ql)

        @blk.sync
        def _(sp):
            ln_drain(LH, "dma", sp)

        @blk.scalar
        def _(act):
            ln_drain(LH, "act", act)

        @blk.vector
        def _(dve):
            ln_drain(LH, "dve", dve)

        @blk.tensor
        def _(pe):
            ln_drain(LH, "pe", pe)


    B0 = dict(CNT)
    with nc.Block() as blk:
        KV = list(range(c.NQ, c.NSLAB_A))
        NK = c.NQ
        L0 = ln_ops(B0, c.NSUB, c.TILE, c.W, c.W, xnh, xnl,
                    dsem=("pxtm0", "plnx", "prowsd"))

        def kv_load(eng, i):
            wh_, wl_ = ws_ring[i % 2]
            incp(eng.dma_start(wh_, s_wah[KV[i]]), "wld", i)
            incp(eng.dma_start(wl_, s_wal[KV[i]]), "wld", i)

        @blk.sync
        def _(sp):
            for i, s in enumerate(KV):
                if i % 2 == 1:
                    continue
                if i >= 2:
                    pi = i - 2
                    if pi < NK:
                        w(sp, "qkmm", B0["qkmm"] + (pi + 1) * c.SJ)
                    else:
                        w(sp, "vmm", B0["vmm"] + (pi - NK + 1))
                kv_load(sp, i)

        @blk.tensor
        def _(pe):
            for i, s in enumerate(KV):
                wp(pe, B0, "wld", i, 32)
                wh_, wl_ = ws_ring[i % 2]
                if i < NK:
                    for j in range(c.SJ):
                        oc = i * c.SJ + j
                        if oc >= 2:
                            w(pe, "qkev", B0["qkev"] + oc - 1)
                        pb = p_mm[oc % 2]
                        nmm = 0
                        for dc in range(c.DC):
                            lt_h = wh_[:, dc * c.SLAB + j * 128: dc * c.SLAB + (j + 1) * 128]
                            lt_l = wl_[:, dc * c.SLAB + j * 128: dc * c.SLAB + (j + 1) * 128]
                            for lt, rh in ((lt_h, qh), (lt_h, ql), (lt_l, qh)):
                                ins = pe.matmul(pb[:, 0:c.W], lt,
                                                rh[:, dc * c.TILE: dc * c.TILE + c.W],
                                                start=(nmm == 0), stop=(nmm == 3 * c.DC - 1))
                                nmm += 1
                        inc(ins, "qkmm", 1)
                else:
                    g = i - NK
                    if g >= 2:
                        w(pe, "vev", B0["vev"] + g - 1)
                    pb = p_mm[2 + g % 2][:, 0:c.SLAB]
                    nmm = 0
                    for dc in range(c.DC):
                        xh_ = qh[:, dc * c.TILE: dc * c.TILE + c.W]
                        xl_ = ql[:, dc * c.TILE: dc * c.TILE + c.W]
                        wh_c = wh_[:, TS(dc, c.SLAB)]
                        wl_c = wl_[:, TS(dc, c.SLAB)]
                        for lt, rh in ((xh_, wh_c), (xh_, wl_c), (xl_, wh_c)):
                            ins = pe.matmul(pb, lt, rh, start=(nmm == 0),
                                            stop=(nmm == 3 * c.DC - 1))
                            nmm += 1
                    inc(ins, "vmm", 1)
                if i in (5, 8):
                    ln_pop(L0, "pe", pe)
            ln_drain(L0, "pe", pe)

        @blk.scalar
        def _(act):
            if len(KV) > 1:
                kv_load(act, 1)
            for oc in range(NK * c.SJ):
                w(act, "qkmm", B0["qkmm"] + oc + 1)
                ins = act.copy(khalo_pr_h[:, TS(oc, c.W)], p_mm[oc % 2][:, 0:c.W])
                inc(ins, "qkhi", 1)
                ln_pop(L0, "act", act)
                if (oc + 1) % c.SJ == 0:
                    i2 = (oc + 1) // c.SJ + 1
                    if i2 % 2 == 1 and 3 <= i2 < len(KV):
                        kv_load(act, i2)
            for g in range(c.NSLAB_A - 2 * c.NQ):
                w(act, "vmm", B0["vmm"] + g + 1)
                ins = act.copy(vhalo_pr_h[:, TS(g, c.SLAB)], p_mm[2 + g % 2][:, 0:c.SLAB])
                inc(ins, "vhi", 1)
                ln_pop(L0, "act", act)
                i2 = NK + g + 2
                if i2 % 2 == 1 and i2 < len(KV):
                    kv_load(act, i2)
            ln_drain(L0, "act", act)

        @blk.vector
        def _(dve):
            for oc in range(NK * c.SJ):
                w(dve, "qkhi", B0["qkhi"] + oc + 1)
                ins = dve.tensor_sub(khalo_pr_l[:, TS(oc, c.W)], p_mm[oc % 2][:, 0:c.W],
                                     khalo_pr_h[:, TS(oc, c.W)])
                inc(ins, "qkev", 1)
                ln_pop(L0, "dve", dve)
            for g in range(c.NSLAB_A - 2 * c.NQ):
                w(dve, "vhi", B0["vhi"] + g + 1)
                ins = dve.tensor_sub(vhalo_pr_l[:, TS(g, c.SLAB)],
                                     p_mm[2 + g % 2][:, 0:c.SLAB],
                                     vhalo_pr_h[:, TS(g, c.SLAB)])
                inc(ins, "vev", 1)
                ln_pop(L0, "dve", dve)
            ln_drain(L0, "dve", dve)

        @blk.gpsimd
        def _(pl):
            ln_drain(L0, "dma", pl)
            w(pl, "qkev", B0["qkev"] + NK * c.SJ)
            w(pl, "vev", B0["vev"] + (c.NSLAB_A - 2 * c.NQ))
            inc(pl.dma_start(s_khalo_h, khalo_pr_h), "halo_st", 16)
            inc(pl.dma_start(s_khalo_l, khalo_pr_l), "halo_st", 16)
            inc(pl.dma_start(s_vhalo_h, vhalo_pr_h), "halo_st", 16)
            inc(pl.dma_start(s_vhalo_l, vhalo_pr_l), "halo_st", 16)

    QL0_prev = None
    for t in range(c.NT):
        GL0 = {}
        WOL0 = {}
        QL0_next = {}
        tok0 = c.W + t * c.TILE
        xnt_h, xnt_l = xnh, xnl
        qt_h, qt_l = qh, ql

        B0 = dict(CNT)
        QBASE = QL0_prev if QL0_prev is not None else B0
        with nc.Block() as blk:
            @blk.sync
            def _(sp):
                for s in range(c.NSLAB_A):
                    if QL0_prev is not None and s < 2:
                        continue
                    if s >= 2:
                        sc = s - 2
                        if sc < 2 * c.NQ:
                            w(sp, "qkmm", B0["qkmm"] + (sc + 1) * c.SJ)
                        else:
                            w(sp, "vmm", B0["vmm"] + (sc - 2 * c.NQ + 1) * c.NSUB)
                    wh_, wl_ = ws_ring[s % 2]
                    incp(sp.dma_start(wh_, s_wah[s]), "wld", s)
                    incp(sp.dma_start(wl_, s_wal[s]), "wld", s)
                s_le = c.NSLAB_A - 2
                w(sp, "vmm", B0["vmm"] + (s_le - 2 * c.NQ + 1) * c.NSUB)
                w(sp, "halo_st", B0["halo_st"])
                inc(sp.dma_start(khalo_sb_h, s_khalo_h), "halo_ld", 16)
                inc(sp.dma_start(khalo_sb_l, s_khalo_l), "halo_ld", 16)
                inc(sp.dma_start(vhalo_sb_h, s_vhalo_h), "halo_ld", 16)
                inc(sp.dma_start(vhalo_sb_l, s_vhalo_l), "halo_ld", 16)

            @blk.tensor
            def _(pe):
                for s in range(c.NSLAB_A):
                    wp(pe, QBASE, "wld", s, 32)
                    wh_, wl_ = ws_ring[s % 2]
                    if s < 2 * c.NQ:
                        for j in range(c.SJ):
                            oc = s * c.SJ + j
                            if oc >= 2:
                                w(pe, "qkev", B0["qkev"] + oc - 1)
                            pb = p_mm[oc % 2]
                            nmm = 0
                            for dc in range(c.DC):
                                lt_h = wh_[:, dc * c.SLAB + j * 128: dc * c.SLAB + (j + 1) * 128]
                                lt_l = wl_[:, dc * c.SLAB + j * 128: dc * c.SLAB + (j + 1) * 128]
                                for lt, rh in ((lt_h, xnt_h), (lt_h, xnt_l), (lt_l, xnt_h)):
                                    ins = pe.matmul(pb, lt, rh[:, TS(dc, c.TILE)],
                                                    start=(nmm == 0), stop=(nmm == 3 * c.DC - 1))
                                    nmm += 1
                            inc(ins, "qkmm", 1)
                    else:
                        sv = s - 2 * c.NQ
                        for ts_ in range(c.NSUB):
                            g = sv * c.NSUB + ts_
                            if g >= 2:
                                w(pe, "vev", B0["vev"] + g - 1)
                            pb = p_mm[2 + g % 2][:, 0:c.SLAB]
                            nmm = 0
                            for dc in range(c.DC):
                                xh_ = xnt_h[:, dc * c.TILE + ts_ * 128: dc * c.TILE + (ts_ + 1) * 128]
                                xl_ = xnt_l[:, dc * c.TILE + ts_ * 128: dc * c.TILE + (ts_ + 1) * 128]
                                wh_c = wh_[:, TS(dc, c.SLAB)]
                                wl_c = wl_[:, TS(dc, c.SLAB)]
                                for lt, rh in ((xh_, wh_c), (xh_, wl_c), (xl_, wh_c)):
                                    ins = pe.matmul(pb, lt, rh, start=(nmm == 0),
                                                    stop=(nmm == 3 * c.DC - 1))
                                    nmm += 1
                            inc(ins, "vmm", 1)

            @blk.scalar
            def _(act):
                w(act, "halo_st", B0["halo_st"])
                if t == 0:
                    for n in WST_FULL:
                        w(act, n, WST_FULL[n])
                for s in range(c.NSLAB_A):
                    if s < 2 * c.NQ:
                        for j in range(c.SJ):
                            oc = s * c.SJ + j
                            w(act, "qkmm", B0["qkmm"] + oc + 1)
                            dst = qt_h if oc < c.DC else kh
                            occ = oc if oc < c.DC else oc - c.DC
                            ins = act.copy(dst[:, TS(occ, c.TILE)], p_mm[oc % 2])
                            inc(ins, "qkhi", 1)
                    else:
                        sv = s - 2 * c.NQ
                        for ts_ in range(c.NSUB):
                            g = sv * c.NSUB + ts_
                            w(act, "vmm", B0["vmm"] + g + 1)
                            sl = slice(ts_ * c.D + sv * c.SLAB, ts_ * c.D + (sv + 1) * c.SLAB)
                            ins = act.copy(vh[:, sl], p_mm[2 + g % 2][:, 0:c.SLAB])
                            inc(ins, "vhi", 1)

            @blk.vector
            def _(dve):
                w(dve, "halo_st", B0["halo_st"])
                if t == 0:
                    for n in WST_FULL:
                        w(dve, n, WST_FULL[n])
                for s in range(c.NSLAB_A):
                    if s < 2 * c.NQ:
                        for j in range(c.SJ):
                            oc = s * c.SJ + j
                            w(dve, "qkhi", B0["qkhi"] + oc + 1)
                            dsth = qt_h if oc < c.DC else kh
                            dstl = qt_l if oc < c.DC else kl
                            occ = oc if oc < c.DC else oc - c.DC
                            ins = dve.tensor_sub(dstl[:, TS(occ, c.TILE)], p_mm[oc % 2],
                                                 dsth[:, TS(occ, c.TILE)])
                            inc(ins, "qkev", 1)
                    else:
                        sv = s - 2 * c.NQ
                        for ts_ in range(c.NSUB):
                            g = sv * c.NSUB + ts_
                            w(dve, "vhi", B0["vhi"] + g + 1)
                            sl = slice(ts_ * c.D + sv * c.SLAB, ts_ * c.D + (sv + 1) * c.SLAB)
                            ins = dve.tensor_sub(vl[:, sl], p_mm[2 + g % 2][:, 0:c.SLAB], vh[:, sl])
                            inc(ins, "vev", 1)

        B0 = dict(CNT)
        with nc.Block() as blk:
            @blk.tensor
            def _(pe):
                w(pe, "halo_ld", B0["halo_ld"])

                def at_group(b):
                    if b >= 2:
                        w(pe, "at_ev", B0["at_ev"] + 2 * (b - 1))
                    nmm = 0
                    for half in range(2):
                        pb = p_at2[b % 2][:, half * 128:(half + 1) * 128]
                        for dc in range(c.DC):
                            if half == 0:
                                if b == 0:
                                    kh_ = khalo_sb_h[:, TS(dc, c.W)]
                                    kl_ = khalo_sb_l[:, TS(dc, c.W)]
                                else:
                                    kh_ = kh[:, dc * c.TILE + (b - 1) * c.W: dc * c.TILE + b * c.W]
                                    kl_ = kl[:, dc * c.TILE + (b - 1) * c.W: dc * c.TILE + b * c.W]
                            else:
                                kh_ = kh[:, dc * c.TILE + b * c.W: dc * c.TILE + (b + 1) * c.W]
                                kl_ = kl[:, dc * c.TILE + b * c.W: dc * c.TILE + (b + 1) * c.W]
                            qh_ = qt_h[:, dc * c.TILE + b * c.W: dc * c.TILE + (b + 1) * c.W]
                            ql_ = qt_l[:, dc * c.TILE + b * c.W: dc * c.TILE + (b + 1) * c.W]
                            for lt, rh in ((kh_, qh_), (kh_, ql_), (kl_, qh_)):
                                ins = pe.matmul(pb, lt, rh, start=(nmm == 0),
                                                stop=(nmm == 6 * c.DC - 1))
                                nmm += 1
                    inc(ins, "atmm", 1)

                def o_groups(b):
                    w(pe, "at_ev", B0["at_ev"] + 2 * b + 2)
                    for dc in range(c.DC):
                        g = b * c.DC + dc
                        if g >= 4:
                            w(pe, "oev", B0["oev"] + g - 3)
                        pb = p_o[g % 4]
                        nmm = 0
                        for half in range(2):
                            if half == 0:
                                if b == 0:
                                    vh_ = vhalo_sb_h[:, TS(dc, 128)]
                                    vl_ = vhalo_sb_l[:, TS(dc, 128)]
                                else:
                                    vh_ = vh[:, (b - 1) * c.D + dc * 128:(b - 1) * c.D + (dc + 1) * 128]
                                    vl_ = vl[:, (b - 1) * c.D + dc * 128:(b - 1) * c.D + (dc + 1) * 128]
                            else:
                                vh_ = vh[:, b * c.D + dc * 128: b * c.D + (dc + 1) * 128]
                                vl_ = vl[:, b * c.D + dc * 128: b * c.D + (dc + 1) * 128]
                            ah_ = (sb_ah if b % 2 == 0 else sb_ah2)[:, TS(half, c.W)]
                            al_ = (sb_al if b % 2 == 0 else sb_al2)[:, TS(half, c.W)]
                            for lt, rh in ((vh_, ah_), (vh_, al_), (vl_, ah_)):
                                ins = pe.matmul(pb, lt, rh, start=(nmm == 0), stop=(nmm == 5))
                                nmm += 1
                        inc(ins, "omm", 1)

                for b in range(c.NB):
                    at_group(b)
                    if b >= 1:
                        o_groups(b - 1)
                o_groups(c.NB - 1)

            @blk.scalar
            def _(act):
                wp_all(act, B0, "outst")
                for b in range(c.NB):
                    w(act, "atmm", B0["atmm"] + b + 1)
                    if b >= 1:
                        w(act, "at_ev", B0["at_ev"] + 2 * b)
                    for half in range(2):
                        ins = act.activation(sb_rbuf[:, TS(half, c.W)],
                                             p_at2[b % 2][:, half * 128:(half + 1) * 128],
                                             AF.Relu)
                        inc(ins, "at_relu", 1)
                    for dc in range(c.DC):
                        g = b * c.DC + dc
                        w(act, "omm", B0["omm"] + g + 1)
                        ins = act.copy(oh[:, dc * c.TILE + b * c.W: dc * c.TILE + (b + 1) * c.W],
                                       p_o[g % 4])
                        inc(ins, "ohi", 1)

            @blk.vector
            def _(dve):
                wp_all(dve, B0, "outst")
                for b in range(c.NB):
                    w(dve, "at_relu", B0["at_relu"] + 2 * b + 2)
                    if b >= 2:
                        w(dve, "omm", B0["omm"] + (b - 1) * c.DC)
                    for half in range(2):
                        pb = p_at2[b % 2][:, half * 128:(half + 1) * 128]
                        ins = dve.scalar_tensor_tensor(sb_ambuf[:, TS(half, c.W)], pb, 1.0,
                                                       sb_rbuf[:, TS(half, c.W)],
                                                       OP.mult, OP.mult)
                    chain(dve, ins)
                    for half in range(2):
                        if half == 0:
                            msk = sb_masks[:, 2 * c.W:3 * c.W] if (t == 0 and b == 0) \
                                else sb_masks[:, 0:c.W]
                        else:
                            msk = sb_masks[:, c.W:2 * c.W]
                        ins = dve.tensor_mul(sb_ambuf[:, TS(half, c.W)],
                                             sb_ambuf[:, TS(half, c.W)], msk)
                    chain(dve, ins)
                    ah_b = sb_ah if b % 2 == 0 else sb_ah2
                    al_b = sb_al if b % 2 == 0 else sb_al2
                    for half in range(2):
                        ins = dve.tensor_copy(ah_b[:, TS(half, c.W)],
                                              sb_ambuf[:, TS(half, c.W)])
                    chain(dve, ins)
                    for half in range(2):
                        ins = dve.tensor_sub(al_b[:, TS(half, c.W)],
                                             sb_ambuf[:, TS(half, c.W)],
                                             ah_b[:, TS(half, c.W)])
                        inc(ins, "at_ev", 1)
                    for dc in range(c.DC):
                        g = b * c.DC + dc
                        w(dve, "ohi", B0["ohi"] + g + 1)
                        ins = dve.tensor_sub(ol[:, dc * c.TILE + b * c.W: dc * c.TILE + (b + 1) * c.W],
                                             p_o[g % 4],
                                             oh[:, dc * c.TILE + b * c.W: dc * c.TILE + (b + 1) * c.W])
                        inc(ins, "oev", 1)

            @blk.sync
            def _(sp):
                if t == 0:
                    for n in WST_FULL:
                        w(sp, n, WST_FULL[n])
                WOL0.update({n: CNT[n] for n in ("wld0", "wld1")})
                w(sp, "omm", B0["omm"] + c.DC)
                incp(sp.dma_start(ws_ring[0][0], s_woh[0]), "wld", 0)
                incp(sp.dma_start(ws_ring[0][1], s_wol[0]), "wld", 0)
                incp(sp.dma_start(ws_ring[1][0], s_woh[1]), "wld", 1)
                incp(sp.dma_start(ws_ring[1][1], s_wol[1]), "wld", 1)

            @blk.gpsimd
            def _(pl):
                w(pl, "halo_ld", B0["halo_ld"])
                src = sb3(kh, c.TILE)[:, :, c.TILE - c.W:c.TILE]
                inc(pl.dma_start(sb3(s_khalo_h, c.W), src), "halo_st", 16)
                src = sb3(kl, c.TILE)[:, :, c.TILE - c.W:c.TILE]
                inc(pl.dma_start(sb3(s_khalo_l, c.W), src), "halo_st", 16)
                inc(pl.dma_start(s_vhalo_h, vh[:, (c.NSUB - 1) * c.D: c.NSUB * c.D]), "halo_st", 16)
                inc(pl.dma_start(s_vhalo_l, vl[:, (c.NSUB - 1) * c.D: c.NSUB * c.D]), "halo_st", 16)

        B0 = dict(CNT)
        with nc.Block() as blk:
            @blk.sync
            def _(sp):
                if t == 0:
                    for n in WST_FULL:
                        w(sp, n, WST_FULL[n])
                for s in range(2, c.NSLAB_O):
                    w(sp, "womm", B0["womm"] + (s - 1) * c.SJ)
                    wh_, wl_ = ws_ring[s % 2]
                    incp(sp.dma_start(wh_, s_woh[s]), "wld", s)
                    incp(sp.dma_start(wl_, s_wol[s]), "wld", s)

            @blk.tensor
            def _(pe):
                for s in range(c.NSLAB_O):
                    wp(pe, WOL0, "wld", s, 32)
                    wh_, wl_ = ws_ring[s % 2]
                    for j in range(c.SJ):
                        oc = s * c.SJ + j
                        if oc >= 2:
                            w(pe, "wocp", B0["wocp"] + oc - 1)
                        pb = p_mm[oc % 2]
                        nmm = 0
                        for dc in range(c.DC):
                            lt_h = wh_[:, dc * c.SLAB + j * 128: dc * c.SLAB + (j + 1) * 128]
                            lt_l = wl_[:, dc * c.SLAB + j * 128: dc * c.SLAB + (j + 1) * 128]
                            for lt, rh in ((lt_h, oh), (lt_h, ol), (lt_l, oh)):
                                ins = pe.matmul(pb, lt, rh[:, TS(dc, c.TILE)],
                                                start=(nmm == 0), stop=(nmm == 3 * c.DC - 1))
                                nmm += 1
                        inc(ins, "womm", 1)

            @blk.scalar
            def _(act):
                for oc in range(c.DC):
                    w(act, "womm", B0["womm"] + oc + 1)
                    if oc >= 2:
                        wp(act, B0, "resst", oc - 2, 16)
                    else:
                        nm = f"resst{oc % 2}"
                        if B0[nm]:
                            w(act, nm, B0[nm])
                    ins = act.copy(reschunk[oc % 2], p_mm[oc % 2])
                    inc(ins, "wocp", 1)

            @blk.gpsimd
            def _(pl):
                w(pl, "halo_st", B0["halo_st"])
                GL0.update({n: CNT[n] for n in ("gld0", "gld1", "uld0", "uld1",
                                                "dld0", "dld1", "dld20", "dld21")})
                for j in (0, 1):
                    g_, u_ = gu_ring[j % 2]
                    srcg = gate_w_t[:, TS(j, 128)].rearrange("(c p) n -> p c n", p=128)
                    srcu = up_w_t[:, TS(j, 128)].rearrange("(c p) n -> p c n", p=128)
                    incp(pl.dma_start(sb3(g_, 128), srcg), "gld", j)
                    incp(pl.dma_start(sb3(u_, 128), srcu), "uld", j)
                fh = c.FCH // 2
                for ch in (0, 1):
                    for a, nm in ((0, "dld"), (1, "dld2")):
                        srcd = down_w_t[a * c.FF4:(a + 1) * c.FF4,
                                        TS(ch, 128)].rearrange("(c p) n -> p c n", p=128)
                        dst = dn_ring[ch % 2][:, a * fh * 128:(a + 1) * fh * 128]
                        incp(pl.dma_start(sb3(dst, 128), srcd), nm, ch)
                wp_all(pl, B0, "resld")
                for oc in range(c.DC):
                    w(pl, "wocp", B0["wocp"] + oc + 1)
                    incp(pl.dma_start(s_res[oc], reschunk[oc % 2]), "resst", oc)

        B0 = dict(CNT)
        with nc.Block() as blk:
            if t + 1 < c.NT:
                nxt = c.W + (t + 1) * c.TILE
                LF = ln_ops(B0, c.NSUB, c.TILE, nxt, nxt, xnh, xnl,
                            overwrite_gate=("umm", B0["umm"] + c.FC), defer_xn=True)
            else:
                LF = None

            @blk.sync
            def _(sp):
                ln_drain(LF, "dma", sp)
                if t + 1 < c.NT:
                    QL0_next.update({n: CNT[n] for n in ("wld0", "wld1")})
                    nch = c.DC * c.SLAB // c.TILE
                    w(sp, "t0s", B0["t0s"] + min(c.DC, nch))
                    incp(sp.dma_start(ws_ring[0][0], s_wah[0]), "wld", 0)
                    incp(sp.dma_start(ws_ring[0][1], s_wal[0]), "wld", 0)
                    w(sp, "t0s", B0["t0s"] + min(c.DC, 2 * nch))
                    incp(sp.dma_start(ws_ring[1][0], s_wah[1]), "wld", 1)
                    incp(sp.dma_start(ws_ring[1][1], s_wal[1]), "wld", 1)

            @blk.tensor
            def _(pe):
                for hf in range(2):
                    for icl in range(c.FCH):
                        j = hf * c.FCH + icl
                        wp(pe, GL0, "gld", j, 16)
                        if hf == 1 and icl == 0:
                            w(pe, "acc", B0["acc"] + c.DC)
                        if j >= 2:
                            w(pe, "hev", B0["hev"] + j - 1)
                        g_, u_ = gu_ring[j % 2]
                        pg = p_mm[(j % 2) * 2]
                        pu = p_mm[(j % 2) * 2 + 1]
                        nmm = 0
                        for dc in range(c.DC):
                            lt = g_[:, TS(dc, 128)]
                            for rh in (xnt_h, xnt_l):
                                ins = pe.matmul(pg, lt, rh[:, TS(dc, c.TILE)],
                                                start=(nmm == 0), stop=(nmm == 2 * c.DC - 1))
                                nmm += 1
                        inc(ins, "gmm", 1)
                        wp(pe, GL0, "uld", j, 16)
                        nmm = 0
                        for dc in range(c.DC):
                            lt = u_[:, TS(dc, 128)]
                            for rh in (xnt_h, xnt_l):
                                ins = pe.matmul(pu, lt, rh[:, TS(dc, c.TILE)],
                                                start=(nmm == 0), stop=(nmm == 2 * c.DC - 1))
                                nmm += 1
                        inc(ins, "umm", 1)
                        if icl >= 2:
                            k = icl - 2
                            if k == 0 and hf == 1:
                                w(pe, "shev", B0["shev"] + 1)
                            pe.matmul(p_sumh, sb_ones_col, hh[:, TS(k, c.TILE)],
                                      start=(k == 0), stop=False)
                            pe.matmul(p_sumh, sb_ones_col, hl[:, TS(k, c.TILE)],
                                      start=False, stop=False)
                        if j in (11, 15):
                            ln_pop(LF, "pe", pe)
                    if hf == 1:
                        ln_drain(LF, "pe", pe)
                    w(pe, "hev", B0["hev"] + (hf + 1) * c.FCH)
                    for k in (c.FCH - 2, c.FCH - 1):
                        pe.matmul(p_sumh, sb_ones_col, hh[:, TS(k, c.TILE)],
                                  start=False, stop=False)
                        ins = pe.matmul(p_sumh, sb_ones_col, hl[:, TS(k, c.TILE)],
                                        start=False, stop=(k == c.FCH - 1))
                    inc(ins, "shmm", 1)
                    w(pe, "shev", B0["shev"] + hf + 1)
                    for ch in range(c.DC):
                        j = hf * c.DC + ch
                        wp(pe, GL0, "dld", j, 16)
                        if hf == 0:
                            if j >= 2:
                                w(pe, "acc", B0["acc"] + j - 1)
                        else:
                            if ch >= 2:
                                w(pe, "fin", B0["fin"] + ch - 1)
                            else:
                                w(pe, "acc", B0["acc"] + c.DC + ch - 1)
                        pb = p_mm[j % 2]
                        nmm = 0
                        for icl in range(c.FCH):
                            if icl == c.FCH // 2:
                                wp(pe, GL0, "dld2", j, 16)
                            lt = dn_ring[j % 2][:, TS(icl, 128)]
                            for rh in (hh, hl):
                                ins = pe.matmul(pb, lt, rh[:, TS(icl, c.TILE)],
                                                start=(nmm == 0), stop=False)
                                nmm += 1
                        pe.matmul(pb, sb_neg128, sumh_h[:, 0:c.TILE], start=False, stop=False)
                        ins = pe.matmul(pb, sb_neg128, sumh_l[:, 0:c.TILE], start=False, stop=True)
                        inc(ins, "dnmm", 1)

            @blk.scalar
            def _(act):
                w(act, "halo_st", B0["halo_st"])
                for hf in range(2):
                    for icl in range(c.FCH):
                        j = hf * c.FCH + icl
                        w(act, "gmm", B0["gmm"] + j + 1)
                        if j >= 2:
                            w(act, "hf32s", B0["hf32s"] + j - 1)
                        if hf == 1 and icl == 0:
                            w(act, "dnmm", B0["dnmm"] + c.DC)
                        pg = p_mm[(j % 2) * 2]
                        i2 = hf * c.FCH + icl
                        ins = act.activation(sgbuf[j % 2], pg, AF.Silu,
                                             scale=sb_gs[:, i2:i2 + 1])
                        inc(ins, "sg", 1)
                        w(act, "hf32s", B0["hf32s"] + j + 1)
                        ins = act.copy(hh[:, TS(icl, c.TILE)], hf32[j % 2])
                        inc(ins, "hhi", 1)
                        ln_pop(LF, "act", act)
                ln_drain(LF, "act", act)
                for ch in range(c.DC):
                    w(act, "acc", B0["acc"] + ch + 1)
                    if ch >= 2:
                        w(act, "obuf", B0["obuf"] + ch - 1)
                    ins = act.activation(t0buf[ch % 2], accum[:, TS(ch, c.TILE)], AF.Copy,
                                         scale=sb_ds[:, ch:ch + 1])
                    inc(ins, "t0s", 1)

            @blk.vector
            def _(dve):
                w(dve, "halo_st", B0["halo_st"])
                for hf in range(2):
                    for icl in range(c.FCH):
                        j = hf * c.FCH + icl
                        w(dve, "sg", B0["sg"] + j + 1)
                        w(dve, "umm", B0["umm"] + j + 1)
                        if j >= 2:
                            w(dve, "hhi", B0["hhi"] + j - 1)
                        if hf == 1 and icl == 0:
                            w(dve, "dnmm", B0["dnmm"] + c.DC)
                        pu = p_mm[(j % 2) * 2 + 1]
                        i2 = hf * c.FCH + icl
                        ins = dve.scalar_tensor_tensor(hf32[j % 2], pu, sb_us[:, i2:i2 + 1],
                                                       sgbuf[j % 2], OP.mult, OP.mult)
                        inc(ins, "hf32s", 1)
                        w(dve, "hhi", B0["hhi"] + j + 1)
                        ins = dve.tensor_sub(hl[:, TS(icl, c.TILE)], hf32[j % 2],
                                             hh[:, TS(icl, c.TILE)])
                        inc(ins, "hev", 1)
                        ln_pop(LF, "dve", dve)
                    if hf == 1:
                        ln_drain(LF, "dve", dve)
                    w(dve, "shmm", B0["shmm"] + hf + 1)
                    chain(dve, dve.tensor_copy(sumh_h[:, 0:c.TILE], p_sumh))
                    ins = dve.tensor_sub(sumh_l[:, 0:c.TILE], p_sumh, sumh_h[:, 0:c.TILE])
                    inc(ins, "shev", 1)
                    for ch in range(c.DC):
                        j = hf * c.DC + ch
                        w(dve, "dnmm", B0["dnmm"] + j + 1)
                        pb = p_mm[j % 2]
                        if hf == 0:
                            ins = dve.tensor_copy(accum[:, TS(ch, c.TILE)], pb)
                            inc(ins, "acc", 1)
                        else:
                            w(dve, "t0s", B0["t0s"] + ch + 1)
                            ins = dve.scalar_tensor_tensor(t0buf[ch % 2], pb,
                                                           sb_ds[:, ch:ch + 1],
                                                           t0buf[ch % 2],
                                                           OP.mult, OP.add)
                            inc(ins, "fin", 1)
                            w(dve, "fin", CNT["fin"])
                            wp(dve, B0, "resld", ch, 16)
                            chain(dve, dve.tensor_add(t0buf[ch % 2], t0buf[ch % 2],
                                                      resbuf[ch % 2]))
                            wp(dve, B0, "xld", ch, 16)
                            ins = dve.tensor_add(xbuf[ch % 2], t0buf[ch % 2], xbuf[ch % 2])
                            inc(ins, "obuf", 1)
                            ln_pop(LF, "dve2", dve)
                            ln_pop(LF, "dve2", dve)
                ln_drain(LF, "dve2", dve)

            @blk.gpsimd
            def _(pl):
                w(pl, "halo_st", B0["halo_st"])
                for hf in range(2):
                    for icl in range(c.FCH):
                        j = hf * c.FCH + icl
                        if j < 2:
                            continue
                        w(pl, "gmm", B0["gmm"] + j - 1)
                        g_, u_ = gu_ring[j % 2]
                        ic = hf * c.FCH + icl
                        srcg = gate_w_t[:, TS(ic, 128)].rearrange("(c p) n -> p c n", p=128)
                        srcu = up_w_t[:, TS(ic, 128)].rearrange("(c p) n -> p c n", p=128)
                        incp(pl.dma_start(sb3(g_, 128), srcg), "gld", j)
                        w(pl, "umm", B0["umm"] + j - 1)
                        incp(pl.dma_start(sb3(u_, 128), srcu), "uld", j)
                    if hf == 0:
                        for ch in range(2, c.DC):
                            w(pl, "dnmm", B0["dnmm"] + ch - 1)
                            fh = c.FCH // 2
                            for a, nm in ((0, "dld"), (1, "dld2")):
                                srcd = down_w_t[a * c.FF4: (a + 1) * c.FF4,
                                                TS(ch, 128)].rearrange(
                                    "(c p) n -> p c n", p=128)
                                dst = dn_ring[ch % 2][:, a * fh * 128:(a + 1) * fh * 128]
                                incp(pl.dma_start(sb3(dst, 128), srcd), nm, ch)
                wp_all(pl, B0, "resst")
                for ch in range(c.DC):
                    j = c.DC + ch
                    w(pl, "dnmm", B0["dnmm"] + j - 1)
                    fh = c.FCH // 2
                    for a, nm in ((0, "dld"), (1, "dld2")):
                        srcd = down_w_t[c.FF2 + a * c.FF4: c.FF2 + (a + 1) * c.FF4,
                                        TS(ch, 128)].rearrange(
                            "(c p) n -> p c n", p=128)
                        dst = dn_ring[j % 2][:, a * fh * 128:(a + 1) * fh * 128]
                        incp(pl.dma_start(sb3(dst, 128), srcd), nm, j)
                    if ch >= 2:
                        w(pl, "obuf", B0["obuf"] + ch - 1)
                        wp(pl, B0, "outst", ch - 2, 16)
                    else:
                        nm = f"outst{ch % 2}"
                        if B0[nm]:
                            w(pl, nm, B0[nm])
                    incp(pl.dma_start(resbuf[ch % 2], s_res[ch]), "resld", ch)
                    src = x_fm[TS(ch, 128), tok0: tok0 + c.TILE]
                    incp(pl.dma_start(xbuf[ch % 2], src), "xld", ch)
                    if ch >= 1:
                        w(pl, "obuf", B0["obuf"] + ch)
                        incp(pl.dma_start(out_fm[TS(ch - 1, 128), TS(t, c.TILE)],
                                          xbuf[(ch - 1) % 2]), "outst", ch - 1)
                w(pl, "obuf", B0["obuf"] + c.DC)
                incp(pl.dma_start(out_fm[TS(c.DC - 1, 128), TS(t, c.TILE)],
                                  xbuf[(c.DC - 1) % 2]), "outst", c.DC - 1)

        QL0_prev = QL0_next if t + 1 < c.NT else None

    with nc.Block() as blk:
        @blk.gpsimd
        def _(pl):
            for name in ["outst0", "outst1", "resst0", "resst1", "halo_st",
                         "wst0", "wst1", "gld0", "gld1", "uld0", "uld1",
                         "dld0", "dld1", "dld20", "dld21",
                         "resld0", "resld1", "xld0", "xld1"]:
                if CNT[name]:
                    pl.wait_ge(S[name], CNT[name])

        @blk.sync
        def _(sp):
            for name in ["wld0", "wld1", "wsth0", "wsth1", "xtm0", "lnx0", "lnx1",
                         "halo_ld", "ld_const"]:
                if CNT[name]:
                    sp.wait_ge(S[name], CNT[name])

        @blk.scalar
        def _(act):
            for name in ["pxtm0", "plnx0", "plnx1", "prowsd", "rowsd"]:
                if CNT[name]:
                    act.wait_ge(S[name], CNT[name])

    return nc


def make_in_maps(cfg, inputs, n_cores=N_CORES):
    c = cfg
    x = np.asarray(inputs["L0_attn_x"], dtype=np.float32)
    B, T, D = x.shape
    n_seq_shards = n_cores // B
    SEQ = T // n_seq_shards
    assert SEQ == c.TOK and D == c.D

    def pack_slabs(wt, nslab):
        arr = wt.reshape(c.DC, 128, nslab, c.SLAB)
        return np.ascontiguousarray(
            arr.transpose(1, 2, 0, 3).reshape(128, nslab * c.DC * c.SLAB))

    w_attn_t = pack_slabs(np.asarray(inputs["L0_attn_w_attn"], np.float32).T,
                          c.NSLAB_A)
    w_o_t = pack_slabs(np.asarray(inputs["L0_attn_w_o"], np.float32).T, c.NSLAB_O)
    gate_w_t = np.ascontiguousarray(np.asarray(inputs["L0_ffn_gate_w"], np.int32).T)
    up_w_t = np.ascontiguousarray(np.asarray(inputs["L0_ffn_up_w"], np.int32).T)
    down_w_t = np.ascontiguousarray(np.asarray(inputs["L0_ffn_down_w"], np.int32).T)
    gate_s = np.ascontiguousarray(np.asarray(inputs["L0_ffn_gate_scale"], np.float32).reshape(c.FC, 128).T)
    up_s = np.ascontiguousarray(np.asarray(inputs["L0_ffn_up_scale"], np.float32).reshape(c.FC, 128).T)
    down_s = np.ascontiguousarray(np.asarray(inputs["L0_ffn_down_scale"], np.float32).reshape(c.DC, 128).T)

    jj = np.arange(c.W)[:, None]
    ii = np.arange(c.W)[None, :]
    m_prev = (jj >= ii).astype(np.float32)
    m_cur = (jj <= ii).astype(np.float32)
    m_zero = np.zeros((c.W, c.W), np.float32)
    ident = np.eye(128, dtype=np.float32)

    in_maps = []
    for core in range(n_cores):
        b = core // n_seq_shards
        sh = core % n_seq_shards
        xt = np.zeros((c.TOKH, c.D), np.float32)
        t0 = sh * SEQ
        if sh > 0:
            xt[:] = x[b, t0 - c.W: t0 + SEQ]
        else:
            xt[c.W:] = x[b, 0:SEQ]
        x_tm_a = np.ascontiguousarray(xt)
        x_fm_a = np.ascontiguousarray(xt.T)
        m_first = m_prev if sh > 0 else m_zero
        import ml_dtypes
        masks = np.ascontiguousarray(
            np.concatenate([m_prev, m_cur, m_first], axis=1).astype(ml_dtypes.bfloat16))
        in_maps.append({
            "x_fm": x_fm_a, "x_tm": x_tm_a,
            "w_attn_t": w_attn_t, "w_o_t": w_o_t,
            "gate_w_t": gate_w_t, "up_w_t": up_w_t, "down_w_t": down_w_t,
            "gate_s": gate_s, "up_s": up_s, "down_s": down_s,
            "masks": masks, "ident": ident,
        })
    return in_maps


def kernel(**inputs):
    cfg = CFG()
    nc = build(cfg)
    in_maps = make_in_maps(cfg, inputs)
    res = run_bass_kernel_spmd(nc, in_maps, core_ids=list(range(N_CORES)))
    x = np.asarray(inputs["L0_attn_x"])
    B, T, D = x.shape
    n_seq_shards = N_CORES // B
    SEQ = T // n_seq_shards
    out = np.empty((B, T, D), np.float32)
    for core in range(N_CORES):
        b = core // n_seq_shards
        sh = core % n_seq_shards
        o = np.asarray(res.results[core]["out_fm"])
        out[b, sh * SEQ:(sh + 1) * SEQ, :] = o.T
    return out
